# version 07
# baseline (speedup 1.0000x reference)
import contextlib
import math
import numpy as np
import ml_dtypes
import concourse.bass as bass
import concourse.mybir as mybir
from concourse.bass_utils import run_bass_kernel_spmd

F32 = mybir.dt.float32
BF16 = mybir.dt.bfloat16
I16 = mybir.dt.int16
I32 = mybir.dt.int32
AX = mybir.AxisListType.X
OP = mybir.AluOpType
AF = mybir.ActivationFunctionType

H = 4096
NH = 32
HD = 128
I_FF = 2048
EPS = 1e-5
MAGIC = 12582912.0
INV127 = 1.0 / 127.0
SCALE = float(HD) ** -0.5
NEG_BIG = -1.0e38
NHC = H // 128
NIC = I_FF // 128


def _slopes32():
    start = 2.0 ** (-(2.0 ** (-(math.log2(NH) - 3))))
    return [float(np.float32(start ** (i + 1))) for i in range(NH)]


class REng:

    def __init__(self, prog, key, sem, default_inc, skip_self=True):
        self.prog = prog
        self.key = key
        self.sem = sem
        self.n = 0
        self.default_inc = default_inc
        self.skip_self = skip_self
        self.waited = {}

    def wait(self, dep):
        if dep is None:
            return
        eng, val = dep
        if val <= 0:
            return
        if eng is self and self.skip_self:
            return
        k = id(eng.sem)
        if self.waited.get(k, -1) >= val:
            return
        self.prog[self.key].append(("w", eng.sem, val))
        self.waited[k] = val

    def run(self, fn, deps=(), inc=None):
        for d in deps:
            self.wait(d)
        if inc is None:
            inc = self.default_inc
        self.prog[self.key].append(("e", fn, self.sem, inc))
        self.n += inc
        return (self, self.n)


class Buf:
    def __init__(self):
        self.wdep = None
        self.rdeps = []

    def write_deps(self):
        d = list(self.rdeps)
        if self.wdep is not None:
            d.append(self.wdep)
        return d

    def wrote(self, dep):
        self.wdep = dep
        self.rdeps = []

    def read_deps(self):
        return [self.wdep] if self.wdep is not None else []

    def read(self, dep):
        self.rdeps.append(dep)


def build_nc(S=2048, TQ=512, debug=False, tap=None):
    assert S % 128 == 0 and TQ % 128 == 0 and S % TQ == 0 and TQ <= 512
    NT = S // 128
    NB = S // TQ
    NQS = TQ // 128
    TQ3 = min(256, TQ)
    NB3 = S // TQ3
    NQS3 = TQ3 // 128

    nc = bass.Bass("TRN2", target_bir_lowering=False, debug=debug,
                   detect_race_conditions=False)

    x_d = nc.dram_tensor("x", [S, H], F32, kind="ExternalInput").ap()
    wq_d = nc.dram_tensor("Wq", [H, H], F32, kind="ExternalInput").ap()
    wkv_d = nc.dram_tensor("Wkv", [H, 2 * HD], F32, kind="ExternalInput").ap()
    wo_d = nc.dram_tensor("Wo", [H, H], F32, kind="ExternalInput").ap()
    wg_d = nc.dram_tensor("w_gate", [I_FF, H], F32, kind="ExternalInput").ap()
    wu_d = nc.dram_tensor("w_up", [I_FF, H], F32, kind="ExternalInput").ap()
    wd_d = nc.dram_tensor("w_down", [H, I_FF], F32, kind="ExternalInput").ap()
    swg_d = nc.dram_tensor("s_wgate", [I_FF], F32, kind="ExternalInput").ap()
    swu_d = nc.dram_tensor("s_wup", [I_FF], F32, kind="ExternalInput").ap()
    swd_d = nc.dram_tensor("s_wdown", [H], F32, kind="ExternalInput").ap()
    invg_d = nc.dram_tensor("inv_gate", [1, H], F32, kind="ExternalInput").ap()
    invu_d = nc.dram_tensor("inv_up", [1, H], F32, kind="ExternalInput").ap()
    invi_d = nc.dram_tensor("inv_inter", [1, I_FF], F32,
                            kind="ExternalInput").ap()
    id32_d = nc.dram_tensor("ident32", [128, 128], F32,
                            kind="ExternalInput").ap()
    id16_d = nc.dram_tensor("ident16", [128, 128], BF16,
                            kind="ExternalInput").ap()
    ones_d = nc.dram_tensor("ones1", [1, 128], F32, kind="ExternalInput").ap()
    jrow_d = nc.dram_tensor("jrow2", [128, S], F32, kind="ExternalInput").ap()
    mask_d = nc.dram_tensor("maskt", [128, 128], F32,
                            kind="ExternalInput").ap()

    y_d = nc.dram_tensor("y", [S, H], F32, kind="ExternalOutput").ap()
    dbg_d = (nc.dram_tensor("dbg", [S, H], F32, kind="ExternalOutput").ap()
             if tap else None)

    xnhi_s = nc.dram_tensor("xnhi_s", [S, H], BF16, kind="Internal").ap()
    xnlo_s = nc.dram_tensor("xnlo_s", [S, H], BF16, kind="Internal").ap()
    xf_s = nc.dram_tensor("xf_s", [S, H], BF16, kind="Internal").ap()
    qhi_s = nc.dram_tensor("qhi_s", [H, TQ], BF16, kind="Internal").ap()
    qlo_s = nc.dram_tensor("qlo_s", [H, TQ], BF16, kind="Internal").ap()
    aohi_s = nc.dram_tensor("aohi_s", [H, TQ], BF16, kind="Internal").ap()
    aolo_s = nc.dram_tensor("aolo_s", [H, TQ], BF16, kind="Internal").ap()
    attn_s = nc.dram_tensor("attn_s", [H, S], F32, kind="Internal").ap()
    xnhfm_s = nc.dram_tensor("xnhfm_s", [NHC * 128, S], BF16,
                             kind="Internal").ap()
    xnlfm_s = nc.dram_tensor("xnlfm_s", [NHC * 128, S], BF16,
                             kind="Internal").ap()
    wgT_s = nc.dram_tensor("wgT_s", [NIC * 128, H], BF16, kind="Internal").ap()
    wuT_s = nc.dram_tensor("wuT_s", [NIC * 128, H], BF16, kind="Internal").ap()
    wdT_s = nc.dram_tensor("wdT_s", [NHC * 128, I_FF], BF16,
                           kind="Internal").ap()

    slopes = _slopes32()

    with (
        nc.semaphore("m_pe") as sem_pe,
        nc.semaphore("m_dve") as sem_dve,
        nc.semaphore("m_act") as sem_act,
        nc.semaphore("m_gp") as sem_gp,
        nc.semaphore("m_gpd") as sem_gpd,
        nc.semaphore("m_sq") as sem_sq,
        nc.Block() as block,
    ):
        prog = {"pe": [], "dve": [], "act": [], "gp": [], "sq": []}
        pe = REng(prog, "pe", sem_pe, 1)
        dve = REng(prog, "dve", sem_dve, 1, skip_self=False)
        act = REng(prog, "act", sem_act, 1, skip_self=False)
        gp = REng(prog, "gp", sem_gp, 1,
                  skip_self=False)
        gpq = REng(prog, "gp", sem_gpd, 16,
                   skip_self=False)
        sq = REng(prog, "sq", sem_sq, 16,
                  skip_self=False)

        _names = {}

        def SB(name, shape, dt):
            i = _names.get(name, 0)
            _names[name] = i + 1
            return nc.sbuf_tensor(f"sb_{name}_{i}", shape, dt)

        bufs = {}

        def B(name):
            if name not in bufs:
                bufs[name] = Buf()
            return bufs[name]

        def dma(q, dst_ap, src_ap, wbuf=None, rbufs=(), extra=(),
                slow=False):
            deps = list(extra)
            if wbuf is not None:
                deps += B(wbuf).write_deps()
            for rb in rbufs:
                deps += B(rb).read_deps()
            d = q.run(lambda h, dst=dst_ap, src=src_ap, slow=slow:
                      h.dma_start(out=dst, in_=src,
                                  allow_slow_non_contiguous=slow),
                      deps, inc=16)
            if wbuf is not None:
                B(wbuf).wrote(d)
            for rb in rbufs:
                B(rb).read(d)
            return d

        def comp(eng, fn, wbufs=(), rbufs=(), extra=()):
            deps = list(extra)
            for wb in wbufs:
                deps += B(wb).write_deps()
            for rb in rbufs:
                deps += B(rb).read_deps()
            d = eng.run(fn, deps, inc=1)
            for wb in wbufs:
                B(wb).wrote(d)
            for rb in rbufs:
                B(rb).read(d)
            return d

        def barrier():
            engs = (pe, dve, act, gp, gpq, sq)
            marks = [(e, e.n) for e in engs]
            for e in engs:
                for m in marks:
                    e.wait(m)

        ps = nc.alloc_psum_tensor("ps", [128, 3072], F32)
        ps_ap = ps.ap()
        psb = nc.alloc_psum_tensor("psb", [128, 2048], BF16)
        psb_ap = psb.ap()

        def bank(i, w=512):
            return ps_ap[:, i * 512:i * 512 + w]

        def bank_bf(i, w=1024):
            return psb_ap[:, i * 1024:i * 1024 + w]

        def sb(name, shape, dt):
            return nc.alloc_sbuf_tensor("sbr_" + name, shape, dt).ap()

        ident32 = sb("ident32", [128, 128], F32)
        ident16 = sb("ident16", [128, 128], BF16)
        ones1 = sb("ones1", [1, 128], F32)
        jrow2 = sb("jrow2", [128, S], F32)
        maskt = sb("maskt", [128, 128], F32)
        khi = sb("khi", [128, S], BF16)
        klo = sb("klo", [128, S], BF16)
        vhi = sb("vhi", [128, S], BF16)
        vlo = sb("vlo", [128, S], BF16)
        swg_c = sb("swg_c", [128, NIC], F32)
        swu_c = sb("swu_c", [128, NIC], F32)
        swd_c = sb("swd_c", [128, NHC], F32)
        invg_r = sb("invg_r", [1, H], F32)
        invu_r = sb("invu_r", [1, H], F32)
        invi_r = sb("invi_r", [1, I_FF], F32)

        c127 = sb("c127", [128, 1], F32)
        inv127c = sb("inv127c", [128, 1], F32)
        c128 = sb("c128", [128, 1], F32)
        scalec = sb("scalec", [128, 1], F32)
        comp(dve, lambda h: h.memset(c127, 127.0), wbufs=("c127",))
        comp(dve, lambda h: h.reciprocal(inv127c, c127),
             wbufs=("inv127c",), rbufs=("c127",))
        comp(dve, lambda h: h.memset(c128, 128.0), wbufs=("c128",))
        comp(act, lambda h: h.activation(c128, c128, AF.Sqrt),
             wbufs=("c128",), rbufs=("c128",))
        comp(dve, lambda h: h.reciprocal(scalec, c128),
             wbufs=("scalec",), rbufs=("c128",))
        m3c = sb("m3c", [128, 1], F32)
        magc = sb("magc", [128, 1], F32)
        comp(dve, lambda h: h.memset(m3c, 1.0), wbufs=("m3c",))
        comp(dve, lambda h: h.tensor_single_scalar(
                m3c, m3c, float(2.0 ** -30), OP.add),
             wbufs=("m3c",), rbufs=("m3c",))
        comp(dve, lambda h: h.tensor_single_scalar(
                m3c, m3c, -1.0, OP.add),
             wbufs=("m3c",), rbufs=("m3c",))
        comp(dve, lambda h: h.tensor_single_scalar(
                m3c, m3c, float(1.5 * 2.0 ** 82), OP.mult),
             wbufs=("m3c",), rbufs=("m3c",))
        comp(dve, lambda h: h.tensor_single_scalar(
                magc, m3c, 12582912.0, OP.add),
             wbufs=("magc",), rbufs=("m3c",))
        dma(sq, ident32, id32_d, wbuf="ident32")
        dma(sq, ident16, id16_d, wbuf="ident16")
        dma(sq, ones1, ones_d, wbuf="ones1")
        dma(sq, jrow2, jrow_d, wbuf="jrow2")
        dma(sq, maskt, mask_d, wbuf="maskt")
        dma(sq, swg_c, swg_d.rearrange("(c p) -> p c", p=128), wbuf="swg", slow=True)
        dma(sq, swu_c, swu_d.rearrange("(c p) -> p c", p=128), wbuf="swu", slow=True)
        dma(sq, swd_c, swd_d.rearrange("(c p) -> p c", p=128), wbuf="swd", slow=True)
        dma(sq, invg_r, invg_d, wbuf="invg_r")
        dma(sq, invu_r, invu_d, wbuf="invu_r")
        dma(sq, invi_r, invi_d, wbuf="invi_r")

        def replicate_row(row_ap, row_buf, n, dst_ap, dst_buf, bank_i=2):
            for c0 in range(0, n, 512):
                w = min(512, n - c0)
                comp(pe, lambda h, row_ap=row_ap, c0=c0, w=w, bank_i=bank_i:
                     h.matmul(bank(bank_i, w), ones1[:1, :],
                              row_ap[:1, c0:c0 + w], start=True, stop=True),
                     wbufs=(f"ps{bank_i}",), rbufs=("ones1", row_buf))
                comp(dve, lambda h, dst_ap=dst_ap, c0=c0, w=w, bank_i=bank_i:
                     h.tensor_copy(dst_ap[:, c0:c0 + w], bank(bank_i, w)),
                     wbufs=(dst_buf,), rbufs=(f"ps{bank_i}",))

        def transpose_batch(src_ap, src_buf, n_tiles, dst_fn, dst_buf,
                            ident, ident_buf, bf, banks=(4, 5)):
            per = 8 if bf else 4
            g0 = 0
            while g0 < n_tiles:
                cnt = min(per, n_tiles - g0)
                if bf:
                    bi = (g0 // per) % 2
                    wb = f"psb{bi}"
                else:
                    bi = banks[(g0 // per) % len(banks)]
                    wb = f"ps{bi}"
                for j in range(cnt):
                    tl = g0 + j
                    out_ap = (bank_bf(bi)[:, j * 128:(j + 1) * 128] if bf
                              else bank(bi)[:, j * 128:(j + 1) * 128])
                    comp(pe, lambda h, out_ap=out_ap, tl=tl:
                         h.transpose(out_ap,
                                     src_ap[:, tl * 128:(tl + 1) * 128],
                                     ident),
                         wbufs=(wb,),
                         rbufs=(src_buf, ident_buf) if j == 0
                               else (src_buf,))
                dst_ap = dst_fn(g0, cnt)
                src2 = (bank_bf(bi)[:, :cnt * 128] if bf
                        else bank(bi)[:, :cnt * 128])
                comp(dve, lambda h, dst_ap=dst_ap, src2=src2:
                     h.tensor_copy(dst_ap, src2),
                     wbufs=(dst_buf,), rbufs=(wb,))
                g0 += cnt

        def emit_bf16_dround(V, Vbuf, A, Abuf, Bv, Bbuf, D, Dbuf, E, Ebuf,
                             O, Obuf):
            EXPM = 0x7F800000
            comp(dve, lambda h: h.tensor_single_scalar(
                    E, V.bitcast(I32), EXPM, OP.bitwise_and),
                 wbufs=(Ebuf,), rbufs=(Vbuf,))
            comp(dve, lambda h: h.tensor_single_scalar(
                    E, E, float(23 * 2 ** 23), OP.subtract),
                 wbufs=(Ebuf,), rbufs=(Ebuf,))
            comp(dve, lambda h: h.tensor_scalar(
                    Bv, E.bitcast(F32), m3c, None, OP.mult),
                 wbufs=(Bbuf,), rbufs=(Ebuf, "m3c"))
            comp(dve, lambda h: h.tensor_tensor(A, V, Bv, OP.add),
                 wbufs=(Abuf,), rbufs=(Vbuf, Bbuf))
            comp(dve, lambda h: h.tensor_tensor(A, A, Bv, OP.subtract),
                 wbufs=(Abuf,), rbufs=(Abuf, Bbuf))
            comp(dve, lambda h: h.tensor_single_scalar(
                    E, A.bitcast(I32), EXPM, OP.bitwise_and),
                 wbufs=(Ebuf,), rbufs=(Abuf,))
            comp(dve, lambda h: h.tensor_single_scalar(
                    E, E, float(7 * 2 ** 23), OP.subtract),
                 wbufs=(Ebuf,), rbufs=(Ebuf,))
            comp(dve, lambda h: h.tensor_scalar(
                    Bv, E.bitcast(F32), m3c, None, OP.mult),
                 wbufs=(Bbuf,), rbufs=(Ebuf, "m3c"))
            comp(dve, lambda h: h.tensor_tensor(A, A, Bv, OP.add),
                 wbufs=(Abuf,), rbufs=(Abuf, Bbuf))
            comp(dve, lambda h: h.tensor_tensor(O, A, Bv, OP.subtract),
                 wbufs=(Obuf,), rbufs=(Abuf, Bbuf))

        with contextlib.ExitStack() as _st:
            wkv32_h = _st.enter_context(SB("wkv32", [128, NHC * 32], F32))
            wkvh_h = _st.enter_context(SB("wkvh", [128, NHC * 256], BF16))
            wkvl_h = _st.enter_context(SB("wkvl", [128, NHC * 256], BF16))
            p1x_h = _st.enter_context(SB("p1x", [128, H], F32))
            p1xn_h = _st.enter_context(SB("p1xn", [128, H], F32))
            p1hi_h = _st.enter_context(SB("p1hi", [128, H], BF16))
            p1lo_h = _st.enter_context(SB("p1lo", [128, H], BF16))
            p1tr_h = _st.enter_context(SB("p1tr", [128, H], BF16))
            wpb_h = _st.enter_context(SB("wpb", [128, H], BF16))
            wpt_h = _st.enter_context(SB("wpt", [128, H], BF16))
            p1fmh_h = _st.enter_context(SB("p1fmh", [128, NHC * 128], BF16))
            p1fml_h = _st.enter_context(SB("p1fml", [128, NHC * 128], BF16))
            p1st_h = _st.enter_context(SB("p1st", [128, 8], F32))
            p1kv_h = _st.enter_context(SB("p1kv", [128, 128], F32))
            p1xc_h = _st.enter_context(SB("p1xc", [128, H // 4], F32))
            p1xw_h = _st.enter_context(SB("p1xw", [128, H // 4], F32))
            p1xe_h = _st.enter_context(SB("p1xe", [128, H // 4], I32))
            p1xd_h = _st.enter_context(SB("p1xd", [128, H // 4], F32))
            wkv32, wkvh, wkvl = wkv32_h[:], wkvh_h[:], wkvl_h[:]
            p1x, p1xn = p1x_h[:], p1xn_h[:]
            p1hi, p1lo, p1tr = p1hi_h[:], p1lo_h[:], p1tr_h[:]
            wpb, wpt = wpb_h[:], wpt_h[:]
            p1fmh, p1fml = p1fmh_h[:], p1fml_h[:]
            p1st, p1kv = p1st_h[:], p1kv_h[:]
            p1xc, p1xw, p1xe = p1xc_h[:], p1xw_h[:], p1xe_h[:]
            p1xd = p1xd_h[:]

            for grp in range(8):
                for j in range(4):
                    hc = grp * 4 + j
                    dma(sq, wkv32[:, j * 256:(j + 1) * 256],
                        wkv_d[hc * 128:(hc + 1) * 128, :], wbuf="wkv32")
                sl = slice(grp * 1024, (grp + 1) * 1024)
                comp(dve, lambda h, sl=sl: h.tensor_copy(
                        wkvh[:, sl], wkv32),
                     wbufs=("wkvh",), rbufs=("wkv32",))
                comp(gp, lambda h, sl=sl: h.tensor_tensor(
                        wkvl[:, sl], wkv32, wkvh[:, sl], OP.subtract),
                     wbufs=("wkvl",), rbufs=("wkv32", "wkvh"))

            for src_d, dst_s, nch, wid in ((wg_d, wgT_s, NIC, H),
                                           (wu_d, wuT_s, NIC, H),
                                           (wd_d, wdT_s, NHC, I_FF)):
                for c in range(nch):
                    dma(gpq, wpb[:, :wid], src_d[c * 128:(c + 1) * 128, :],
                        wbuf="wpb")
                    transpose_batch(
                        wpb[:, :wid], "wpb", wid // 128,
                        lambda g0, cnt: wpt[:, g0 * 128:(g0 + cnt) * 128],
                        "wpt", ident16, "ident16", bf=True)
                    dma(sq, dst_s[c * 128:(c + 1) * 128, :wid],
                        wpt[:, :wid], rbufs=("wpt",))
            for t in range(NT):
                xb = "p1x"
                xt = p1x
                dma(sq, xt, x_d[t * 128:(t + 1) * 128, :], wbuf=xb)
                comp(dve, lambda h, xt=xt: h.reduce_sum(p1st[:, 0:1], xt, AX),
                     wbufs=("p1s1",), rbufs=(xb,))
                for qq in range(4):
                    sl = slice(qq * (H // 4), (qq + 1) * (H // 4))
                    comp(act, lambda h, xt=xt, sl=sl: h.activation(
                            p1xc, xt[:, sl], AF.Square),
                         wbufs=("p1xc",), rbufs=(xb,))
                    comp(dve, lambda h, qq=qq: h.reduce_sum(
                            p1st[:, 1:2] if qq == 0 else p1st[:, 7:8],
                            p1xc, AX),
                         wbufs=("p1s2" if qq == 0 else "p1ve",),
                         rbufs=("p1xc",))
                    if qq > 0:
                        comp(dve, lambda h: h.tensor_tensor(
                                p1st[:, 1:2], p1st[:, 1:2], p1st[:, 7:8],
                                OP.add),
                             wbufs=("p1s2",), rbufs=("p1s2", "p1ve"))
                comp(dve, lambda h: h.tensor_scalar(
                        p1st[:, 2:3], p1st[:, 0:1], 1.0 / H, None, OP.mult),
                     wbufs=("p1mu",), rbufs=("p1s1",))
                comp(dve, lambda h: h.tensor_tensor(
                        p1st[:, 3:4], p1st[:, 2:3], p1st[:, 2:3], OP.mult),
                     wbufs=("p1mu2",), rbufs=("p1mu",))
                comp(dve, lambda h: h.scalar_tensor_tensor(
                        p1st[:, 4:5], p1st[:, 1:2], 1.0 / H, p1st[:, 3:4],
                        OP.mult, OP.subtract),
                     wbufs=("p1var",), rbufs=("p1s2", "p1mu2"))
                comp(dve, lambda h: h.tensor_scalar(
                        p1st[:, 7:8], p1st[:, 4:5], EPS, None, OP.add),
                     wbufs=("p1ve",), rbufs=("p1var",))
                comp(act, lambda h: h.activation(
                        p1st[:, 5:6], p1st[:, 7:8], AF.Sqrt),
                     wbufs=("p1sd",), rbufs=("p1ve",))
                comp(dve, lambda h: h.reciprocal(p1st[:, 6:7], p1st[:, 5:6]),
                     wbufs=("p1istd",), rbufs=("p1sd",))
                comp(dve, lambda h, xt=xt: h.tensor_scalar(
                        p1xn, xt, p1st[:, 2:3], p1st[:, 6:7],
                        OP.subtract, OP.mult),
                     wbufs=("p1xn",), rbufs=(xb, "p1mu", "p1istd"))
                if tap == "st":
                    dma(gpq, dbg_d[t * 128:(t + 1) * 128, 0:8], p1st,
                        rbufs=("p1s1", "p1s2", "p1mu", "p1mu2", "p1var",
                               "p1sd", "p1istd", "p1ve"))
                if tap == "xn":
                    dma(gpq, dbg_d[t * 128:(t + 1) * 128, :], p1xn,
                        rbufs=("p1xn",))
                if tap == "x":
                    dma(gpq, dbg_d[t * 128:(t + 1) * 128, :], xt,
                        rbufs=(xb,))
                comp(dve, lambda h: h.tensor_copy(p1hi, p1xn),
                     wbufs=("p1hi",), rbufs=("p1xn",))
                comp(gp, lambda h: h.tensor_tensor(
                        p1lo, p1xn, p1hi, OP.subtract),
                     wbufs=("p1lo",), rbufs=("p1xn", "p1hi"))
                dma(sq, xnhi_s[t * 128:(t + 1) * 128, :], p1hi,
                    rbufs=("p1hi",))
                dma(sq, xnlo_s[t * 128:(t + 1) * 128, :], p1lo,
                    rbufs=("p1lo",))
                for hhalf in range(4):
                    sl = slice(hhalf * (H // 4), (hhalf + 1) * (H // 4))
                    emit_bf16_dround(p1xn[:, sl], "p1xn", p1xc, "p1xc",
                                     p1xw, "p1xw", p1xd, "p1xd",
                                     p1xe, "p1xe", p1tr[:, sl], "p1tr")
                dma(sq, xf_s[t * 128:(t + 1) * 128, :], p1tr,
                    rbufs=("p1tr",))
                transpose_batch(p1hi, "p1hi", NHC,
                                lambda g0, cnt: p1fmh[:, g0 * 128:
                                                      (g0 + cnt) * 128],
                                "p1fmh", ident16, "ident16", bf=True)
                transpose_batch(p1lo, "p1lo", NHC,
                                lambda g0, cnt: p1fml[:, g0 * 128:
                                                      (g0 + cnt) * 128],
                                "p1fml", ident16, "ident16", bf=True)
                for fmt, dst_s, fb in ((p1fmh, xnhfm_s, "p1fmh"),
                                       (p1fml, xnlfm_s, "p1fml")):
                    dma(gpq,
                        dst_s.rearrange("(hc p) s -> p hc s", p=128)
                        [:, :, t * 128:(t + 1) * 128],
                        fmt.rearrange("p (hc c) -> p hc c", c=128),
                        rbufs=(fb,))
                for oc in range(2):
                    wb = "ps4"
                    seq = [(wkvh, p1fmh, "wkvh", "p1fmh"),
                           (wkvl, p1fmh, "wkvl", "p1fmh"),
                           (wkvh, p1fml, "wkvh", "p1fml")]
                    idx = 0
                    for hc in range(NHC):
                        for si, (wt, mt, wbn, mbn) in enumerate(seq):
                            st = (idx == 0)
                            sp = (idx == NHC * 3 - 1)
                            comp(pe, lambda h, wt=wt, mt=mt, hc=hc, oc=oc,
                                 st=st, sp=sp:
                                 h.matmul(bank(4, 128),
                                          wt[:, hc * 256 + oc * 128:
                                             hc * 256 + oc * 128 + 128],
                                          mt[:, hc * 128:(hc + 1) * 128],
                                          start=st, stop=sp),
                                 wbufs=(wb,),
                                 rbufs=(wbn, mbn))
                            idx += 1
                    if oc == 0:
                        comp(dve, lambda h, t=t: h.tensor_copy(
                                khi[:, t * 128:(t + 1) * 128], bank(4, 128)),
                             wbufs=("khi",), rbufs=(wb,))
                        comp(dve, lambda h, t=t: h.scalar_tensor_tensor(
                                klo[:, t * 128:(t + 1) * 128], bank(4, 128),
                                0.0, khi[:, t * 128:(t + 1) * 128],
                                OP.bypass, OP.subtract),
                             wbufs=("klo",), rbufs=(wb, "khi"))
                    else:
                        comp(dve, lambda h: h.tensor_copy(p1kv, bank(4, 128)),
                             wbufs=("p1kv",), rbufs=(wb,))
                        comp(pe, lambda h: h.transpose(
                                bank(5, 128), p1kv, ident32),
                             wbufs=("ps5",), rbufs=("p1kv", "ident32"))
                        comp(act, lambda h, t=t: h.activation(
                                vhi[:, t * 128:(t + 1) * 128], bank(5, 128),
                                AF.Copy),
                             wbufs=("vhi",), rbufs=("ps5",))
                        comp(dve, lambda h, t=t: h.scalar_tensor_tensor(
                                vlo[:, t * 128:(t + 1) * 128], bank(5, 128),
                                0.0, vhi[:, t * 128:(t + 1) * 128],
                                OP.bypass, OP.subtract),
                             wbufs=("vlo",), rbufs=("ps5", "vhi"))
            if tap == "kv":
                dma(gpq, dbg_d[0:128, 0:S].bitcast(BF16)[:, :S], khi,
                    rbufs=("khi",))
                dma(gpq, dbg_d[128:256, 0:S].bitcast(BF16)[:, :S], vhi,
                    rbufs=("vhi",))
        barrier()

        def phase2_block(b):
            t0 = b * TQ

            def qproj():
                with contextlib.ExitStack() as _s1:
                    p2fmh_h = _s1.enter_context(
                        SB("p2fmh", [128, NHC * TQ], BF16))
                    p2fml_h = _s1.enter_context(
                        SB("p2fml", [128, NHC * TQ], BF16))
                    p2fmh, p2fml = p2fmh_h[:], p2fml_h[:]
                    for src_s, fmt, fb in ((xnhfm_s, p2fmh, "p2fmh"),
                                           (xnlfm_s, p2fml, "p2fml")):
                        fmt3 = fmt.rearrange("p (c m) -> p c m", m=TQ)
                        dma(sq, fmt3,
                            src_s.rearrange("(hc p) s -> p hc s", p=128)
                            [:, :, t0:t0 + TQ],
                            wbuf=fb)
                    barrier()
                    with contextlib.ExitStack() as _s2:
                        wq32_h = _s2.enter_context(
                            SB("wq32", [128, 2 * 512], F32))
                        wqh_h = _s2.enter_context(
                            SB("wqh", [128, 2 * 512], BF16))
                        wql_h = _s2.enter_context(
                            SB("wql", [128, 2 * 512], BF16))
                        q32_h = _s2.enter_context(
                            SB("q32", [128, 2 * TQ], F32))
                        qhib_h = _s2.enter_context(
                            SB("qhib", [128, 2 * TQ], BF16))
                        qlob_h = _s2.enter_context(
                            SB("qlob", [128, 2 * TQ], BF16))
                        wq32_2, wqh2, wql2 = wq32_h[:], wqh_h[:], wql_h[:]
                        q32_2, qhib2, qlob2 = q32_h[:], qhib_h[:], qlob_h[:]
                        for ocg in range(NHC // 4):
                            for hc in range(NHC):
                                par = hc % 2
                                w32 = wq32_2[:, par * 512:(par + 1) * 512]
                                wqh = wqh2[:, par * 512:(par + 1) * 512]
                                wql = wql2[:, par * 512:(par + 1) * 512]
                                w32b, whb, wlb = (f"wq32{par}",
                                                  f"wqh{par}", f"wql{par}")
                                dma(gpq if par == 0 else sq, w32,
                                    wq_d[hc * 128:(hc + 1) * 128,
                                         ocg * 512:(ocg + 1) * 512],
                                    wbuf=w32b)
                                comp(dve, lambda h, wqh=wqh, w32=w32:
                                     h.tensor_copy(wqh, w32),
                                     wbufs=(whb,), rbufs=(w32b,))
                                comp(gp, lambda h, wql=wql, w32=w32,
                                     wqh=wqh:
                                     h.tensor_tensor(wql, w32, wqh,
                                                     OP.subtract),
                                     wbufs=(wlb,), rbufs=(w32b, whb))
                                for j in range(4):
                                    seq = [(wqh, p2fmh, whb, "p2fmh"),
                                           (wql, p2fmh, wlb, "p2fmh"),
                                           (wqh, p2fml, whb, "p2fml")]
                                    for si, (wt, mt, wbn, mbn) in \
                                            enumerate(seq):
                                        st = (hc == 0 and si == 0)
                                        sp = (hc == NHC - 1 and si == 2)
                                        comp(pe, lambda h, wt=wt, mt=mt,
                                             hc=hc, j=j, st=st, sp=sp:
                                             h.matmul(
                                                bank(j, TQ),
                                                wt[:, j * 128:
                                                   (j + 1) * 128],
                                                mt[:, hc * TQ:
                                                   (hc + 1) * TQ],
                                                start=st, stop=sp),
                                             wbufs=(f"ps{j}",),
                                             rbufs=(wbn, mbn))
                            for j in range(4):
                                oc = ocg * 4 + j
                                jp = j % 2
                                qhib = qhib2[:, jp * TQ:(jp + 1) * TQ]
                                qlob = qlob2[:, jp * TQ:(jp + 1) * TQ]
                                qhb, qlb = f"qhib{jp}", f"qlob{jp}"
                                comp(act, lambda h, qhib=qhib, j=j:
                                     h.activation(qhib, bank(j, TQ),
                                                  AF.Copy, scale=scalec),
                                     wbufs=(qhb,),
                                     rbufs=(f"ps{j}", "scalec"))
                                comp(dve, lambda h, qlob=qlob, qhib=qhib,
                                     j=j:
                                     h.scalar_tensor_tensor(
                                        qlob, bank(j, TQ), scalec, qhib,
                                        OP.mult, OP.subtract),
                                     wbufs=(qlb,),
                                     rbufs=(f"ps{j}", qhb, "scalec"))
                                dma(sq, qhi_s[oc * 128:(oc + 1) * 128, :],
                                    qhib, rbufs=(qhb,))
                                dma(sq, qlo_s[oc * 128:(oc + 1) * 128, :],
                                    qlob, rbufs=(qlb,))
            qproj()
            barrier()

            def attention():
                with contextlib.ExitStack() as _s1:
                    biasb_h = _s1.enter_context(SB("biasb", [128, S], I32))
                    s2b_h = _s1.enter_context(
                        SB("s2b", [128, NQS * S], F32))
                    e32_h = _s1.enter_context(
                        SB("e32", [128, NQS * S], F32))
                    mx_h = _s1.enter_context(SB("mx", [128, 2 * NQS], F32))
                    sume_h = _s1.enter_context(SB("sume", [128, NQS], F32))
                    ehiT_h = _s1.enter_context(SB("ehiT", [128, 1024], BF16))
                    eloT_h = _s1.enter_context(SB("eloT", [128, 1024], BF16))
                    repr_h = _s1.enter_context(SB("repr", [1, TQ], F32))
                    reps_h = _s1.enter_context(SB("reps", [128, TQ], F32))
                    ao32_h = _s1.enter_context(SB("ao32", [128, TQ], F32))
                    aohib_h = _s1.enter_context(
                        SB("aohib", [128, 2 * TQ], BF16))
                    aolob_h = _s1.enter_context(
                        SB("aolob", [128, 2 * TQ], BF16))
                    qhib_h = _s1.enter_context(
                        SB("qhibA", [128, 2 * TQ], BF16))
                    qlob_h = _s1.enter_context(
                        SB("qlobA", [128, 2 * TQ], BF16))
                    bias_view = biasb_h[:].bitcast(F32)
                    s2b2, e32_2 = s2b_h[:], e32_h[:]
                    mx, sume = mx_h[:], sume_h[:]
                    ehiT2, eloT2 = ehiT_h[:], eloT_h[:]
                    repr_, reps = repr_h[:], reps_h[:]
                    ao32 = ao32_h[:]
                    aohib2, aolob2 = aohib_h[:], aolob_h[:]
                    qhib2, qlob2 = qhib_h[:], qlob_h[:]
                    for hh in range(NH):
                        hp = hh % 2
                        qhib = qhib2[:, hp * TQ:(hp + 1) * TQ]
                        qlob = qlob2[:, hp * TQ:(hp + 1) * TQ]
                        qhb, qlb = f"qhibA{hp}", f"qlobA{hp}"
                        aohib = aohib2[:, hp * TQ:(hp + 1) * TQ]
                        aolob = aolob2[:, hp * TQ:(hp + 1) * TQ]
                        ahb, alb = f"aohib{hp}", f"aolob{hp}"
                        dma(sq, qhib, qhi_s[hh * 128:(hh + 1) * 128, :],
                            wbuf=qhb)
                        dma(sq, qlob, qlo_s[hh * 128:(hh + 1) * 128, :],
                            wbuf=qlb)
                        comp(dve, lambda h, hh=hh: h.tensor_scalar(
                                bias_view, jrow2, slopes[hh], None,
                                OP.mult),
                             wbufs=("biasb",), rbufs=("jrow2",))
                        comp(dve, lambda h: h.tensor_tensor(
                                bias_view[:, S - 128:S],
                                bias_view[:, S - 128:S], maskt, OP.add),
                             wbufs=("biasb",), rbufs=("biasb", "maskt"))
                        infos = []
                        for qs in range(NQS):
                            pq = qs
                            q0 = t0 + qs * 128
                            nk = q0 // 128 + 1
                            k_hi = nk * 128
                            infos.append((qs, pq, nk, k_hi,
                                          s2b2[:, pq * S:(pq + 1) * S],
                                          e32_2[:, pq * S:(pq + 1) * S],
                                          f"s2b{pq}", f"e32{pq}"))
                        for qs, pq, nk, k_hi, s2b, e32, s2n, e3n in infos:
                            nkt = (k_hi + 511) // 512
                            for kt in range(nkt):
                                ntk = min(512, k_hi - kt * 512)
                                bk = 2 * (qs % 2) + (kt % 2)
                                wb = f"ps{bk}"
                                seq = [(qhib, khi, qhb, "khi"),
                                       (qlob, khi, qlb, "khi"),
                                       (qhib, klo, qhb, "klo")]
                                for si, (qt, ktt, qbn, kbn) in \
                                        enumerate(seq):
                                    comp(pe, lambda h, qt=qt, ktt=ktt,
                                         qs=qs, kt=kt, ntk=ntk, si=si,
                                         bk=bk:
                                         h.matmul(
                                            bank(bk, ntk),
                                            qt[:, qs * 128:(qs + 1) * 128],
                                            ktt[:, kt * 512:
                                                kt * 512 + ntk],
                                            start=(si == 0),
                                            stop=(si == 2)),
                                         wbufs=(wb,),
                                         rbufs=(qbn, kbn))
                                comp(dve, lambda h, kt=kt, ntk=ntk,
                                     k_hi=k_hi, s2b=s2b, bk=bk:
                                     h.tensor_tensor(
                                        s2b[:, kt * 512:kt * 512 + ntk],
                                        bank(bk, ntk),
                                        bias_view[:, S - k_hi + kt * 512:
                                                  S - k_hi + kt * 512
                                                  + ntk],
                                        OP.add),
                                     wbufs=(s2n,), rbufs=(wb, "biasb"))
                        for qs, pq, nk, k_hi, s2b, e32, s2n, e3n in infos:
                            comp(dve, lambda h, k_hi=k_hi, s2b=s2b, pq=pq:
                                 h.reduce_max(
                                    mx[:, 2 * pq + 1:2 * pq + 2],
                                    s2b[:, :k_hi], AX, negate=True),
                                 wbufs=(f"mx1_{pq}",), rbufs=(s2n,))
                            comp(act, lambda h, k_hi=k_hi, s2b=s2b,
                                 e32=e32, pq=pq: h.activation(
                                    e32[:, :k_hi], s2b[:, :k_hi], AF.Exp,
                                    bias=mx[:, 2 * pq + 1:2 * pq + 2]),
                                 wbufs=(e3n,), rbufs=(s2n, f"mx1_{pq}"))
                        for qs, pq, nk, k_hi, s2b, e32, s2n, e3n in infos:
                            comp(dve, lambda h, k_hi=k_hi, qs=qs, e32=e32:
                                 h.reduce_sum(
                                    sume[:, qs:qs + 1], e32[:, :k_hi],
                                    AX),
                                 wbufs=("sume",), rbufs=(e3n,))
                            for g in range((nk + 3) // 4):
                                njj = min(4, nk - g * 4)
                                gp2 = g % 2
                                ebk = 4
                                ehiT = ehiT2[:, gp2 * 512:(gp2 + 1) * 512]
                                eloT = eloT2[:, gp2 * 512:(gp2 + 1) * 512]
                                ehn, eln = f"ehiT{gp2}", f"eloT{gp2}"
                                for j in range(njj):
                                    kc = g * 4 + j
                                    comp(pe, lambda h, kc=kc, j=j,
                                         e32=e32, ebk=ebk:
                                         h.transpose(
                                            bank(ebk)[:, j * 128:
                                                      (j + 1) * 128],
                                            e32[:, kc * 128:
                                                (kc + 1) * 128],
                                            ident32),
                                         wbufs=(f"ps{ebk}",),
                                         rbufs=(e3n, "ident32") if j == 0
                                               else (e3n,))
                                comp(act, lambda h, njj=njj, ehiT=ehiT,
                                     ebk=ebk: h.activation(
                                        ehiT[:, :njj * 128],
                                        bank(ebk, njj * 128), AF.Copy),
                                     wbufs=(ehn,), rbufs=(f"ps{ebk}",))
                                comp(dve, lambda h, njj=njj, eloT=eloT,
                                     ehiT=ehiT, ebk=ebk:
                                     h.scalar_tensor_tensor(
                                        eloT[:, :njj * 128],
                                        bank(ebk, njj * 128), 0.0,
                                        ehiT[:, :njj * 128],
                                        OP.bypass, OP.subtract),
                                     wbufs=(eln,), rbufs=(f"ps{ebk}", ehn))
                                for j in range(njj):
                                    kc = g * 4 + j
                                    seq = [(vhi, ehiT, "vhi", ehn),
                                           (vlo, ehiT, "vlo", ehn),
                                           (vhi, eloT, "vhi", eln)]
                                    for si, (vt, et, vbn, ebn) in \
                                            enumerate(seq):
                                        st = (kc == 0 and si == 0)
                                        sp = (kc == nk - 1 and si == 2)
                                        comp(pe, lambda h, vt=vt, et=et,
                                             kc=kc, j=j, qs=qs, st=st,
                                             sp=sp:
                                             h.matmul(
                                                bank(5)[:, qs * 128:
                                                        (qs + 1) * 128],
                                                vt[:, kc * 128:
                                                   (kc + 1) * 128],
                                                et[:, j * 128:
                                                   (j + 1) * 128],
                                                start=st, stop=sp),
                                             wbufs=("ps5",),
                                             rbufs=(vbn, ebn))
                        for qs in range(NQS):
                            comp(pe, lambda h, qs=qs: h.transpose(
                                    bank(1)[0:1, qs * 128:(qs + 1) * 128],
                                    sume[:, qs:qs + 1], ident32),
                                 wbufs=("ps1",),
                                 rbufs=("sume", "ident32"))
                        comp(dve, lambda h: h.reciprocal(
                                repr_[:1, :], bank(1, TQ)[0:1, :]),
                             wbufs=("repr",), rbufs=("ps1",))
                        replicate_row(repr_, "repr", TQ, reps, "reps",
                                      bank_i=0)
                        comp(dve, lambda h: h.scalar_tensor_tensor(
                                ao32, bank(5, TQ), 0.0, reps,
                                OP.bypass, OP.mult),
                             wbufs=("ao32",), rbufs=("ps5", "reps"))
                        comp(dve, lambda h, aohib=aohib:
                             h.tensor_copy(aohib, ao32),
                             wbufs=(ahb,), rbufs=("ao32",))
                        comp(gp, lambda h, aolob=aolob, aohib=aohib:
                             h.tensor_tensor(aolob, ao32, aohib,
                                             OP.subtract),
                             wbufs=(alb,), rbufs=("ao32", ahb))
                        dma(sq, aohi_s[hh * 128:(hh + 1) * 128, :], aohib,
                            rbufs=(ahb,))
                        dma(sq, aolo_s[hh * 128:(hh + 1) * 128, :], aolob,
                            rbufs=(alb,))
            attention()
            barrier()

            def wo():
                with contextlib.ExitStack() as _s1:
                    aofh_h = _s1.enter_context(
                        SB("aofh", [128, NHC * TQ], BF16))
                    aofl_h = _s1.enter_context(
                        SB("aofl", [128, NHC * TQ], BF16))
                    wq32_h = _s1.enter_context(
                        SB("wo32", [128, 2 * 512], F32))
                    wqh_h = _s1.enter_context(
                        SB("woh", [128, 2 * 512], BF16))
                    wql_h = _s1.enter_context(
                        SB("wol", [128, 2 * 512], BF16))
                    at32_h = _s1.enter_context(SB("at32", [128, 2 * TQ], F32))
                    aofh, aofl = aofh_h[:], aofl_h[:]
                    wq32_2, wqh2, wql2 = wq32_h[:], wqh_h[:], wql_h[:]
                    at32_2 = at32_h[:]
                    for hc in range(NHC):
                        dma(sq, aofh[:, hc * TQ:(hc + 1) * TQ],
                            aohi_s[hc * 128:(hc + 1) * 128, :], wbuf="aofh")
                        dma(sq, aofl[:, hc * TQ:(hc + 1) * TQ],
                            aolo_s[hc * 128:(hc + 1) * 128, :], wbuf="aofl")
                    for ocg in range(NHC // 4):
                        for hc in range(NHC):
                            par = hc % 2
                            w32 = wq32_2[:, par * 512:(par + 1) * 512]
                            wqh = wqh2[:, par * 512:(par + 1) * 512]
                            wql = wql2[:, par * 512:(par + 1) * 512]
                            w32b, whb, wlb = (f"wo32{par}", f"woh{par}",
                                              f"wol{par}")
                            dma(gpq if par == 0 else sq, w32,
                                wo_d[hc * 128:(hc + 1) * 128,
                                     ocg * 512:(ocg + 1) * 512],
                                wbuf=w32b)
                            comp(dve, lambda h, wqh=wqh, w32=w32:
                                 h.tensor_copy(wqh, w32),
                                 wbufs=(whb,), rbufs=(w32b,))
                            comp(gp, lambda h, wql=wql, w32=w32, wqh=wqh:
                                 h.tensor_tensor(wql, w32, wqh,
                                                 OP.subtract),
                                 wbufs=(wlb,), rbufs=(w32b, whb))
                            for j in range(4):
                                seq = [(wqh, aofh, whb, "aofh"),
                                       (wql, aofh, wlb, "aofh"),
                                       (wqh, aofl, whb, "aofl")]
                                for si, (wt, mt, wbn, mbn) in \
                                        enumerate(seq):
                                    st = (hc == 0 and si == 0)
                                    sp = (hc == NHC - 1 and si == 2)
                                    comp(pe, lambda h, wt=wt, mt=mt, hc=hc,
                                         j=j, st=st, sp=sp:
                                         h.matmul(
                                            bank(j, TQ),
                                            wt[:, j * 128:(j + 1) * 128],
                                            mt[:, hc * TQ:(hc + 1) * TQ],
                                            start=st, stop=sp),
                                         wbufs=(f"ps{j}",),
                                         rbufs=(wbn, mbn))
                        for j in range(4):
                            oc = ocg * 4 + j
                            jp = j % 2
                            at32 = at32_2[:, jp * TQ:(jp + 1) * TQ]
                            atb = f"at32{jp}"
                            comp(act, lambda h, at32=at32, j=j:
                                 h.activation(at32, bank(j, TQ), AF.Copy),
                                 wbufs=(atb,), rbufs=(f"ps{j}",))
                            dma(sq, attn_s[oc * 128:(oc + 1) * 128,
                                           t0:t0 + TQ],
                                at32, rbufs=(atb,))
                            if tap == "attn" and b == 0:
                                dma(gpq, dbg_d[oc * 128:(oc + 1) * 128,
                                               :TQ],
                                    at32, rbufs=(atb,))
            wo()
            barrier()

        for b in range(NB):
            phase2_block(b)
        barrier()

        with contextlib.ExitStack() as _st:
            qgT_h = _st.enter_context(SB("qgT", [128, NHC * TQ3], BF16))
            quT_h = _st.enter_context(SB("quT", [128, NHC * TQ3], BF16))
            intm_h = _st.enter_context(SB("intm", [128, NQS3 * I_FF], BF16))
            qiT_h = _st.enter_context(SB("qiT", [128, NIC * TQ3], BF16))
            stat_h = _st.enter_context(SB("stat", [128, 4], F32))
            sccol_h = _st.enter_context(SB("sccol", [128, 3 * NQS3], F32))
            scrow_h = _st.enter_context(SB("scrow", [1, 3 * TQ3], F32))
            sgrep_h = _st.enter_context(SB("sgrep", [128, TQ3], F32))
            surep_h = _st.enter_context(SB("surep", [128, TQ3], F32))
            sirep_h = _st.enter_context(SB("sirep", [128, TQ3], F32))
            qgT, quT = qgT_h[:], quT_h[:]
            intm, qiT = intm_h[:], qiT_h[:]
            stat, sccol, scrow = stat_h[:], sccol_h[:], scrow_h[:]
            sgrep, surep, sirep = sgrep_h[:], surep_h[:], sirep_h[:]
            qgT3 = qgT.rearrange("p (c m) -> p c m", m=TQ3)
            quT3 = quT.rearrange("p (c m) -> p c m", m=TQ3)
            intm3 = intm.rearrange("p (c m) -> p c m", m=I_FF)
            qiT3 = qiT.rearrange("p (c m) -> p c m", m=TQ3)

            def phase3_block(w):
                t0 = w * TQ3
                with contextlib.ExitStack() as _st:
                    xfb_h = _st.enter_context(SB("xfb", [128, 2 * H], BF16))
                    xsb_h = _st.enter_context(SB("xsb", [128, H // 2], F32))
                    tpb_h = _st.enter_context(SB("tpb", [128, H // 2], F32))
                    qrb_h = _st.enter_context(SB("qrb", [128, H // 2], F32))
                    qtm_h = _st.enter_context(SB("qtm", [128, H], BF16))
                    invgr_h = _st.enter_context(SB("invgr", [128, H], F32))
                    xfb2, xsb, tpb = xfb_h[:], xsb_h[:], tpb_h[:]
                    qrb, qtm = qrb_h[:], qtm_h[:]
                    invgr = invgr_h[:]
                    for branch, (invsrc, invsb, qT3, qTb) in enumerate((
                            (invg_r, "invg_r", qgT3, "qgT"),
                            (invu_r, "invu_r", quT3, "quT"))):
                        invt, invb = invgr, "invgr"
                        replicate_row(invsrc, invsb, H, invgr, "invgr")
                        for tt in range(NQS3):
                            xp = (branch * NQS3 + tt) % 2
                            xfb = xfb2[:, xp * H:(xp + 1) * H]
                            xfn = f"xfb{xp}"
                            dma(sq, xfb,
                                xf_s[t0 + tt * 128:t0 + (tt + 1) * 128, :],
                                wbuf=xfn)
                            for hf in range(2):
                                sl = slice(hf * (H // 2), (hf + 1) * (H // 2))
                                comp(dve, lambda h, invt=invt, sl=sl,
                                     xfb=xfb:
                                     h.tensor_tensor(
                                        xsb, xfb[:, sl], invt[:, sl],
                                        OP.mult),
                                     wbufs=("xsb",), rbufs=(xfn, invb))
                                comp(dve, lambda h, hf=hf: h.reduce_max(
                                        stat[:, hf:hf + 1], xsb, AX,
                                        apply_absolute_value=True),
                                     wbufs=(f"st{hf}",), rbufs=("xsb",))
                            comp(dve, lambda h: h.tensor_tensor(
                                    stat[:, 0:1], stat[:, 0:1],
                                    stat[:, 1:2], OP.max),
                                 wbufs=("st0",), rbufs=("st0", "st1"))
                            comp(dve, lambda h: h.tensor_scalar(
                                    stat[:, 1:2], stat[:, 0:1], inv127c,
                                    1e-8, OP.mult, OP.max),
                                 wbufs=("st1",), rbufs=("st0", "inv127c"))
                            comp(dve, lambda h: h.reciprocal(
                                    stat[:, 2:3], stat[:, 1:2]),
                                 wbufs=("st2",), rbufs=("st1",))
                            comp(dve, lambda h, branch=branch, tt=tt:
                                 h.tensor_copy(
                                    sccol[:, branch * NQS3 + tt:
                                          branch * NQS3 + tt + 1],
                                    stat[:, 1:2]),
                                 wbufs=("sccol",), rbufs=("st1",))
                            for hf in range(2):
                                sl = slice(hf * (H // 2), (hf + 1) * (H // 2))
                                comp(dve, lambda h, invt=invt, sl=sl,
                                     xfb=xfb:
                                     h.tensor_tensor(
                                        xsb, xfb[:, sl], invt[:, sl],
                                        OP.mult),
                                     wbufs=("xsb",), rbufs=(xfn, invb))
                                comp(dve, lambda h: h.tensor_scalar(
                                        tpb, xsb, stat[:, 2:3], magc,
                                        OP.mult, OP.add),
                                     wbufs=("tpb",),
                                     rbufs=("xsb", "st2", "magc"))
                                comp(dve, lambda h: h.tensor_scalar(
                                        qrb, tpb, magc, None, OP.subtract),
                                     wbufs=("qrb",), rbufs=("tpb", "magc"))
                                comp(dve, lambda h, sl=sl: h.tensor_scalar(
                                        qtm[:, sl], qrb, -127.0, 127.0,
                                        OP.max, OP.min),
                                     wbufs=("qtm",), rbufs=("qrb",))
                            transpose_batch(
                                qtm, "qtm", NHC,
                                lambda g0, cnt, tt=tt, qT3=qT3:
                                    qT3[:, g0:g0 + cnt,
                                        tt * 128:(tt + 1) * 128],
                                qTb, ident16, "ident16", bf=True)
                barrier()
                for branch, (rep, repb) in enumerate(((sgrep, "sgrep"),
                                                      (surep, "surep"))):
                    for tt in range(NQS3):
                        comp(pe, lambda h, branch=branch, tt=tt: h.transpose(
                                bank(1)[0:1, tt * 128:(tt + 1) * 128],
                                sccol[:, branch * NQS3 + tt:
                                      branch * NQS3 + tt + 1], ident32),
                             wbufs=("ps1",),
                             rbufs=("sccol", "ident32"))
                    comp(dve, lambda h, branch=branch: h.tensor_copy(
                            scrow[:1, branch * TQ3:(branch + 1) * TQ3],
                            bank(1, TQ3)[0:1, :]),
                         wbufs=("scrow",), rbufs=("ps1",))
                    replicate_row(scrow[:1, branch * TQ3:
                                        (branch + 1) * TQ3],
                                  "scrow", TQ3, rep, repb)

                with contextlib.ExitStack() as _st:
                    wgT_h = _st.enter_context(SB("wgT", [128, 2 * H], BF16))
                    s1b_h = _st.enter_context(SB("s1b", [128, NIC * TQ3], F32))
                    gtmp_h = _st.enter_context(SB("gtmp", [128, 2 * TQ3], F32))
                    utmp_h = _st.enter_context(SB("utmp", [128, TQ3], F32))
                    ibg_h = _st.enter_context(SB("ibg", [128, TQ3], BF16))
                    vcb_h = _st.enter_context(SB("vcb", [128, TQ3], F32))
                    dfb_h = _st.enter_context(SB("dfb", [128, TQ3], F32))
                    eib_h = _st.enter_context(SB("eib", [128, TQ3], I32))
                    wgT2 = wgT_h[:]
                    s1b = s1b_h[:]
                    gtmp2 = gtmp_h[:]
                    utmp, ibg = utmp_h[:], ibg_h[:]
                    vcb, dfb, eib = vcb_h[:], dfb_h[:], eib_h[:]
                    for ic in range(NIC):
                        par = ic % 2
                        wgT = wgT2[:, par * H:(par + 1) * H]
                        wgb = f"wgT{par}"
                        bk = 0 if par == 0 else 2
                        dma(gpq if par == 0 else sq, wgT,
                            wgT_s[ic * 128:(ic + 1) * 128, :], wbuf=wgb)
                        for hc in range(NHC):
                            comp(pe, lambda h, hc=hc, wgT=wgT, bk=bk:
                                 h.matmul(
                                    bank(bk, TQ3),
                                    wgT[:, hc * 128:(hc + 1) * 128],
                                    qgT[:, hc * TQ3:(hc + 1) * TQ3],
                                    start=(hc == 0), stop=(hc == NHC - 1)),
                                 wbufs=(f"ps{bk}",),
                                 rbufs=(wgb, "qgT"))
                        gtp = gtmp2[:, par * TQ3:(par + 1) * TQ3]
                        gtn = f"gtmp{par}"
                        comp(dve, lambda h, ic=ic, bk=bk, gtp=gtp:
                             h.scalar_tensor_tensor(
                                gtp, bank(bk, TQ3), swg_c[:, ic:ic + 1],
                                sgrep, OP.mult, OP.mult),
                             wbufs=(gtn,),
                             rbufs=(f"ps{bk}", "swg", "sgrep"))
                        comp(act, lambda h, ic=ic, gtp=gtp: h.activation(
                                s1b[:, ic * TQ3:(ic + 1) * TQ3], gtp,
                                AF.Silu),
                             wbufs=("s1b",), rbufs=(gtn,))
                    for ic in range(NIC):
                        par = ic % 2
                        wgT = wgT2[:, par * H:(par + 1) * H]
                        wgb = f"wgT{par}"
                        bk = 0 if par == 0 else 2
                        gtp = gtmp2[:, par * TQ3:(par + 1) * TQ3]
                        gtn = f"gtmp{par}"
                        dma(gpq if par == 0 else sq, wgT,
                            wuT_s[ic * 128:(ic + 1) * 128, :], wbuf=wgb)
                        for hc in range(NHC):
                            comp(pe, lambda h, hc=hc, wgT=wgT, bk=bk:
                                 h.matmul(
                                    bank(bk, TQ3),
                                    wgT[:, hc * 128:(hc + 1) * 128],
                                    quT[:, hc * TQ3:(hc + 1) * TQ3],
                                    start=(hc == 0), stop=(hc == NHC - 1)),
                                 wbufs=(f"ps{bk}",),
                                 rbufs=(wgb, "quT"))
                        comp(dve, lambda h, ic=ic, bk=bk:
                             h.scalar_tensor_tensor(
                                utmp, bank(bk, TQ3), swu_c[:, ic:ic + 1],
                                surep, OP.mult, OP.mult),
                             wbufs=("utmp",),
                             rbufs=(f"ps{bk}", "swu", "surep"))
                        comp(dve, lambda h, ic=ic: h.tensor_tensor(
                                gtp, utmp,
                                s1b[:, ic * TQ3:(ic + 1) * TQ3],
                                OP.mult),
                             wbufs=(gtn,), rbufs=("utmp", "s1b"))
                        comp(dve, lambda h: h.tensor_scalar(
                                vcb, gtp, -10.0, 10.0, OP.max, OP.min),
                             wbufs=("vcb",), rbufs=(gtn,))
                        emit_bf16_dround(vcb, "vcb", gtp, gtn,
                                         utmp, "utmp", dfb, "dfb",
                                         eib, "eib", ibg, "ibg")
                        transpose_batch(
                            ibg, "ibg", NQS3,
                            lambda g0, cnt, ic=ic:
                                intm3[:, g0:g0 + cnt, ic * 128:
                                      (ic + 1) * 128],
                            "intm", ident16, "ident16", bf=True)
                barrier()
                with contextlib.ExitStack() as _st:
                    xsi_h = _st.enter_context(SB("xsi", [128, I_FF], F32))
                    tpi_h = _st.enter_context(SB("tpi", [128, I_FF], F32))
                    qri_h = _st.enter_context(SB("qri", [128, I_FF], F32))
                    qtmi_h = _st.enter_context(SB("qtmi", [128, I_FF], BF16))
                    invir_h = _st.enter_context(SB("invir", [128, I_FF], F32))
                    xsi, tpi = xsi_h[:], tpi_h[:]
                    qri, qtmi = qri_h[:], qtmi_h[:]
                    invir = invir_h[:]
                    replicate_row(invi_r, "invi_r", I_FF, invir, "invir")
                    for tt in range(NQS3):
                        comp(dve, lambda h, tt=tt: h.tensor_tensor(
                                xsi, intm[:, tt * I_FF:(tt + 1) * I_FF],
                                invir, OP.mult),
                             wbufs=("xsi",), rbufs=("intm", "invir"))
                        comp(dve, lambda h: h.reduce_max(
                                stat[:, 0:1], xsi, AX,
                                apply_absolute_value=True),
                             wbufs=("st0",), rbufs=("xsi",))
                        comp(dve, lambda h: h.tensor_scalar(
                                stat[:, 1:2], stat[:, 0:1], inv127c,
                                1e-8, OP.mult, OP.max),
                             wbufs=("st1",), rbufs=("st0", "inv127c"))
                        comp(dve, lambda h: h.reciprocal(
                                stat[:, 2:3], stat[:, 1:2]),
                             wbufs=("st2",), rbufs=("st1",))
                        comp(dve, lambda h, tt=tt: h.tensor_copy(
                                sccol[:, 2 * NQS3 + tt:2 * NQS3 + tt + 1],
                                stat[:, 1:2]),
                             wbufs=("sccol",), rbufs=("st1",))
                        comp(dve, lambda h: h.tensor_scalar(
                                tpi, xsi, stat[:, 2:3], magc,
                                OP.mult, OP.add),
                             wbufs=("tpi",), rbufs=("xsi", "st2", "magc"))
                        comp(dve, lambda h: h.tensor_scalar(
                                qri, tpi, magc, None, OP.subtract),
                             wbufs=("qri",), rbufs=("tpi", "magc"))
                        comp(dve, lambda h: h.tensor_scalar(
                                qtmi, qri, -127.0, 127.0,
                                OP.max, OP.min),
                             wbufs=("qtmi",), rbufs=("qri",))
                        transpose_batch(
                            qtmi, "qtmi", NIC,
                            lambda g0, cnt, tt=tt:
                                qiT3[:, g0:g0 + cnt,
                                     tt * 128:(tt + 1) * 128],
                            "qiT", ident16, "ident16", bf=True)
                    for tt in range(NQS3):
                        comp(pe, lambda h, tt=tt: h.transpose(
                                bank(1)[0:1, tt * 128:(tt + 1) * 128],
                                sccol[:, 2 * NQS3 + tt:2 * NQS3 + tt + 1],
                                ident32),
                             wbufs=("ps1",),
                             rbufs=("sccol", "ident32"))
                    comp(dve, lambda h: h.tensor_copy(
                            scrow[:1, 2 * TQ3:3 * TQ3], bank(1, TQ3)[0:1, :]),
                         wbufs=("scrow",), rbufs=("ps1",))
                    replicate_row(scrow[:1, 2 * TQ3:3 * TQ3], "scrow",
                                  TQ3, sirep, "sirep")

                barrier()
                with contextlib.ExitStack() as _st:
                    wdT_h = _st.enter_context(SB("wdT", [128, 2 * I_FF], BF16))
                    dtmp_h = _st.enter_context(SB("dtmp", [128, 2 * TQ3], F32))
                    attnb_h = _st.enter_context(SB("attnb", [128, 2 * TQ3], F32))
                    xcol_h = _st.enter_context(SB("xcol", [128, 256], F32))
                    youtb_h = _st.enter_context(SB("youtb", [128, 256], F32))
                    wdT2 = wdT_h[:]
                    dtmp2, attnb2 = dtmp_h[:], attnb_h[:]
                    xcol2, youtb2 = xcol_h[:], youtb_h[:]
                    for hc in range(NHC):
                        par = hc % 2
                        wdT = wdT2[:, par * I_FF:(par + 1) * I_FF]
                        wdb = f"wdT{par}"
                        dtmp = dtmp2[:, par * TQ3:(par + 1) * TQ3]
                        attnb = attnb2[:, par * TQ3:(par + 1) * TQ3]
                        dtb, anb = f"dtmp{par}", f"attnb{par}"
                        bk = 0 if par == 0 else 2
                        dma(gpq if par == 0 else sq, wdT,
                            wdT_s[hc * 128:(hc + 1) * 128, :], wbuf=wdb)
                        for ic in range(NIC):
                            comp(pe, lambda h, ic=ic, wdT=wdT, bk=bk:
                                 h.matmul(
                                    bank(bk, TQ3),
                                    wdT[:, ic * 128:(ic + 1) * 128],
                                    qiT[:, ic * TQ3:(ic + 1) * TQ3],
                                    start=(ic == 0), stop=(ic == NIC - 1)),
                                 wbufs=(f"ps{bk}",),
                                 rbufs=(wdb, "qiT"))
                        comp(dve, lambda h, hc=hc, bk=bk, dtmp=dtmp:
                             h.scalar_tensor_tensor(
                                dtmp, bank(bk, TQ3), swd_c[:, hc:hc + 1],
                                sirep, OP.mult, OP.mult),
                             wbufs=(dtb,),
                             rbufs=(f"ps{bk}", "swd", "sirep"))
                        dma(sq, attnb,
                            attn_s[hc * 128:(hc + 1) * 128, t0:t0 + TQ3],
                            wbuf=anb)
                        comp(dve, lambda h, dtmp=dtmp, attnb=attnb:
                             h.tensor_tensor(dtmp, dtmp, attnb, OP.add),
                             wbufs=(dtb,), rbufs=(dtb, anb))
                        for j in range(NQS3):
                            jp = j % 2
                            xcol = xcol2[:, jp * 128:(jp + 1) * 128]
                            youtb = youtb2[:, jp * 128:(jp + 1) * 128]
                            xcb, ytb = f"xcol{jp}", f"youtb{jp}"
                            tbk = 1 if jp == 0 else 3
                            comp(pe, lambda h, j=j, dtmp=dtmp, tbk=tbk:
                                 h.transpose(
                                    bank(tbk)[:, 0:128],
                                    dtmp[:, j * 128:(j + 1) * 128],
                                    ident32),
                                 wbufs=(f"ps{tbk}",),
                                 rbufs=(dtb, "ident32"))
                            dma(sq, xcol,
                                x_d[t0 + j * 128:t0 + (j + 1) * 128,
                                    hc * 128:(hc + 1) * 128],
                                wbuf=xcb)
                            comp(dve, lambda h, youtb=youtb, xcol=xcol,
                                 tbk=tbk:
                                 h.scalar_tensor_tensor(
                                    youtb, bank(tbk, 128), 0.0, xcol,
                                    OP.bypass, OP.add),
                                 wbufs=(ytb,), rbufs=(f"ps{tbk}", xcb))
                            dma(gpq, y_d[t0 + j * 128:t0 + (j + 1) * 128,
                                        hc * 128:(hc + 1) * 128],
                                youtb, rbufs=(ytb,))
                barrier()


            for w in range(NB3):
                phase3_block(w)
        prog["sq"].append(("w", sem_gpd, gpq.n))
        prog["sq"].append(("w", sem_sq, sq.n))

        def play(h, key):
            for item in prog[key]:
                if item[0] == "w":
                    h.wait_ge(item[1], item[2])
                else:
                    item[1](h).then_inc(item[2], item[3])

        @block.tensor
        def _(h):
            play(h, "pe")

        @block.vector
        def _(h):
            play(h, "dve")

        @block.scalar
        def _(h):
            play(h, "act")

        @block.gpsimd
        def _(h):
            play(h, "gp")

        @block.sync
        def _(h):
            play(h, "sq")

    return nc


def _consts(S):
    id32 = np.eye(128, dtype=np.float32)
    id16 = np.eye(128).astype(ml_dtypes.bfloat16)
    ones1 = np.ones((1, 128), dtype=np.float32)
    c = np.arange(S, dtype=np.float64)
    p = np.arange(128, dtype=np.float64)
    jrow2 = (c[None, :] - (S - 128) - p[:, None]).astype(np.float32)
    mask = np.where(np.arange(128)[None, :] > np.arange(128)[:, None],
                    np.float32(NEG_BIG), np.float32(0.0)).astype(np.float32)
    return id32, id16, ones1, jrow2, mask


def kernel(L0_attn_hidden_states, L0_attn_Wq, L0_attn_Wkv, L0_attn_Wo,
           L0_ffn_w_gate, L0_ffn_s_wgate, L0_ffn_w_up, L0_ffn_s_wup,
           L0_ffn_w_down, L0_ffn_s_wdown, L0_ffn_inv_gate, L0_ffn_inv_up,
           L0_ffn_inv_inter):
    x = np.ascontiguousarray(L0_attn_hidden_states)
    Bb, S, _ = x.shape
    nc = build_nc(S=S, TQ=min(512, S))
    id32, id16, ones1, jrow2, mask = _consts(S)
    base = {
        "Wq": np.ascontiguousarray(L0_attn_Wq),
        "Wkv": np.ascontiguousarray(L0_attn_Wkv),
        "Wo": np.ascontiguousarray(L0_attn_Wo),
        "w_gate": np.ascontiguousarray(L0_ffn_w_gate),
        "w_up": np.ascontiguousarray(L0_ffn_w_up),
        "w_down": np.ascontiguousarray(L0_ffn_w_down),
        "s_wgate": np.ascontiguousarray(L0_ffn_s_wgate),
        "s_wup": np.ascontiguousarray(L0_ffn_s_wup),
        "s_wdown": np.ascontiguousarray(L0_ffn_s_wdown),
        "inv_gate": np.ascontiguousarray(
            np.asarray(L0_ffn_inv_gate).reshape(1, -1)),
        "inv_up": np.ascontiguousarray(
            np.asarray(L0_ffn_inv_up).reshape(1, -1)),
        "inv_inter": np.ascontiguousarray(
            np.asarray(L0_ffn_inv_inter).reshape(1, -1)),
        "ident32": id32, "ident16": id16, "ones1": ones1,
        "jrow2": jrow2, "maskt": mask,
    }
    in_maps = [dict(base, x=np.ascontiguousarray(x[b])) for b in range(Bb)]
    res = run_bass_kernel_spmd(nc, in_maps, core_ids=list(range(Bb)))
    return np.stack([np.asarray(res.results[b]["y"]) for b in range(Bb)],
                    axis=0)


# version 08
# speedup vs baseline: 1.0153x; 1.0153x over previous
import contextlib
import math
import numpy as np
import ml_dtypes
import concourse.bass as bass
import concourse.mybir as mybir
from concourse.bass_utils import run_bass_kernel_spmd

F32 = mybir.dt.float32
BF16 = mybir.dt.bfloat16
I16 = mybir.dt.int16
I32 = mybir.dt.int32
AX = mybir.AxisListType.X
OP = mybir.AluOpType
AF = mybir.ActivationFunctionType

H = 4096
NH = 32
HD = 128
I_FF = 2048
EPS = 1e-5
MAGIC = 12582912.0
INV127 = 1.0 / 127.0
SCALE = float(HD) ** -0.5
NEG_BIG = -1.0e38
NHC = H // 128
NIC = I_FF // 128


def _slopes32():
    start = 2.0 ** (-(2.0 ** (-(math.log2(NH) - 3))))
    return [float(np.float32(start ** (i + 1))) for i in range(NH)]


class REng:

    def __init__(self, prog, key, sem, default_inc, skip_self=True):
        self.prog = prog
        self.key = key
        self.sem = sem
        self.n = 0
        self.default_inc = default_inc
        self.skip_self = skip_self
        self.waited = {}

    def wait(self, dep):
        if dep is None:
            return
        eng, val = dep
        if val <= 0:
            return
        if eng is self and self.skip_self:
            return
        k = id(eng.sem)
        if self.waited.get(k, -1) >= val:
            return
        self.prog[self.key].append(("w", eng.sem, val))
        self.waited[k] = val

    def run(self, fn, deps=(), inc=None):
        for d in deps:
            self.wait(d)
        if inc is None:
            inc = self.default_inc
        self.prog[self.key].append(("e", fn, self.sem, inc))
        self.n += inc
        return (self, self.n)

    def run_raw(self, fn, dsem):
        self.prog[self.key].append(("e", fn, dsem.sem, 16))
        dsem.n += 16
        return None


class DmaSem:

    def __init__(self, sem):
        self.sem = sem
        self.n = 0


class Buf:
    def __init__(self):
        self.wdep = None
        self.rdeps = []

    def write_deps(self):
        d = list(self.rdeps)
        if self.wdep is not None:
            d.append(self.wdep)
        return d

    def wrote(self, dep):
        self.wdep = dep
        self.rdeps = []

    def read_deps(self):
        return [self.wdep] if self.wdep is not None else []

    def read(self, dep):
        self.rdeps.append(dep)


def build_nc(S=2048, TQ=512, debug=False, tap=None):
    assert S % 128 == 0 and TQ % 128 == 0 and S % TQ == 0 and TQ <= 512
    NT = S // 128
    NB = S // TQ
    NQS = TQ // 128
    TQ3 = min(256, TQ)
    NB3 = S // TQ3
    NQS3 = TQ3 // 128

    nc = bass.Bass("TRN2", target_bir_lowering=False, debug=debug,
                   detect_race_conditions=False)

    x_d = nc.dram_tensor("x", [S, H], F32, kind="ExternalInput").ap()
    wq_d = nc.dram_tensor("Wq", [H, H], F32, kind="ExternalInput").ap()
    wkv_d = nc.dram_tensor("Wkv", [H, 2 * HD], F32, kind="ExternalInput").ap()
    wo_d = nc.dram_tensor("Wo", [H, H], F32, kind="ExternalInput").ap()
    wg_d = nc.dram_tensor("w_gate", [I_FF, H], F32, kind="ExternalInput").ap()
    wu_d = nc.dram_tensor("w_up", [I_FF, H], F32, kind="ExternalInput").ap()
    wd_d = nc.dram_tensor("w_down", [H, I_FF], F32, kind="ExternalInput").ap()
    swg_d = nc.dram_tensor("s_wgate", [I_FF], F32, kind="ExternalInput").ap()
    swu_d = nc.dram_tensor("s_wup", [I_FF], F32, kind="ExternalInput").ap()
    swd_d = nc.dram_tensor("s_wdown", [H], F32, kind="ExternalInput").ap()
    invg_d = nc.dram_tensor("inv_gate", [1, H], F32, kind="ExternalInput").ap()
    invu_d = nc.dram_tensor("inv_up", [1, H], F32, kind="ExternalInput").ap()
    invi_d = nc.dram_tensor("inv_inter", [1, I_FF], F32,
                            kind="ExternalInput").ap()
    id32_d = nc.dram_tensor("ident32", [128, 128], F32,
                            kind="ExternalInput").ap()
    id16_d = nc.dram_tensor("ident16", [128, 128], BF16,
                            kind="ExternalInput").ap()
    ones_d = nc.dram_tensor("ones1", [1, 128], F32, kind="ExternalInput").ap()
    jrow_d = nc.dram_tensor("jrow2", [128, S], F32, kind="ExternalInput").ap()
    mask_d = nc.dram_tensor("maskt", [128, 128], F32,
                            kind="ExternalInput").ap()

    y_d = nc.dram_tensor("y", [S, H], F32, kind="ExternalOutput").ap()
    dbg_d = (nc.dram_tensor("dbg", [S, H], F32, kind="ExternalOutput").ap()
             if tap else None)

    xnhi_s = nc.dram_tensor("xnhi_s", [S, H], BF16, kind="Internal").ap()
    xnlo_s = nc.dram_tensor("xnlo_s", [S, H], BF16, kind="Internal").ap()
    xf_s = nc.dram_tensor("xf_s", [S, H], BF16, kind="Internal").ap()
    qhi_s = nc.dram_tensor("qhi_s", [H, TQ], BF16, kind="Internal").ap()
    qlo_s = nc.dram_tensor("qlo_s", [H, TQ], BF16, kind="Internal").ap()
    aohi_s = nc.dram_tensor("aohi_s", [H, TQ], BF16, kind="Internal").ap()
    aolo_s = nc.dram_tensor("aolo_s", [H, TQ], BF16, kind="Internal").ap()
    attn_s = nc.dram_tensor("attn_s", [H, S], F32, kind="Internal").ap()
    xnhfm_s = nc.dram_tensor("xnhfm_s", [NHC * 128, S], BF16,
                             kind="Internal").ap()
    xnlfm_s = nc.dram_tensor("xnlfm_s", [NHC * 128, S], BF16,
                             kind="Internal").ap()
    wgT_s = nc.dram_tensor("wgT_s", [NIC * 128, H], BF16, kind="Internal").ap()
    wuT_s = nc.dram_tensor("wuT_s", [NIC * 128, H], BF16, kind="Internal").ap()
    wdT_s = nc.dram_tensor("wdT_s", [NHC * 128, I_FF], BF16,
                           kind="Internal").ap()

    slopes = _slopes32()

    _semstack = contextlib.ExitStack()
    with (
        _semstack,
        nc.semaphore("m_pe") as sem_pe,
        nc.semaphore("m_dve") as sem_dve,
        nc.semaphore("m_act") as sem_act,
        nc.semaphore("m_gp") as sem_gp,
        nc.semaphore("m_gpd") as sem_gpd,
        nc.semaphore("m_sq") as sem_sq,
        nc.Block() as block,
    ):
        prog = {"pe": [], "dve": [], "act": [], "gp": [], "sq": []}
        pe = REng(prog, "pe", sem_pe, 1)
        dve = REng(prog, "dve", sem_dve, 1, skip_self=False)
        act = REng(prog, "act", sem_act, 1, skip_self=False)
        gp = REng(prog, "gp", sem_gp, 1,
                  skip_self=False)
        gpq = REng(prog, "gp", sem_gpd, 16,
                   skip_self=False)
        sq = REng(prog, "sq", sem_sq, 16,
                  skip_self=False)

        _names = {}

        def SB(name, shape, dt):
            i = _names.get(name, 0)
            _names[name] = i + 1
            return nc.sbuf_tensor(f"sb_{name}_{i}", shape, dt)

        bufs = {}

        def B(name):
            if name not in bufs:
                bufs[name] = Buf()
            return bufs[name]

        dmasems = {}

        def DS(key):
            if key not in dmasems:
                dmasems[key] = DmaSem(_semstack.enter_context(
                    nc.semaphore(f"ds_{key}")))
            return dmasems[key]

        def dma(q, dst_ap, src_ap, wbuf=None, rbufs=(), extra=(),
                slow=False):
            deps = list(extra)
            if wbuf is not None:
                deps += B(wbuf).write_deps()
            for rb in rbufs:
                deps += B(rb).read_deps()
            key = wbuf if wbuf is not None else (rbufs[0] if rbufs
                                                 else "misc")
            ds = DS(key)
            for dd in deps:
                q.wait(dd)
            inst = q.run_raw(lambda h, dst=dst_ap, src=src_ap, slow=slow:
                             h.dma_start(out=dst, in_=src,
                                         allow_slow_non_contiguous=slow),
                             ds)
            d = (ds, ds.n)
            if wbuf is not None:
                B(wbuf).wrote(d)
            for rb in rbufs:
                B(rb).read(d)
            return d

        def comp(eng, fn, wbufs=(), rbufs=(), extra=()):
            deps = list(extra)
            for wb in wbufs:
                deps += B(wb).write_deps()
            for rb in rbufs:
                deps += B(rb).read_deps()
            d = eng.run(fn, deps, inc=1)
            for wb in wbufs:
                B(wb).wrote(d)
            for rb in rbufs:
                B(rb).read(d)
            return d

        def barrier():
            engs = (pe, dve, act, gp, gpq, sq)
            marks = [(e, e.n) for e in engs]
            marks += [(ds, ds.n) for ds in dmasems.values() if ds.n > 0]
            for e in engs:
                for m in marks:
                    e.wait(m)

        ps = nc.alloc_psum_tensor("ps", [128, 3072], F32)
        ps_ap = ps.ap()
        psb = nc.alloc_psum_tensor("psb", [128, 2048], BF16)
        psb_ap = psb.ap()

        def bank(i, w=512):
            return ps_ap[:, i * 512:i * 512 + w]

        def bank_bf(i, w=1024):
            return psb_ap[:, i * 1024:i * 1024 + w]

        def sb(name, shape, dt):
            return nc.alloc_sbuf_tensor("sbr_" + name, shape, dt).ap()

        ident32 = sb("ident32", [128, 128], F32)
        ident16 = sb("ident16", [128, 128], BF16)
        ones1 = sb("ones1", [1, 128], F32)
        jrow2 = sb("jrow2", [128, S], F32)
        maskt = sb("maskt", [128, 128], F32)
        khi = sb("khi", [128, S], BF16)
        klo = sb("klo", [128, S], BF16)
        vhi = sb("vhi", [128, S], BF16)
        vlo = sb("vlo", [128, S], BF16)
        swg_c = sb("swg_c", [128, NIC], F32)
        swu_c = sb("swu_c", [128, NIC], F32)
        swd_c = sb("swd_c", [128, NHC], F32)
        invg_r = sb("invg_r", [1, H], F32)
        invu_r = sb("invu_r", [1, H], F32)
        invi_r = sb("invi_r", [1, I_FF], F32)

        c127 = sb("c127", [128, 1], F32)
        inv127c = sb("inv127c", [128, 1], F32)
        c128 = sb("c128", [128, 1], F32)
        scalec = sb("scalec", [128, 1], F32)
        comp(dve, lambda h: h.memset(c127, 127.0), wbufs=("c127",))
        comp(dve, lambda h: h.reciprocal(inv127c, c127),
             wbufs=("inv127c",), rbufs=("c127",))
        comp(dve, lambda h: h.memset(c128, 128.0), wbufs=("c128",))
        comp(act, lambda h: h.activation(c128, c128, AF.Sqrt),
             wbufs=("c128",), rbufs=("c128",))
        comp(dve, lambda h: h.reciprocal(scalec, c128),
             wbufs=("scalec",), rbufs=("c128",))
        m3c = sb("m3c", [128, 1], F32)
        magc = sb("magc", [128, 1], F32)
        comp(dve, lambda h: h.memset(m3c, 1.0), wbufs=("m3c",))
        comp(dve, lambda h: h.tensor_single_scalar(
                m3c, m3c, float(2.0 ** -30), OP.add),
             wbufs=("m3c",), rbufs=("m3c",))
        comp(dve, lambda h: h.tensor_single_scalar(
                m3c, m3c, -1.0, OP.add),
             wbufs=("m3c",), rbufs=("m3c",))
        comp(dve, lambda h: h.tensor_single_scalar(
                m3c, m3c, float(1.5 * 2.0 ** 82), OP.mult),
             wbufs=("m3c",), rbufs=("m3c",))
        comp(dve, lambda h: h.tensor_single_scalar(
                magc, m3c, 12582912.0, OP.add),
             wbufs=("magc",), rbufs=("m3c",))
        dma(sq, ident32, id32_d, wbuf="ident32")
        dma(sq, ident16, id16_d, wbuf="ident16")
        dma(sq, ones1, ones_d, wbuf="ones1")
        dma(sq, jrow2, jrow_d, wbuf="jrow2")
        dma(sq, maskt, mask_d, wbuf="maskt")
        dma(sq, swg_c, swg_d.rearrange("(c p) -> p c", p=128), wbuf="swg", slow=True)
        dma(sq, swu_c, swu_d.rearrange("(c p) -> p c", p=128), wbuf="swu", slow=True)
        dma(sq, swd_c, swd_d.rearrange("(c p) -> p c", p=128), wbuf="swd", slow=True)
        dma(sq, invg_r, invg_d, wbuf="invg_r")
        dma(sq, invu_r, invu_d, wbuf="invu_r")
        dma(sq, invi_r, invi_d, wbuf="invi_r")

        def replicate_row(row_ap, row_buf, n, dst_ap, dst_buf, bank_i=2):
            for c0 in range(0, n, 512):
                w = min(512, n - c0)
                comp(pe, lambda h, row_ap=row_ap, c0=c0, w=w, bank_i=bank_i:
                     h.matmul(bank(bank_i, w), ones1[:1, :],
                              row_ap[:1, c0:c0 + w], start=True, stop=True),
                     wbufs=(f"ps{bank_i}",), rbufs=("ones1", row_buf))
                comp(dve, lambda h, dst_ap=dst_ap, c0=c0, w=w, bank_i=bank_i:
                     h.tensor_copy(dst_ap[:, c0:c0 + w], bank(bank_i, w)),
                     wbufs=(dst_buf,), rbufs=(f"ps{bank_i}",))

        def transpose_batch(src_ap, src_buf, n_tiles, dst_fn, dst_buf,
                            ident, ident_buf, bf, banks=(4, 5)):
            per = 8 if bf else 4
            g0 = 0
            while g0 < n_tiles:
                cnt = min(per, n_tiles - g0)
                if bf:
                    bi = (g0 // per) % 2
                    wb = f"psb{bi}"
                else:
                    bi = banks[(g0 // per) % len(banks)]
                    wb = f"ps{bi}"
                for j in range(cnt):
                    tl = g0 + j
                    out_ap = (bank_bf(bi)[:, j * 128:(j + 1) * 128] if bf
                              else bank(bi)[:, j * 128:(j + 1) * 128])
                    comp(pe, lambda h, out_ap=out_ap, tl=tl:
                         h.transpose(out_ap,
                                     src_ap[:, tl * 128:(tl + 1) * 128],
                                     ident),
                         wbufs=(wb,),
                         rbufs=(src_buf, ident_buf) if j == 0
                               else (src_buf,))
                dst_ap = dst_fn(g0, cnt)
                src2 = (bank_bf(bi)[:, :cnt * 128] if bf
                        else bank(bi)[:, :cnt * 128])
                comp(dve, lambda h, dst_ap=dst_ap, src2=src2:
                     h.tensor_copy(dst_ap, src2),
                     wbufs=(dst_buf,), rbufs=(wb,))
                g0 += cnt

        def emit_bf16_dround(V, Vbuf, A, Abuf, Bv, Bbuf, D, Dbuf, E, Ebuf,
                             O, Obuf):
            EXPM = 0x7F800000
            comp(dve, lambda h: h.tensor_single_scalar(
                    E, V.bitcast(I32), EXPM, OP.bitwise_and),
                 wbufs=(Ebuf,), rbufs=(Vbuf,))
            comp(dve, lambda h: h.tensor_single_scalar(
                    E, E, float(23 * 2 ** 23), OP.subtract),
                 wbufs=(Ebuf,), rbufs=(Ebuf,))
            comp(dve, lambda h: h.tensor_scalar(
                    Bv, E.bitcast(F32), m3c, None, OP.mult),
                 wbufs=(Bbuf,), rbufs=(Ebuf, "m3c"))
            comp(dve, lambda h: h.tensor_tensor(A, V, Bv, OP.add),
                 wbufs=(Abuf,), rbufs=(Vbuf, Bbuf))
            comp(dve, lambda h: h.tensor_tensor(A, A, Bv, OP.subtract),
                 wbufs=(Abuf,), rbufs=(Abuf, Bbuf))
            comp(dve, lambda h: h.tensor_single_scalar(
                    E, A.bitcast(I32), EXPM, OP.bitwise_and),
                 wbufs=(Ebuf,), rbufs=(Abuf,))
            comp(dve, lambda h: h.tensor_single_scalar(
                    E, E, float(7 * 2 ** 23), OP.subtract),
                 wbufs=(Ebuf,), rbufs=(Ebuf,))
            comp(dve, lambda h: h.tensor_scalar(
                    Bv, E.bitcast(F32), m3c, None, OP.mult),
                 wbufs=(Bbuf,), rbufs=(Ebuf, "m3c"))
            comp(dve, lambda h: h.tensor_tensor(A, A, Bv, OP.add),
                 wbufs=(Abuf,), rbufs=(Abuf, Bbuf))
            comp(dve, lambda h: h.tensor_tensor(O, A, Bv, OP.subtract),
                 wbufs=(Obuf,), rbufs=(Abuf, Bbuf))

        with contextlib.ExitStack() as _st:
            wkv32_h = _st.enter_context(SB("wkv32", [128, NHC * 32], F32))
            wkvh_h = _st.enter_context(SB("wkvh", [128, NHC * 256], BF16))
            wkvl_h = _st.enter_context(SB("wkvl", [128, NHC * 256], BF16))
            p1x_h = _st.enter_context(SB("p1x", [128, H], F32))
            p1xn_h = _st.enter_context(SB("p1xn", [128, H], F32))
            p1hi_h = _st.enter_context(SB("p1hi", [128, H], BF16))
            p1lo_h = _st.enter_context(SB("p1lo", [128, H], BF16))
            p1tr_h = _st.enter_context(SB("p1tr", [128, H], BF16))
            wpb_h = _st.enter_context(SB("wpb", [128, H], BF16))
            wpt_h = _st.enter_context(SB("wpt", [128, H], BF16))
            p1fmh_h = _st.enter_context(SB("p1fmh", [128, NHC * 128], BF16))
            p1fml_h = _st.enter_context(SB("p1fml", [128, NHC * 128], BF16))
            p1st_h = _st.enter_context(SB("p1st", [128, 8], F32))
            p1kv_h = _st.enter_context(SB("p1kv", [128, 128], F32))
            p1xc_h = _st.enter_context(SB("p1xc", [128, H // 4], F32))
            p1xw_h = _st.enter_context(SB("p1xw", [128, H // 4], F32))
            p1xe_h = _st.enter_context(SB("p1xe", [128, H // 4], I32))
            p1xd_h = _st.enter_context(SB("p1xd", [128, H // 4], F32))
            wkv32, wkvh, wkvl = wkv32_h[:], wkvh_h[:], wkvl_h[:]
            p1x, p1xn = p1x_h[:], p1xn_h[:]
            p1hi, p1lo, p1tr = p1hi_h[:], p1lo_h[:], p1tr_h[:]
            wpb, wpt = wpb_h[:], wpt_h[:]
            p1fmh, p1fml = p1fmh_h[:], p1fml_h[:]
            p1st, p1kv = p1st_h[:], p1kv_h[:]
            p1xc, p1xw, p1xe = p1xc_h[:], p1xw_h[:], p1xe_h[:]
            p1xd = p1xd_h[:]

            for grp in range(8):
                for j in range(4):
                    hc = grp * 4 + j
                    dma(sq, wkv32[:, j * 256:(j + 1) * 256],
                        wkv_d[hc * 128:(hc + 1) * 128, :], wbuf="wkv32")
                sl = slice(grp * 1024, (grp + 1) * 1024)
                comp(dve, lambda h, sl=sl: h.tensor_copy(
                        wkvh[:, sl], wkv32),
                     wbufs=("wkvh",), rbufs=("wkv32",))
                comp(gp, lambda h, sl=sl: h.tensor_tensor(
                        wkvl[:, sl], wkv32, wkvh[:, sl], OP.subtract),
                     wbufs=("wkvl",), rbufs=("wkv32", "wkvh"))

            for src_d, dst_s, nch, wid in ((wg_d, wgT_s, NIC, H),
                                           (wu_d, wuT_s, NIC, H),
                                           (wd_d, wdT_s, NHC, I_FF)):
                for c in range(nch):
                    dma(gpq, wpb[:, :wid], src_d[c * 128:(c + 1) * 128, :],
                        wbuf="wpb")
                    transpose_batch(
                        wpb[:, :wid], "wpb", wid // 128,
                        lambda g0, cnt: wpt[:, g0 * 128:(g0 + cnt) * 128],
                        "wpt", ident16, "ident16", bf=True)
                    dma(sq, dst_s[c * 128:(c + 1) * 128, :wid],
                        wpt[:, :wid], rbufs=("wpt",))
            for t in range(NT):
                xb = "p1x"
                xt = p1x
                dma(sq, xt, x_d[t * 128:(t + 1) * 128, :], wbuf=xb)
                comp(dve, lambda h, xt=xt: h.reduce_sum(p1st[:, 0:1], xt, AX),
                     wbufs=("p1s1",), rbufs=(xb,))
                for qq in range(4):
                    sl = slice(qq * (H // 4), (qq + 1) * (H // 4))
                    comp(act, lambda h, xt=xt, sl=sl: h.activation(
                            p1xc, xt[:, sl], AF.Square),
                         wbufs=("p1xc",), rbufs=(xb,))
                    comp(dve, lambda h, qq=qq: h.reduce_sum(
                            p1st[:, 1:2] if qq == 0 else p1st[:, 7:8],
                            p1xc, AX),
                         wbufs=("p1s2" if qq == 0 else "p1ve",),
                         rbufs=("p1xc",))
                    if qq > 0:
                        comp(dve, lambda h: h.tensor_tensor(
                                p1st[:, 1:2], p1st[:, 1:2], p1st[:, 7:8],
                                OP.add),
                             wbufs=("p1s2",), rbufs=("p1s2", "p1ve"))
                comp(dve, lambda h: h.tensor_scalar(
                        p1st[:, 2:3], p1st[:, 0:1], 1.0 / H, None, OP.mult),
                     wbufs=("p1mu",), rbufs=("p1s1",))
                comp(dve, lambda h: h.tensor_tensor(
                        p1st[:, 3:4], p1st[:, 2:3], p1st[:, 2:3], OP.mult),
                     wbufs=("p1mu2",), rbufs=("p1mu",))
                comp(dve, lambda h: h.scalar_tensor_tensor(
                        p1st[:, 4:5], p1st[:, 1:2], 1.0 / H, p1st[:, 3:4],
                        OP.mult, OP.subtract),
                     wbufs=("p1var",), rbufs=("p1s2", "p1mu2"))
                comp(dve, lambda h: h.tensor_scalar(
                        p1st[:, 7:8], p1st[:, 4:5], EPS, None, OP.add),
                     wbufs=("p1ve",), rbufs=("p1var",))
                comp(act, lambda h: h.activation(
                        p1st[:, 5:6], p1st[:, 7:8], AF.Sqrt),
                     wbufs=("p1sd",), rbufs=("p1ve",))
                comp(dve, lambda h: h.reciprocal(p1st[:, 6:7], p1st[:, 5:6]),
                     wbufs=("p1istd",), rbufs=("p1sd",))
                comp(dve, lambda h, xt=xt: h.tensor_scalar(
                        p1xn, xt, p1st[:, 2:3], p1st[:, 6:7],
                        OP.subtract, OP.mult),
                     wbufs=("p1xn",), rbufs=(xb, "p1mu", "p1istd"))
                if tap == "st":
                    dma(gpq, dbg_d[t * 128:(t + 1) * 128, 0:8], p1st,
                        rbufs=("p1s1", "p1s2", "p1mu", "p1mu2", "p1var",
                               "p1sd", "p1istd", "p1ve"))
                if tap == "xn":
                    dma(gpq, dbg_d[t * 128:(t + 1) * 128, :], p1xn,
                        rbufs=("p1xn",))
                if tap == "x":
                    dma(gpq, dbg_d[t * 128:(t + 1) * 128, :], xt,
                        rbufs=(xb,))
                comp(dve, lambda h: h.tensor_copy(p1hi, p1xn),
                     wbufs=("p1hi",), rbufs=("p1xn",))
                comp(gp, lambda h: h.tensor_tensor(
                        p1lo, p1xn, p1hi, OP.subtract),
                     wbufs=("p1lo",), rbufs=("p1xn", "p1hi"))
                dma(sq, xnhi_s[t * 128:(t + 1) * 128, :], p1hi,
                    rbufs=("p1hi",))
                dma(sq, xnlo_s[t * 128:(t + 1) * 128, :], p1lo,
                    rbufs=("p1lo",))
                for hhalf in range(4):
                    sl = slice(hhalf * (H // 4), (hhalf + 1) * (H // 4))
                    emit_bf16_dround(p1xn[:, sl], "p1xn", p1xc, "p1xc",
                                     p1xw, "p1xw", p1xd, "p1xd",
                                     p1xe, "p1xe", p1tr[:, sl], "p1tr")
                dma(sq, xf_s[t * 128:(t + 1) * 128, :], p1tr,
                    rbufs=("p1tr",))
                transpose_batch(p1hi, "p1hi", NHC,
                                lambda g0, cnt: p1fmh[:, g0 * 128:
                                                      (g0 + cnt) * 128],
                                "p1fmh", ident16, "ident16", bf=True)
                transpose_batch(p1lo, "p1lo", NHC,
                                lambda g0, cnt: p1fml[:, g0 * 128:
                                                      (g0 + cnt) * 128],
                                "p1fml", ident16, "ident16", bf=True)
                for fmt, dst_s, fb in ((p1fmh, xnhfm_s, "p1fmh"),
                                       (p1fml, xnlfm_s, "p1fml")):
                    dma(gpq,
                        dst_s.rearrange("(hc p) s -> p hc s", p=128)
                        [:, :, t * 128:(t + 1) * 128],
                        fmt.rearrange("p (hc c) -> p hc c", c=128),
                        rbufs=(fb,))
                for oc in range(2):
                    wb = "ps4"
                    seq = [(wkvh, p1fmh, "wkvh", "p1fmh"),
                           (wkvl, p1fmh, "wkvl", "p1fmh"),
                           (wkvh, p1fml, "wkvh", "p1fml")]
                    idx = 0
                    for hc in range(NHC):
                        for si, (wt, mt, wbn, mbn) in enumerate(seq):
                            st = (idx == 0)
                            sp = (idx == NHC * 3 - 1)
                            comp(pe, lambda h, wt=wt, mt=mt, hc=hc, oc=oc,
                                 st=st, sp=sp:
                                 h.matmul(bank(4, 128),
                                          wt[:, hc * 256 + oc * 128:
                                             hc * 256 + oc * 128 + 128],
                                          mt[:, hc * 128:(hc + 1) * 128],
                                          start=st, stop=sp),
                                 wbufs=(wb,),
                                 rbufs=(wbn, mbn))
                            idx += 1
                    if oc == 0:
                        comp(dve, lambda h, t=t: h.tensor_copy(
                                khi[:, t * 128:(t + 1) * 128], bank(4, 128)),
                             wbufs=("khi",), rbufs=(wb,))
                        comp(dve, lambda h, t=t: h.scalar_tensor_tensor(
                                klo[:, t * 128:(t + 1) * 128], bank(4, 128),
                                0.0, khi[:, t * 128:(t + 1) * 128],
                                OP.bypass, OP.subtract),
                             wbufs=("klo",), rbufs=(wb, "khi"))
                    else:
                        comp(dve, lambda h: h.tensor_copy(p1kv, bank(4, 128)),
                             wbufs=("p1kv",), rbufs=(wb,))
                        comp(pe, lambda h: h.transpose(
                                bank(5, 128), p1kv, ident32),
                             wbufs=("ps5",), rbufs=("p1kv", "ident32"))
                        comp(act, lambda h, t=t: h.activation(
                                vhi[:, t * 128:(t + 1) * 128], bank(5, 128),
                                AF.Copy),
                             wbufs=("vhi",), rbufs=("ps5",))
                        comp(dve, lambda h, t=t: h.scalar_tensor_tensor(
                                vlo[:, t * 128:(t + 1) * 128], bank(5, 128),
                                0.0, vhi[:, t * 128:(t + 1) * 128],
                                OP.bypass, OP.subtract),
                             wbufs=("vlo",), rbufs=("ps5", "vhi"))
            if tap == "kv":
                dma(gpq, dbg_d[0:128, 0:S].bitcast(BF16)[:, :S], khi,
                    rbufs=("khi",))
                dma(gpq, dbg_d[128:256, 0:S].bitcast(BF16)[:, :S], vhi,
                    rbufs=("vhi",))
        barrier()

        def phase2_block(b):
            t0 = b * TQ

            def qproj():
                with contextlib.ExitStack() as _s1:
                    p2fmh_h = _s1.enter_context(
                        SB("p2fmh", [128, NHC * TQ], BF16))
                    p2fml_h = _s1.enter_context(
                        SB("p2fml", [128, NHC * TQ], BF16))
                    p2fmh, p2fml = p2fmh_h[:], p2fml_h[:]
                    for src_s, fmt, fb in ((xnhfm_s, p2fmh, "p2fmh"),
                                           (xnlfm_s, p2fml, "p2fml")):
                        fmt3 = fmt.rearrange("p (c m) -> p c m", m=TQ)
                        dma(sq, fmt3,
                            src_s.rearrange("(hc p) s -> p hc s", p=128)
                            [:, :, t0:t0 + TQ],
                            wbuf=fb)
                    barrier()
                    with contextlib.ExitStack() as _s2:
                        wq32_h = _s2.enter_context(
                            SB("wq32", [128, 2 * 512], F32))
                        wqh_h = _s2.enter_context(
                            SB("wqh", [128, 2 * 512], BF16))
                        wql_h = _s2.enter_context(
                            SB("wql", [128, 2 * 512], BF16))
                        q32_h = _s2.enter_context(
                            SB("q32", [128, 2 * TQ], F32))
                        qhib_h = _s2.enter_context(
                            SB("qhib", [128, 2 * TQ], BF16))
                        qlob_h = _s2.enter_context(
                            SB("qlob", [128, 2 * TQ], BF16))
                        wq32_2, wqh2, wql2 = wq32_h[:], wqh_h[:], wql_h[:]
                        q32_2, qhib2, qlob2 = q32_h[:], qhib_h[:], qlob_h[:]
                        for ocg in range(NHC // 4):
                            for hc in range(NHC):
                                par = hc % 2
                                w32 = wq32_2[:, par * 512:(par + 1) * 512]
                                wqh = wqh2[:, par * 512:(par + 1) * 512]
                                wql = wql2[:, par * 512:(par + 1) * 512]
                                w32b, whb, wlb = (f"wq32{par}",
                                                  f"wqh{par}", f"wql{par}")
                                dma(gpq if par == 0 else sq, w32,
                                    wq_d[hc * 128:(hc + 1) * 128,
                                         ocg * 512:(ocg + 1) * 512],
                                    wbuf=w32b)
                                comp(dve, lambda h, wqh=wqh, w32=w32:
                                     h.tensor_copy(wqh, w32),
                                     wbufs=(whb,), rbufs=(w32b,))
                                comp(gp, lambda h, wql=wql, w32=w32,
                                     wqh=wqh:
                                     h.tensor_tensor(wql, w32, wqh,
                                                     OP.subtract),
                                     wbufs=(wlb,), rbufs=(w32b, whb))
                                for j in range(4):
                                    seq = [(wqh, p2fmh, whb, "p2fmh"),
                                           (wql, p2fmh, wlb, "p2fmh"),
                                           (wqh, p2fml, whb, "p2fml")]
                                    for si, (wt, mt, wbn, mbn) in \
                                            enumerate(seq):
                                        st = (hc == 0 and si == 0)
                                        sp = (hc == NHC - 1 and si == 2)
                                        comp(pe, lambda h, wt=wt, mt=mt,
                                             hc=hc, j=j, st=st, sp=sp:
                                             h.matmul(
                                                bank(j, TQ),
                                                wt[:, j * 128:
                                                   (j + 1) * 128],
                                                mt[:, hc * TQ:
                                                   (hc + 1) * TQ],
                                                start=st, stop=sp),
                                             wbufs=(f"ps{j}",),
                                             rbufs=(wbn, mbn))
                            for j in range(4):
                                oc = ocg * 4 + j
                                jp = j % 2
                                qhib = qhib2[:, jp * TQ:(jp + 1) * TQ]
                                qlob = qlob2[:, jp * TQ:(jp + 1) * TQ]
                                qhb, qlb = f"qhib{jp}", f"qlob{jp}"
                                comp(act, lambda h, qhib=qhib, j=j:
                                     h.activation(qhib, bank(j, TQ),
                                                  AF.Copy, scale=scalec),
                                     wbufs=(qhb,),
                                     rbufs=(f"ps{j}", "scalec"))
                                comp(dve, lambda h, qlob=qlob, qhib=qhib,
                                     j=j:
                                     h.scalar_tensor_tensor(
                                        qlob, bank(j, TQ), scalec, qhib,
                                        OP.mult, OP.subtract),
                                     wbufs=(qlb,),
                                     rbufs=(f"ps{j}", qhb, "scalec"))
                                dma(sq, qhi_s[oc * 128:(oc + 1) * 128, :],
                                    qhib, rbufs=(qhb,))
                                dma(sq, qlo_s[oc * 128:(oc + 1) * 128, :],
                                    qlob, rbufs=(qlb,))
            qproj()
            barrier()

            def attention():
                with contextlib.ExitStack() as _s1:
                    biasb_h = _s1.enter_context(SB("biasb", [128, S], I32))
                    s2b_h = _s1.enter_context(
                        SB("s2b", [128, NQS * S], F32))
                    e32_h = _s1.enter_context(
                        SB("e32", [128, NQS * S], F32))
                    mx_h = _s1.enter_context(SB("mx", [128, 2 * NQS], F32))
                    sume_h = _s1.enter_context(SB("sume", [128, NQS], F32))
                    ehiT_h = _s1.enter_context(SB("ehiT", [128, 1024], BF16))
                    eloT_h = _s1.enter_context(SB("eloT", [128, 1024], BF16))
                    repr_h = _s1.enter_context(SB("repr", [1, TQ], F32))
                    reps_h = _s1.enter_context(SB("reps", [128, TQ], F32))
                    ao32_h = _s1.enter_context(SB("ao32", [128, TQ], F32))
                    aohib_h = _s1.enter_context(
                        SB("aohib", [128, 2 * TQ], BF16))
                    aolob_h = _s1.enter_context(
                        SB("aolob", [128, 2 * TQ], BF16))
                    qhib_h = _s1.enter_context(
                        SB("qhibA", [128, 2 * TQ], BF16))
                    qlob_h = _s1.enter_context(
                        SB("qlobA", [128, 2 * TQ], BF16))
                    bias_view = biasb_h[:].bitcast(F32)
                    s2b2, e32_2 = s2b_h[:], e32_h[:]
                    mx, sume = mx_h[:], sume_h[:]
                    ehiT2, eloT2 = ehiT_h[:], eloT_h[:]
                    repr_, reps = repr_h[:], reps_h[:]
                    ao32 = ao32_h[:]
                    aohib2, aolob2 = aohib_h[:], aolob_h[:]
                    qhib2, qlob2 = qhib_h[:], qlob_h[:]
                    for hh in range(NH):
                        hp = hh % 2
                        qhib = qhib2[:, hp * TQ:(hp + 1) * TQ]
                        qlob = qlob2[:, hp * TQ:(hp + 1) * TQ]
                        qhb, qlb = f"qhibA{hp}", f"qlobA{hp}"
                        aohib = aohib2[:, hp * TQ:(hp + 1) * TQ]
                        aolob = aolob2[:, hp * TQ:(hp + 1) * TQ]
                        ahb, alb = f"aohib{hp}", f"aolob{hp}"
                        dma(sq, qhib, qhi_s[hh * 128:(hh + 1) * 128, :],
                            wbuf=qhb)
                        dma(sq, qlob, qlo_s[hh * 128:(hh + 1) * 128, :],
                            wbuf=qlb)
                        comp(dve, lambda h, hh=hh: h.tensor_scalar(
                                bias_view, jrow2, slopes[hh], None,
                                OP.mult),
                             wbufs=("biasb",), rbufs=("jrow2",))
                        comp(dve, lambda h: h.tensor_tensor(
                                bias_view[:, S - 128:S],
                                bias_view[:, S - 128:S], maskt, OP.add),
                             wbufs=("biasb",), rbufs=("biasb", "maskt"))
                        infos = []
                        for qs in range(NQS):
                            pq = qs
                            q0 = t0 + qs * 128
                            nk = q0 // 128 + 1
                            k_hi = nk * 128
                            infos.append((qs, pq, nk, k_hi,
                                          s2b2[:, pq * S:(pq + 1) * S],
                                          e32_2[:, pq * S:(pq + 1) * S],
                                          f"s2b{pq}", f"e32{pq}"))
                        for qs, pq, nk, k_hi, s2b, e32, s2n, e3n in infos:
                            nkt = (k_hi + 511) // 512
                            for kt in range(nkt):
                                ntk = min(512, k_hi - kt * 512)
                                bk = 2 * (qs % 2) + (kt % 2)
                                wb = f"ps{bk}"
                                seq = [(qhib, khi, qhb, "khi"),
                                       (qlob, khi, qlb, "khi"),
                                       (qhib, klo, qhb, "klo")]
                                for si, (qt, ktt, qbn, kbn) in \
                                        enumerate(seq):
                                    comp(pe, lambda h, qt=qt, ktt=ktt,
                                         qs=qs, kt=kt, ntk=ntk, si=si,
                                         bk=bk:
                                         h.matmul(
                                            bank(bk, ntk),
                                            qt[:, qs * 128:(qs + 1) * 128],
                                            ktt[:, kt * 512:
                                                kt * 512 + ntk],
                                            start=(si == 0),
                                            stop=(si == 2)),
                                         wbufs=(wb,),
                                         rbufs=(qbn, kbn))
                                comp(dve, lambda h, kt=kt, ntk=ntk,
                                     k_hi=k_hi, s2b=s2b, bk=bk:
                                     h.tensor_tensor(
                                        s2b[:, kt * 512:kt * 512 + ntk],
                                        bank(bk, ntk),
                                        bias_view[:, S - k_hi + kt * 512:
                                                  S - k_hi + kt * 512
                                                  + ntk],
                                        OP.add),
                                     wbufs=(s2n,), rbufs=(wb, "biasb"))
                        for qs, pq, nk, k_hi, s2b, e32, s2n, e3n in infos:
                            comp(dve, lambda h, k_hi=k_hi, s2b=s2b, pq=pq:
                                 h.reduce_max(
                                    mx[:, 2 * pq + 1:2 * pq + 2],
                                    s2b[:, :k_hi], AX, negate=True),
                                 wbufs=(f"mx1_{pq}",), rbufs=(s2n,))
                            comp(act, lambda h, k_hi=k_hi, s2b=s2b,
                                 e32=e32, pq=pq: h.activation(
                                    e32[:, :k_hi], s2b[:, :k_hi], AF.Exp,
                                    bias=mx[:, 2 * pq + 1:2 * pq + 2]),
                                 wbufs=(e3n,), rbufs=(s2n, f"mx1_{pq}"))
                        for qs, pq, nk, k_hi, s2b, e32, s2n, e3n in infos:
                            comp(dve, lambda h, k_hi=k_hi, qs=qs, e32=e32:
                                 h.reduce_sum(
                                    sume[:, qs:qs + 1], e32[:, :k_hi],
                                    AX),
                                 wbufs=("sume",), rbufs=(e3n,))
                            for g in range((nk + 3) // 4):
                                njj = min(4, nk - g * 4)
                                gp2 = g % 2
                                ebk = 4
                                ehiT = ehiT2[:, gp2 * 512:(gp2 + 1) * 512]
                                eloT = eloT2[:, gp2 * 512:(gp2 + 1) * 512]
                                ehn, eln = f"ehiT{gp2}", f"eloT{gp2}"
                                for j in range(njj):
                                    kc = g * 4 + j
                                    comp(pe, lambda h, kc=kc, j=j,
                                         e32=e32, ebk=ebk:
                                         h.transpose(
                                            bank(ebk)[:, j * 128:
                                                      (j + 1) * 128],
                                            e32[:, kc * 128:
                                                (kc + 1) * 128],
                                            ident32),
                                         wbufs=(f"ps{ebk}",),
                                         rbufs=(e3n, "ident32") if j == 0
                                               else (e3n,))
                                comp(act, lambda h, njj=njj, ehiT=ehiT,
                                     ebk=ebk: h.activation(
                                        ehiT[:, :njj * 128],
                                        bank(ebk, njj * 128), AF.Copy),
                                     wbufs=(ehn,), rbufs=(f"ps{ebk}",))
                                comp(dve, lambda h, njj=njj, eloT=eloT,
                                     ehiT=ehiT, ebk=ebk:
                                     h.scalar_tensor_tensor(
                                        eloT[:, :njj * 128],
                                        bank(ebk, njj * 128), 0.0,
                                        ehiT[:, :njj * 128],
                                        OP.bypass, OP.subtract),
                                     wbufs=(eln,), rbufs=(f"ps{ebk}", ehn))
                                for j in range(njj):
                                    kc = g * 4 + j
                                    seq = [(vhi, ehiT, "vhi", ehn),
                                           (vlo, ehiT, "vlo", ehn),
                                           (vhi, eloT, "vhi", eln)]
                                    for si, (vt, et, vbn, ebn) in \
                                            enumerate(seq):
                                        st = (kc == 0 and si == 0)
                                        sp = (kc == nk - 1 and si == 2)
                                        comp(pe, lambda h, vt=vt, et=et,
                                             kc=kc, j=j, qs=qs, st=st,
                                             sp=sp:
                                             h.matmul(
                                                bank(5)[:, qs * 128:
                                                        (qs + 1) * 128],
                                                vt[:, kc * 128:
                                                   (kc + 1) * 128],
                                                et[:, j * 128:
                                                   (j + 1) * 128],
                                                start=st, stop=sp),
                                             wbufs=("ps5",),
                                             rbufs=(vbn, ebn))
                        for qs in range(NQS):
                            comp(pe, lambda h, qs=qs: h.transpose(
                                    bank(1)[0:1, qs * 128:(qs + 1) * 128],
                                    sume[:, qs:qs + 1], ident32),
                                 wbufs=("ps1",),
                                 rbufs=("sume", "ident32"))
                        comp(dve, lambda h: h.reciprocal(
                                repr_[:1, :], bank(1, TQ)[0:1, :]),
                             wbufs=("repr",), rbufs=("ps1",))
                        replicate_row(repr_, "repr", TQ, reps, "reps",
                                      bank_i=0)
                        comp(dve, lambda h: h.scalar_tensor_tensor(
                                ao32, bank(5, TQ), 0.0, reps,
                                OP.bypass, OP.mult),
                             wbufs=("ao32",), rbufs=("ps5", "reps"))
                        comp(dve, lambda h, aohib=aohib:
                             h.tensor_copy(aohib, ao32),
                             wbufs=(ahb,), rbufs=("ao32",))
                        comp(gp, lambda h, aolob=aolob, aohib=aohib:
                             h.tensor_tensor(aolob, ao32, aohib,
                                             OP.subtract),
                             wbufs=(alb,), rbufs=("ao32", ahb))
                        dma(sq, aohi_s[hh * 128:(hh + 1) * 128, :], aohib,
                            rbufs=(ahb,))
                        dma(sq, aolo_s[hh * 128:(hh + 1) * 128, :], aolob,
                            rbufs=(alb,))
            attention()
            barrier()

            def wo():
                with contextlib.ExitStack() as _s1:
                    aofh_h = _s1.enter_context(
                        SB("aofh", [128, NHC * TQ], BF16))
                    aofl_h = _s1.enter_context(
                        SB("aofl", [128, NHC * TQ], BF16))
                    wq32_h = _s1.enter_context(
                        SB("wo32", [128, 2 * 512], F32))
                    wqh_h = _s1.enter_context(
                        SB("woh", [128, 2 * 512], BF16))
                    wql_h = _s1.enter_context(
                        SB("wol", [128, 2 * 512], BF16))
                    at32_h = _s1.enter_context(SB("at32", [128, 2 * TQ], F32))
                    aofh, aofl = aofh_h[:], aofl_h[:]
                    wq32_2, wqh2, wql2 = wq32_h[:], wqh_h[:], wql_h[:]
                    at32_2 = at32_h[:]
                    for hc in range(NHC):
                        dma(sq, aofh[:, hc * TQ:(hc + 1) * TQ],
                            aohi_s[hc * 128:(hc + 1) * 128, :], wbuf="aofh")
                        dma(sq, aofl[:, hc * TQ:(hc + 1) * TQ],
                            aolo_s[hc * 128:(hc + 1) * 128, :], wbuf="aofl")
                    for ocg in range(NHC // 4):
                        for hc in range(NHC):
                            par = hc % 2
                            w32 = wq32_2[:, par * 512:(par + 1) * 512]
                            wqh = wqh2[:, par * 512:(par + 1) * 512]
                            wql = wql2[:, par * 512:(par + 1) * 512]
                            w32b, whb, wlb = (f"wo32{par}", f"woh{par}",
                                              f"wol{par}")
                            dma(gpq if par == 0 else sq, w32,
                                wo_d[hc * 128:(hc + 1) * 128,
                                     ocg * 512:(ocg + 1) * 512],
                                wbuf=w32b)
                            comp(dve, lambda h, wqh=wqh, w32=w32:
                                 h.tensor_copy(wqh, w32),
                                 wbufs=(whb,), rbufs=(w32b,))
                            comp(gp, lambda h, wql=wql, w32=w32, wqh=wqh:
                                 h.tensor_tensor(wql, w32, wqh,
                                                 OP.subtract),
                                 wbufs=(wlb,), rbufs=(w32b, whb))
                            for j in range(4):
                                seq = [(wqh, aofh, whb, "aofh"),
                                       (wql, aofh, wlb, "aofh"),
                                       (wqh, aofl, whb, "aofl")]
                                for si, (wt, mt, wbn, mbn) in \
                                        enumerate(seq):
                                    st = (hc == 0 and si == 0)
                                    sp = (hc == NHC - 1 and si == 2)
                                    comp(pe, lambda h, wt=wt, mt=mt, hc=hc,
                                         j=j, st=st, sp=sp:
                                         h.matmul(
                                            bank(j, TQ),
                                            wt[:, j * 128:(j + 1) * 128],
                                            mt[:, hc * TQ:(hc + 1) * TQ],
                                            start=st, stop=sp),
                                         wbufs=(f"ps{j}",),
                                         rbufs=(wbn, mbn))
                        for j in range(4):
                            oc = ocg * 4 + j
                            jp = j % 2
                            at32 = at32_2[:, jp * TQ:(jp + 1) * TQ]
                            atb = f"at32{jp}"
                            comp(act, lambda h, at32=at32, j=j:
                                 h.activation(at32, bank(j, TQ), AF.Copy),
                                 wbufs=(atb,), rbufs=(f"ps{j}",))
                            dma(sq, attn_s[oc * 128:(oc + 1) * 128,
                                           t0:t0 + TQ],
                                at32, rbufs=(atb,))
                            if tap == "attn" and b == 0:
                                dma(gpq, dbg_d[oc * 128:(oc + 1) * 128,
                                               :TQ],
                                    at32, rbufs=(atb,))
            wo()
            barrier()

        for b in range(NB):
            phase2_block(b)
        barrier()

        with contextlib.ExitStack() as _st:
            qgT_h = _st.enter_context(SB("qgT", [128, NHC * TQ3], BF16))
            quT_h = _st.enter_context(SB("quT", [128, NHC * TQ3], BF16))
            intm_h = _st.enter_context(SB("intm", [128, NQS3 * I_FF], BF16))
            qiT_h = _st.enter_context(SB("qiT", [128, NIC * TQ3], BF16))
            stat_h = _st.enter_context(SB("stat", [128, 4], F32))
            sccol_h = _st.enter_context(SB("sccol", [128, 3 * NQS3], F32))
            scrow_h = _st.enter_context(SB("scrow", [1, 3 * TQ3], F32))
            sgrep_h = _st.enter_context(SB("sgrep", [128, TQ3], F32))
            surep_h = _st.enter_context(SB("surep", [128, TQ3], F32))
            sirep_h = _st.enter_context(SB("sirep", [128, TQ3], F32))
            qgT, quT = qgT_h[:], quT_h[:]
            intm, qiT = intm_h[:], qiT_h[:]
            stat, sccol, scrow = stat_h[:], sccol_h[:], scrow_h[:]
            sgrep, surep, sirep = sgrep_h[:], surep_h[:], sirep_h[:]
            qgT3 = qgT.rearrange("p (c m) -> p c m", m=TQ3)
            quT3 = quT.rearrange("p (c m) -> p c m", m=TQ3)
            intm3 = intm.rearrange("p (c m) -> p c m", m=I_FF)
            qiT3 = qiT.rearrange("p (c m) -> p c m", m=TQ3)

            def phase3_block(w):
                t0 = w * TQ3
                with contextlib.ExitStack() as _st:
                    xfb_h = _st.enter_context(SB("xfb", [128, 2 * H], BF16))
                    xsb_h = _st.enter_context(SB("xsb", [128, H // 2], F32))
                    tpb_h = _st.enter_context(SB("tpb", [128, H // 2], F32))
                    qrb_h = _st.enter_context(SB("qrb", [128, H // 2], F32))
                    qtm_h = _st.enter_context(SB("qtm", [128, H], BF16))
                    invgr_h = _st.enter_context(SB("invgr", [128, H], F32))
                    xfb2, xsb, tpb = xfb_h[:], xsb_h[:], tpb_h[:]
                    qrb, qtm = qrb_h[:], qtm_h[:]
                    invgr = invgr_h[:]
                    for branch, (invsrc, invsb, qT3, qTb) in enumerate((
                            (invg_r, "invg_r", qgT3, "qgT"),
                            (invu_r, "invu_r", quT3, "quT"))):
                        invt, invb = invgr, "invgr"
                        replicate_row(invsrc, invsb, H, invgr, "invgr")
                        for tt in range(NQS3):
                            xp = (branch * NQS3 + tt) % 2
                            xfb = xfb2[:, xp * H:(xp + 1) * H]
                            xfn = f"xfb{xp}"
                            dma(sq, xfb,
                                xf_s[t0 + tt * 128:t0 + (tt + 1) * 128, :],
                                wbuf=xfn)
                            for hf in range(2):
                                sl = slice(hf * (H // 2), (hf + 1) * (H // 2))
                                comp(dve, lambda h, invt=invt, sl=sl,
                                     xfb=xfb:
                                     h.tensor_tensor(
                                        xsb, xfb[:, sl], invt[:, sl],
                                        OP.mult),
                                     wbufs=("xsb",), rbufs=(xfn, invb))
                                comp(dve, lambda h, hf=hf: h.reduce_max(
                                        stat[:, hf:hf + 1], xsb, AX,
                                        apply_absolute_value=True),
                                     wbufs=(f"st{hf}",), rbufs=("xsb",))
                            comp(dve, lambda h: h.tensor_tensor(
                                    stat[:, 0:1], stat[:, 0:1],
                                    stat[:, 1:2], OP.max),
                                 wbufs=("st0",), rbufs=("st0", "st1"))
                            comp(dve, lambda h: h.tensor_scalar(
                                    stat[:, 1:2], stat[:, 0:1], inv127c,
                                    1e-8, OP.mult, OP.max),
                                 wbufs=("st1",), rbufs=("st0", "inv127c"))
                            comp(dve, lambda h: h.reciprocal(
                                    stat[:, 2:3], stat[:, 1:2]),
                                 wbufs=("st2",), rbufs=("st1",))
                            comp(dve, lambda h, branch=branch, tt=tt:
                                 h.tensor_copy(
                                    sccol[:, branch * NQS3 + tt:
                                          branch * NQS3 + tt + 1],
                                    stat[:, 1:2]),
                                 wbufs=("sccol",), rbufs=("st1",))
                            for hf in range(2):
                                sl = slice(hf * (H // 2), (hf + 1) * (H // 2))
                                comp(dve, lambda h, invt=invt, sl=sl,
                                     xfb=xfb:
                                     h.tensor_tensor(
                                        xsb, xfb[:, sl], invt[:, sl],
                                        OP.mult),
                                     wbufs=("xsb",), rbufs=(xfn, invb))
                                comp(dve, lambda h: h.tensor_scalar(
                                        tpb, xsb, stat[:, 2:3], magc,
                                        OP.mult, OP.add),
                                     wbufs=("tpb",),
                                     rbufs=("xsb", "st2", "magc"))
                                comp(dve, lambda h: h.tensor_scalar(
                                        qrb, tpb, magc, None, OP.subtract),
                                     wbufs=("qrb",), rbufs=("tpb", "magc"))
                                comp(dve, lambda h, sl=sl: h.tensor_scalar(
                                        qtm[:, sl], qrb, -127.0, 127.0,
                                        OP.max, OP.min),
                                     wbufs=("qtm",), rbufs=("qrb",))
                            transpose_batch(
                                qtm, "qtm", NHC,
                                lambda g0, cnt, tt=tt, qT3=qT3:
                                    qT3[:, g0:g0 + cnt,
                                        tt * 128:(tt + 1) * 128],
                                qTb, ident16, "ident16", bf=True)
                barrier()
                for branch, (rep, repb) in enumerate(((sgrep, "sgrep"),
                                                      (surep, "surep"))):
                    for tt in range(NQS3):
                        comp(pe, lambda h, branch=branch, tt=tt: h.transpose(
                                bank(1)[0:1, tt * 128:(tt + 1) * 128],
                                sccol[:, branch * NQS3 + tt:
                                      branch * NQS3 + tt + 1], ident32),
                             wbufs=("ps1",),
                             rbufs=("sccol", "ident32"))
                    comp(dve, lambda h, branch=branch: h.tensor_copy(
                            scrow[:1, branch * TQ3:(branch + 1) * TQ3],
                            bank(1, TQ3)[0:1, :]),
                         wbufs=("scrow",), rbufs=("ps1",))
                    replicate_row(scrow[:1, branch * TQ3:
                                        (branch + 1) * TQ3],
                                  "scrow", TQ3, rep, repb)

                with contextlib.ExitStack() as _st:
                    wgT_h = _st.enter_context(SB("wgT", [128, 2 * H], BF16))
                    s1b_h = _st.enter_context(SB("s1b", [128, NIC * TQ3], F32))
                    gtmp_h = _st.enter_context(SB("gtmp", [128, 2 * TQ3], F32))
                    utmp_h = _st.enter_context(SB("utmp", [128, TQ3], F32))
                    ibg_h = _st.enter_context(SB("ibg", [128, TQ3], BF16))
                    vcb_h = _st.enter_context(SB("vcb", [128, TQ3], F32))
                    dfb_h = _st.enter_context(SB("dfb", [128, TQ3], F32))
                    eib_h = _st.enter_context(SB("eib", [128, TQ3], I32))
                    wgT2 = wgT_h[:]
                    s1b = s1b_h[:]
                    gtmp2 = gtmp_h[:]
                    utmp, ibg = utmp_h[:], ibg_h[:]
                    vcb, dfb, eib = vcb_h[:], dfb_h[:], eib_h[:]
                    for ic in range(NIC):
                        par = ic % 2
                        wgT = wgT2[:, par * H:(par + 1) * H]
                        wgb = f"wgT{par}"
                        bk = 0 if par == 0 else 2
                        dma(gpq if par == 0 else sq, wgT,
                            wgT_s[ic * 128:(ic + 1) * 128, :], wbuf=wgb)
                        for hc in range(NHC):
                            comp(pe, lambda h, hc=hc, wgT=wgT, bk=bk:
                                 h.matmul(
                                    bank(bk, TQ3),
                                    wgT[:, hc * 128:(hc + 1) * 128],
                                    qgT[:, hc * TQ3:(hc + 1) * TQ3],
                                    start=(hc == 0), stop=(hc == NHC - 1)),
                                 wbufs=(f"ps{bk}",),
                                 rbufs=(wgb, "qgT"))
                        gtp = gtmp2[:, par * TQ3:(par + 1) * TQ3]
                        gtn = f"gtmp{par}"
                        comp(dve, lambda h, ic=ic, bk=bk, gtp=gtp:
                             h.scalar_tensor_tensor(
                                gtp, bank(bk, TQ3), swg_c[:, ic:ic + 1],
                                sgrep, OP.mult, OP.mult),
                             wbufs=(gtn,),
                             rbufs=(f"ps{bk}", "swg", "sgrep"))
                        comp(act, lambda h, ic=ic, gtp=gtp: h.activation(
                                s1b[:, ic * TQ3:(ic + 1) * TQ3], gtp,
                                AF.Silu),
                             wbufs=("s1b",), rbufs=(gtn,))
                    for ic in range(NIC):
                        par = ic % 2
                        wgT = wgT2[:, par * H:(par + 1) * H]
                        wgb = f"wgT{par}"
                        bk = 0 if par == 0 else 2
                        gtp = gtmp2[:, par * TQ3:(par + 1) * TQ3]
                        gtn = f"gtmp{par}"
                        dma(gpq if par == 0 else sq, wgT,
                            wuT_s[ic * 128:(ic + 1) * 128, :], wbuf=wgb)
                        for hc in range(NHC):
                            comp(pe, lambda h, hc=hc, wgT=wgT, bk=bk:
                                 h.matmul(
                                    bank(bk, TQ3),
                                    wgT[:, hc * 128:(hc + 1) * 128],
                                    quT[:, hc * TQ3:(hc + 1) * TQ3],
                                    start=(hc == 0), stop=(hc == NHC - 1)),
                                 wbufs=(f"ps{bk}",),
                                 rbufs=(wgb, "quT"))
                        comp(dve, lambda h, ic=ic, bk=bk:
                             h.scalar_tensor_tensor(
                                utmp, bank(bk, TQ3), swu_c[:, ic:ic + 1],
                                surep, OP.mult, OP.mult),
                             wbufs=("utmp",),
                             rbufs=(f"ps{bk}", "swu", "surep"))
                        comp(dve, lambda h, ic=ic: h.tensor_tensor(
                                gtp, utmp,
                                s1b[:, ic * TQ3:(ic + 1) * TQ3],
                                OP.mult),
                             wbufs=(gtn,), rbufs=("utmp", "s1b"))
                        comp(dve, lambda h: h.tensor_scalar(
                                vcb, gtp, -10.0, 10.0, OP.max, OP.min),
                             wbufs=("vcb",), rbufs=(gtn,))
                        emit_bf16_dround(vcb, "vcb", gtp, gtn,
                                         utmp, "utmp", dfb, "dfb",
                                         eib, "eib", ibg, "ibg")
                        transpose_batch(
                            ibg, "ibg", NQS3,
                            lambda g0, cnt, ic=ic:
                                intm3[:, g0:g0 + cnt, ic * 128:
                                      (ic + 1) * 128],
                            "intm", ident16, "ident16", bf=True)
                barrier()
                with contextlib.ExitStack() as _st:
                    xsi_h = _st.enter_context(SB("xsi", [128, I_FF], F32))
                    tpi_h = _st.enter_context(SB("tpi", [128, I_FF], F32))
                    qri_h = _st.enter_context(SB("qri", [128, I_FF], F32))
                    qtmi_h = _st.enter_context(SB("qtmi", [128, I_FF], BF16))
                    invir_h = _st.enter_context(SB("invir", [128, I_FF], F32))
                    xsi, tpi = xsi_h[:], tpi_h[:]
                    qri, qtmi = qri_h[:], qtmi_h[:]
                    invir = invir_h[:]
                    replicate_row(invi_r, "invi_r", I_FF, invir, "invir")
                    for tt in range(NQS3):
                        comp(dve, lambda h, tt=tt: h.tensor_tensor(
                                xsi, intm[:, tt * I_FF:(tt + 1) * I_FF],
                                invir, OP.mult),
                             wbufs=("xsi",), rbufs=("intm", "invir"))
                        comp(dve, lambda h: h.reduce_max(
                                stat[:, 0:1], xsi, AX,
                                apply_absolute_value=True),
                             wbufs=("st0",), rbufs=("xsi",))
                        comp(dve, lambda h: h.tensor_scalar(
                                stat[:, 1:2], stat[:, 0:1], inv127c,
                                1e-8, OP.mult, OP.max),
                             wbufs=("st1",), rbufs=("st0", "inv127c"))
                        comp(dve, lambda h: h.reciprocal(
                                stat[:, 2:3], stat[:, 1:2]),
                             wbufs=("st2",), rbufs=("st1",))
                        comp(dve, lambda h, tt=tt: h.tensor_copy(
                                sccol[:, 2 * NQS3 + tt:2 * NQS3 + tt + 1],
                                stat[:, 1:2]),
                             wbufs=("sccol",), rbufs=("st1",))
                        comp(dve, lambda h: h.tensor_scalar(
                                tpi, xsi, stat[:, 2:3], magc,
                                OP.mult, OP.add),
                             wbufs=("tpi",), rbufs=("xsi", "st2", "magc"))
                        comp(dve, lambda h: h.tensor_scalar(
                                qri, tpi, magc, None, OP.subtract),
                             wbufs=("qri",), rbufs=("tpi", "magc"))
                        comp(dve, lambda h: h.tensor_scalar(
                                qtmi, qri, -127.0, 127.0,
                                OP.max, OP.min),
                             wbufs=("qtmi",), rbufs=("qri",))
                        transpose_batch(
                            qtmi, "qtmi", NIC,
                            lambda g0, cnt, tt=tt:
                                qiT3[:, g0:g0 + cnt,
                                     tt * 128:(tt + 1) * 128],
                            "qiT", ident16, "ident16", bf=True)
                    for tt in range(NQS3):
                        comp(pe, lambda h, tt=tt: h.transpose(
                                bank(1)[0:1, tt * 128:(tt + 1) * 128],
                                sccol[:, 2 * NQS3 + tt:2 * NQS3 + tt + 1],
                                ident32),
                             wbufs=("ps1",),
                             rbufs=("sccol", "ident32"))
                    comp(dve, lambda h: h.tensor_copy(
                            scrow[:1, 2 * TQ3:3 * TQ3], bank(1, TQ3)[0:1, :]),
                         wbufs=("scrow",), rbufs=("ps1",))
                    replicate_row(scrow[:1, 2 * TQ3:3 * TQ3], "scrow",
                                  TQ3, sirep, "sirep")

                barrier()
                with contextlib.ExitStack() as _st:
                    wdT_h = _st.enter_context(SB("wdT", [128, 2 * I_FF], BF16))
                    dtmp_h = _st.enter_context(SB("dtmp", [128, 2 * TQ3], F32))
                    attnb_h = _st.enter_context(SB("attnb", [128, 2 * TQ3], F32))
                    xcol_h = _st.enter_context(SB("xcol", [128, 256], F32))
                    youtb_h = _st.enter_context(SB("youtb", [128, 256], F32))
                    wdT2 = wdT_h[:]
                    dtmp2, attnb2 = dtmp_h[:], attnb_h[:]
                    xcol2, youtb2 = xcol_h[:], youtb_h[:]
                    for hc in range(NHC):
                        par = hc % 2
                        wdT = wdT2[:, par * I_FF:(par + 1) * I_FF]
                        wdb = f"wdT{par}"
                        dtmp = dtmp2[:, par * TQ3:(par + 1) * TQ3]
                        attnb = attnb2[:, par * TQ3:(par + 1) * TQ3]
                        dtb, anb = f"dtmp{par}", f"attnb{par}"
                        bk = 0 if par == 0 else 2
                        dma(gpq if par == 0 else sq, wdT,
                            wdT_s[hc * 128:(hc + 1) * 128, :], wbuf=wdb)
                        for ic in range(NIC):
                            comp(pe, lambda h, ic=ic, wdT=wdT, bk=bk:
                                 h.matmul(
                                    bank(bk, TQ3),
                                    wdT[:, ic * 128:(ic + 1) * 128],
                                    qiT[:, ic * TQ3:(ic + 1) * TQ3],
                                    start=(ic == 0), stop=(ic == NIC - 1)),
                                 wbufs=(f"ps{bk}",),
                                 rbufs=(wdb, "qiT"))
                        comp(dve, lambda h, hc=hc, bk=bk, dtmp=dtmp:
                             h.scalar_tensor_tensor(
                                dtmp, bank(bk, TQ3), swd_c[:, hc:hc + 1],
                                sirep, OP.mult, OP.mult),
                             wbufs=(dtb,),
                             rbufs=(f"ps{bk}", "swd", "sirep"))
                        dma(sq, attnb,
                            attn_s[hc * 128:(hc + 1) * 128, t0:t0 + TQ3],
                            wbuf=anb)
                        comp(dve, lambda h, dtmp=dtmp, attnb=attnb:
                             h.tensor_tensor(dtmp, dtmp, attnb, OP.add),
                             wbufs=(dtb,), rbufs=(dtb, anb))
                        for j in range(NQS3):
                            jp = j % 2
                            xcol = xcol2[:, jp * 128:(jp + 1) * 128]
                            youtb = youtb2[:, jp * 128:(jp + 1) * 128]
                            xcb, ytb = f"xcol{jp}", f"youtb{jp}"
                            tbk = 1 if jp == 0 else 3
                            comp(pe, lambda h, j=j, dtmp=dtmp, tbk=tbk:
                                 h.transpose(
                                    bank(tbk)[:, 0:128],
                                    dtmp[:, j * 128:(j + 1) * 128],
                                    ident32),
                                 wbufs=(f"ps{tbk}",),
                                 rbufs=(dtb, "ident32"))
                            dma(sq, xcol,
                                x_d[t0 + j * 128:t0 + (j + 1) * 128,
                                    hc * 128:(hc + 1) * 128],
                                wbuf=xcb)
                            comp(dve, lambda h, youtb=youtb, xcol=xcol,
                                 tbk=tbk:
                                 h.scalar_tensor_tensor(
                                    youtb, bank(tbk, 128), 0.0, xcol,
                                    OP.bypass, OP.add),
                                 wbufs=(ytb,), rbufs=(f"ps{tbk}", xcb))
                            dma(gpq, y_d[t0 + j * 128:t0 + (j + 1) * 128,
                                        hc * 128:(hc + 1) * 128],
                                youtb, rbufs=(ytb,))
                barrier()


            for w in range(NB3):
                phase3_block(w)
        for ds in dmasems.values():
            if ds.n > 0:
                prog["sq"].append(("w", ds.sem, ds.n))

        def play(h, key):
            for item in prog[key]:
                if item[0] == "w":
                    h.wait_ge(item[1], item[2])
                else:
                    item[1](h).then_inc(item[2], item[3])

        @block.tensor
        def _(h):
            play(h, "pe")

        @block.vector
        def _(h):
            play(h, "dve")

        @block.scalar
        def _(h):
            play(h, "act")

        @block.gpsimd
        def _(h):
            play(h, "gp")

        @block.sync
        def _(h):
            play(h, "sq")

    return nc


def _consts(S):
    id32 = np.eye(128, dtype=np.float32)
    id16 = np.eye(128).astype(ml_dtypes.bfloat16)
    ones1 = np.ones((1, 128), dtype=np.float32)
    c = np.arange(S, dtype=np.float64)
    p = np.arange(128, dtype=np.float64)
    jrow2 = (c[None, :] - (S - 128) - p[:, None]).astype(np.float32)
    mask = np.where(np.arange(128)[None, :] > np.arange(128)[:, None],
                    np.float32(NEG_BIG), np.float32(0.0)).astype(np.float32)
    return id32, id16, ones1, jrow2, mask


def kernel(L0_attn_hidden_states, L0_attn_Wq, L0_attn_Wkv, L0_attn_Wo,
           L0_ffn_w_gate, L0_ffn_s_wgate, L0_ffn_w_up, L0_ffn_s_wup,
           L0_ffn_w_down, L0_ffn_s_wdown, L0_ffn_inv_gate, L0_ffn_inv_up,
           L0_ffn_inv_inter):
    x = np.ascontiguousarray(L0_attn_hidden_states)
    Bb, S, _ = x.shape
    nc = build_nc(S=S, TQ=min(512, S))
    id32, id16, ones1, jrow2, mask = _consts(S)
    base = {
        "Wq": np.ascontiguousarray(L0_attn_Wq),
        "Wkv": np.ascontiguousarray(L0_attn_Wkv),
        "Wo": np.ascontiguousarray(L0_attn_Wo),
        "w_gate": np.ascontiguousarray(L0_ffn_w_gate),
        "w_up": np.ascontiguousarray(L0_ffn_w_up),
        "w_down": np.ascontiguousarray(L0_ffn_w_down),
        "s_wgate": np.ascontiguousarray(L0_ffn_s_wgate),
        "s_wup": np.ascontiguousarray(L0_ffn_s_wup),
        "s_wdown": np.ascontiguousarray(L0_ffn_s_wdown),
        "inv_gate": np.ascontiguousarray(
            np.asarray(L0_ffn_inv_gate).reshape(1, -1)),
        "inv_up": np.ascontiguousarray(
            np.asarray(L0_ffn_inv_up).reshape(1, -1)),
        "inv_inter": np.ascontiguousarray(
            np.asarray(L0_ffn_inv_inter).reshape(1, -1)),
        "ident32": id32, "ident16": id16, "ones1": ones1,
        "jrow2": jrow2, "maskt": mask,
    }
    in_maps = [dict(base, x=np.ascontiguousarray(x[b])) for b in range(Bb)]
    res = run_bass_kernel_spmd(nc, in_maps, core_ids=list(range(Bb)))
    return np.stack([np.asarray(res.results[b]["y"]) for b in range(Bb)],
                    axis=0)


# version 09
# speedup vs baseline: 1.0605x; 1.0445x over previous
import contextlib
import math
import numpy as np
import ml_dtypes
import concourse.bass as bass
import concourse.mybir as mybir
from concourse.bass_utils import run_bass_kernel_spmd

F32 = mybir.dt.float32
BF16 = mybir.dt.bfloat16
I16 = mybir.dt.int16
I32 = mybir.dt.int32
AX = mybir.AxisListType.X
OP = mybir.AluOpType
AF = mybir.ActivationFunctionType

H = 4096
NH = 32
HD = 128
I_FF = 2048
EPS = 1e-5
MAGIC = 12582912.0
INV127 = 1.0 / 127.0
SCALE = float(HD) ** -0.5
NEG_BIG = -1.0e38
NHC = H // 128
NIC = I_FF // 128


def _slopes32():
    start = 2.0 ** (-(2.0 ** (-(math.log2(NH) - 3))))
    return [float(np.float32(start ** (i + 1))) for i in range(NH)]


class REng:

    def __init__(self, prog, key, sem, default_inc, skip_self=True):
        self.prog = prog
        self.key = key
        self.sem = sem
        self.n = 0
        self.default_inc = default_inc
        self.skip_self = skip_self
        self.waited = {}

    def wait(self, dep):
        if dep is None:
            return
        eng, val = dep
        if val <= 0:
            return
        if eng is self and self.skip_self:
            return
        k = id(eng.sem)
        if self.waited.get(k, -1) >= val:
            return
        self.prog[self.key].append(("w", eng.sem, val))
        self.waited[k] = val

    def run(self, fn, deps=(), inc=None):
        for d in deps:
            self.wait(d)
        if inc is None:
            inc = self.default_inc
        self.prog[self.key].append(("e", fn, self.sem, inc))
        self.n += inc
        return (self, self.n)

    def run_raw(self, fn, dsem):
        self.prog[self.key].append(("e", fn, dsem.sem, 16))
        dsem.n += 16
        return None


class DmaSem:

    def __init__(self, sem):
        self.sem = sem
        self.n = 0


class Buf:
    def __init__(self):
        self.wdep = None
        self.rdeps = []

    def write_deps(self):
        d = list(self.rdeps)
        if self.wdep is not None:
            d.append(self.wdep)
        return d

    def wrote(self, dep):
        self.wdep = dep
        self.rdeps = []

    def read_deps(self):
        return [self.wdep] if self.wdep is not None else []

    def read(self, dep):
        self.rdeps.append(dep)


def build_nc(S=2048, TQ=512, debug=False, tap=None):
    assert S % 128 == 0 and TQ % 128 == 0 and S % TQ == 0 and TQ <= 512
    NT = S // 128
    NB = S // TQ
    NQS = TQ // 128
    TQ3 = min(256, TQ)
    NB3 = S // TQ3
    NQS3 = TQ3 // 128

    nc = bass.Bass("TRN2", target_bir_lowering=False, debug=debug,
                   detect_race_conditions=False)

    x_d = nc.dram_tensor("x", [S, H], F32, kind="ExternalInput").ap()
    wq_d = nc.dram_tensor("Wq", [H, H], F32, kind="ExternalInput").ap()
    wkv_d = nc.dram_tensor("Wkv", [H, 2 * HD], F32, kind="ExternalInput").ap()
    wo_d = nc.dram_tensor("Wo", [H, H], F32, kind="ExternalInput").ap()
    wg_d = nc.dram_tensor("w_gate", [I_FF, H], F32, kind="ExternalInput").ap()
    wu_d = nc.dram_tensor("w_up", [I_FF, H], F32, kind="ExternalInput").ap()
    wd_d = nc.dram_tensor("w_down", [H, I_FF], F32, kind="ExternalInput").ap()
    swg_d = nc.dram_tensor("s_wgate", [I_FF], F32, kind="ExternalInput").ap()
    swu_d = nc.dram_tensor("s_wup", [I_FF], F32, kind="ExternalInput").ap()
    swd_d = nc.dram_tensor("s_wdown", [H], F32, kind="ExternalInput").ap()
    invg_d = nc.dram_tensor("inv_gate", [1, H], F32, kind="ExternalInput").ap()
    invu_d = nc.dram_tensor("inv_up", [1, H], F32, kind="ExternalInput").ap()
    invi_d = nc.dram_tensor("inv_inter", [1, I_FF], F32,
                            kind="ExternalInput").ap()
    id32_d = nc.dram_tensor("ident32", [128, 128], F32,
                            kind="ExternalInput").ap()
    id16_d = nc.dram_tensor("ident16", [128, 128], BF16,
                            kind="ExternalInput").ap()
    ones_d = nc.dram_tensor("ones1", [1, 128], F32, kind="ExternalInput").ap()
    jrow_d = nc.dram_tensor("jrow2", [128, S], F32, kind="ExternalInput").ap()
    mask_d = nc.dram_tensor("maskt", [128, 128], F32,
                            kind="ExternalInput").ap()

    y_d = nc.dram_tensor("y", [S, H], F32, kind="ExternalOutput").ap()
    dbg_d = (nc.dram_tensor("dbg", [S, H], F32, kind="ExternalOutput").ap()
             if tap else None)

    xnhi_s = nc.dram_tensor("xnhi_s", [S, H], BF16, kind="Internal").ap()
    xnlo_s = nc.dram_tensor("xnlo_s", [S, H], BF16, kind="Internal").ap()
    xf_s = nc.dram_tensor("xf_s", [S, H], BF16, kind="Internal").ap()
    qhi_s = nc.dram_tensor("qhi_s", [H, TQ], BF16, kind="Internal").ap()
    qlo_s = nc.dram_tensor("qlo_s", [H, TQ], BF16, kind="Internal").ap()
    aohi_s = nc.dram_tensor("aohi_s", [H, TQ], BF16, kind="Internal").ap()
    aolo_s = nc.dram_tensor("aolo_s", [H, TQ], BF16, kind="Internal").ap()
    attn_s = nc.dram_tensor("attn_s", [H, S], F32, kind="Internal").ap()
    xnhfm_s = nc.dram_tensor("xnhfm_s", [NHC * 128, S], BF16,
                             kind="Internal").ap()
    xnlfm_s = nc.dram_tensor("xnlfm_s", [NHC * 128, S], BF16,
                             kind="Internal").ap()
    wgT_s = nc.dram_tensor("wgT_s", [NIC * 128, H], BF16, kind="Internal").ap()
    wuT_s = nc.dram_tensor("wuT_s", [NIC * 128, H], BF16, kind="Internal").ap()
    wdT_s = nc.dram_tensor("wdT_s", [NHC * 128, I_FF], BF16,
                           kind="Internal").ap()

    slopes = _slopes32()

    _semstack = contextlib.ExitStack()
    with (
        _semstack,
        nc.semaphore("m_pe") as sem_pe,
        nc.semaphore("m_dve") as sem_dve,
        nc.semaphore("m_act") as sem_act,
        nc.semaphore("m_gp") as sem_gp,
        nc.semaphore("m_gpd") as sem_gpd,
        nc.semaphore("m_sq") as sem_sq,
        nc.Block() as block,
    ):
        prog = {"pe": [], "dve": [], "act": [], "gp": [], "sq": []}
        pe = REng(prog, "pe", sem_pe, 1)
        dve = REng(prog, "dve", sem_dve, 1, skip_self=False)
        act = REng(prog, "act", sem_act, 1, skip_self=False)
        gp = REng(prog, "gp", sem_gp, 1,
                  skip_self=False)
        gpq = REng(prog, "gp", sem_gpd, 16,
                   skip_self=False)
        sq = REng(prog, "sq", sem_sq, 16,
                  skip_self=False)

        _names = {}

        def SB(name, shape, dt):
            i = _names.get(name, 0)
            _names[name] = i + 1
            return nc.sbuf_tensor(f"sb_{name}_{i}", shape, dt)

        bufs = {}

        def B(name):
            if name not in bufs:
                bufs[name] = Buf()
            return bufs[name]

        dmasems = {}

        def DS(key):
            if key not in dmasems:
                dmasems[key] = DmaSem(_semstack.enter_context(
                    nc.semaphore(f"ds_{key}")))
            return dmasems[key]

        def dma(q, dst_ap, src_ap, wbuf=None, rbufs=(), extra=(),
                slow=False):
            deps = list(extra)
            if wbuf is not None:
                deps += B(wbuf).write_deps()
            for rb in rbufs:
                deps += B(rb).read_deps()
            key = wbuf if wbuf is not None else (rbufs[0] if rbufs
                                                 else "misc")
            ds = DS(key)
            for dd in deps:
                q.wait(dd)
            inst = q.run_raw(lambda h, dst=dst_ap, src=src_ap, slow=slow:
                             h.dma_start(out=dst, in_=src,
                                         allow_slow_non_contiguous=slow),
                             ds)
            d = (ds, ds.n)
            if wbuf is not None:
                B(wbuf).wrote(d)
            for rb in rbufs:
                B(rb).read(d)
            return d

        def comp(eng, fn, wbufs=(), rbufs=(), extra=()):
            deps = list(extra)
            for wb in wbufs:
                deps += B(wb).write_deps()
            for rb in rbufs:
                deps += B(rb).read_deps()
            d = eng.run(fn, deps, inc=1)
            for wb in wbufs:
                B(wb).wrote(d)
            for rb in rbufs:
                B(rb).read(d)
            return d

        def barrier():
            engs = (pe, dve, act, gp, gpq, sq)
            marks = [(e, e.n) for e in engs]
            marks += [(ds, ds.n) for ds in dmasems.values() if ds.n > 0]
            for e in engs:
                for m in marks:
                    e.wait(m)

        ps = nc.alloc_psum_tensor("ps", [128, 3072], F32)
        ps_ap = ps.ap()
        psb = nc.alloc_psum_tensor("psb", [128, 2048], BF16)
        psb_ap = psb.ap()

        def bank(i, w=512):
            return ps_ap[:, i * 512:i * 512 + w]

        def bank_bf(i, w=1024):
            return psb_ap[:, i * 1024:i * 1024 + w]

        def sb(name, shape, dt):
            return nc.alloc_sbuf_tensor("sbr_" + name, shape, dt).ap()

        ident32 = sb("ident32", [128, 128], F32)
        ident16 = sb("ident16", [128, 128], BF16)
        ones1 = sb("ones1", [1, 128], F32)
        jrow2 = sb("jrow2", [128, S], F32)
        maskt = sb("maskt", [128, 128], F32)
        khi = sb("khi", [128, S], BF16)
        klo = sb("klo", [128, S], BF16)
        vhi = sb("vhi", [128, S], BF16)
        vlo = sb("vlo", [128, S], BF16)
        swg_c = sb("swg_c", [128, NIC], F32)
        swu_c = sb("swu_c", [128, NIC], F32)
        swd_c = sb("swd_c", [128, NHC], F32)
        invg_r = sb("invg_r", [1, H], F32)
        invu_r = sb("invu_r", [1, H], F32)
        invi_r = sb("invi_r", [1, I_FF], F32)

        c127 = sb("c127", [128, 1], F32)
        inv127c = sb("inv127c", [128, 1], F32)
        c128 = sb("c128", [128, 1], F32)
        scalec = sb("scalec", [128, 1], F32)
        comp(dve, lambda h: h.memset(c127, 127.0), wbufs=("c127",))
        comp(dve, lambda h: h.reciprocal(inv127c, c127),
             wbufs=("inv127c",), rbufs=("c127",))
        comp(dve, lambda h: h.memset(c128, 128.0), wbufs=("c128",))
        comp(act, lambda h: h.activation(c128, c128, AF.Sqrt),
             wbufs=("c128",), rbufs=("c128",))
        comp(dve, lambda h: h.reciprocal(scalec, c128),
             wbufs=("scalec",), rbufs=("c128",))
        m3c = sb("m3c", [128, 1], F32)
        magc = sb("magc", [128, 1], F32)
        comp(dve, lambda h: h.memset(m3c, 1.0), wbufs=("m3c",))
        comp(dve, lambda h: h.tensor_single_scalar(
                m3c, m3c, float(2.0 ** -30), OP.add),
             wbufs=("m3c",), rbufs=("m3c",))
        comp(dve, lambda h: h.tensor_single_scalar(
                m3c, m3c, -1.0, OP.add),
             wbufs=("m3c",), rbufs=("m3c",))
        comp(dve, lambda h: h.tensor_single_scalar(
                m3c, m3c, float(1.5 * 2.0 ** 82), OP.mult),
             wbufs=("m3c",), rbufs=("m3c",))
        comp(dve, lambda h: h.tensor_single_scalar(
                magc, m3c, 12582912.0, OP.add),
             wbufs=("magc",), rbufs=("m3c",))
        dma(sq, ident32, id32_d, wbuf="ident32")
        dma(sq, ident16, id16_d, wbuf="ident16")
        dma(sq, ones1, ones_d, wbuf="ones1")
        dma(sq, jrow2, jrow_d, wbuf="jrow2")
        dma(sq, maskt, mask_d, wbuf="maskt")
        dma(sq, swg_c, swg_d.rearrange("(c p) -> p c", p=128), wbuf="swg", slow=True)
        dma(sq, swu_c, swu_d.rearrange("(c p) -> p c", p=128), wbuf="swu", slow=True)
        dma(sq, swd_c, swd_d.rearrange("(c p) -> p c", p=128), wbuf="swd", slow=True)
        dma(sq, invg_r, invg_d, wbuf="invg_r")
        dma(sq, invu_r, invu_d, wbuf="invu_r")
        dma(sq, invi_r, invi_d, wbuf="invi_r")

        def replicate_row(row_ap, row_buf, n, dst_ap, dst_buf, bank_i=2):
            for c0 in range(0, n, 512):
                w = min(512, n - c0)
                comp(pe, lambda h, row_ap=row_ap, c0=c0, w=w, bank_i=bank_i:
                     h.matmul(bank(bank_i, w), ones1[:1, :],
                              row_ap[:1, c0:c0 + w], start=True, stop=True),
                     wbufs=(f"ps{bank_i}",), rbufs=("ones1", row_buf))
                comp(dve, lambda h, dst_ap=dst_ap, c0=c0, w=w, bank_i=bank_i:
                     h.tensor_copy(dst_ap[:, c0:c0 + w], bank(bank_i, w)),
                     wbufs=(dst_buf,), rbufs=(f"ps{bank_i}",))

        def transpose_batch(src_ap, src_buf, n_tiles, dst_fn, dst_buf,
                            ident, ident_buf, bf, banks=(4, 5)):
            per = 8 if bf else 4
            g0 = 0
            while g0 < n_tiles:
                cnt = min(per, n_tiles - g0)
                if bf:
                    bi = (g0 // per) % 2
                    wb = f"psb{bi}"
                else:
                    bi = banks[(g0 // per) % len(banks)]
                    wb = f"ps{bi}"
                for j in range(cnt):
                    tl = g0 + j
                    out_ap = (bank_bf(bi)[:, j * 128:(j + 1) * 128] if bf
                              else bank(bi)[:, j * 128:(j + 1) * 128])
                    comp(pe, lambda h, out_ap=out_ap, tl=tl:
                         h.transpose(out_ap,
                                     src_ap[:, tl * 128:(tl + 1) * 128],
                                     ident),
                         wbufs=(wb,),
                         rbufs=(src_buf, ident_buf) if j == 0
                               else (src_buf,))
                dst_ap = dst_fn(g0, cnt)
                src2 = (bank_bf(bi)[:, :cnt * 128] if bf
                        else bank(bi)[:, :cnt * 128])
                comp(dve, lambda h, dst_ap=dst_ap, src2=src2:
                     h.tensor_copy(dst_ap, src2),
                     wbufs=(dst_buf,), rbufs=(wb,))
                g0 += cnt

        def emit_bf16_dround(V, Vbuf, A, Abuf, Bv, Bbuf, D, Dbuf, E, Ebuf,
                             O, Obuf):
            EXPM = 0x7F800000
            comp(dve, lambda h: h.tensor_single_scalar(
                    E, V.bitcast(I32), EXPM, OP.bitwise_and),
                 wbufs=(Ebuf,), rbufs=(Vbuf,))
            comp(dve, lambda h: h.tensor_single_scalar(
                    E, E, float(23 * 2 ** 23), OP.subtract),
                 wbufs=(Ebuf,), rbufs=(Ebuf,))
            comp(dve, lambda h: h.tensor_scalar(
                    Bv, E.bitcast(F32), m3c, None, OP.mult),
                 wbufs=(Bbuf,), rbufs=(Ebuf, "m3c"))
            comp(dve, lambda h: h.tensor_tensor(A, V, Bv, OP.add),
                 wbufs=(Abuf,), rbufs=(Vbuf, Bbuf))
            comp(dve, lambda h: h.tensor_tensor(A, A, Bv, OP.subtract),
                 wbufs=(Abuf,), rbufs=(Abuf, Bbuf))
            comp(dve, lambda h: h.tensor_single_scalar(
                    E, A.bitcast(I32), EXPM, OP.bitwise_and),
                 wbufs=(Ebuf,), rbufs=(Abuf,))
            comp(dve, lambda h: h.tensor_single_scalar(
                    E, E, float(7 * 2 ** 23), OP.subtract),
                 wbufs=(Ebuf,), rbufs=(Ebuf,))
            comp(dve, lambda h: h.tensor_scalar(
                    Bv, E.bitcast(F32), m3c, None, OP.mult),
                 wbufs=(Bbuf,), rbufs=(Ebuf, "m3c"))
            comp(dve, lambda h: h.tensor_tensor(A, A, Bv, OP.add),
                 wbufs=(Abuf,), rbufs=(Abuf, Bbuf))
            comp(dve, lambda h: h.tensor_tensor(O, A, Bv, OP.subtract),
                 wbufs=(Obuf,), rbufs=(Abuf, Bbuf))

        with contextlib.ExitStack() as _st:
            wkv32_h = _st.enter_context(SB("wkv32", [128, NHC * 32], F32))
            wkvh_h = _st.enter_context(SB("wkvh", [128, NHC * 256], BF16))
            wkvl_h = _st.enter_context(SB("wkvl", [128, NHC * 256], BF16))
            p1x_h = _st.enter_context(SB("p1x", [128, H], F32))
            p1xn_h = _st.enter_context(SB("p1xn", [128, H], F32))
            p1hi_h = _st.enter_context(SB("p1hi", [128, H], BF16))
            p1lo_h = _st.enter_context(SB("p1lo", [128, H], BF16))
            p1tr_h = _st.enter_context(SB("p1tr", [128, H], BF16))
            wpb_h = _st.enter_context(SB("wpb", [128, H], BF16))
            wpt_h = _st.enter_context(SB("wpt", [128, H], BF16))
            p1fmh_h = _st.enter_context(SB("p1fmh", [128, NHC * 128], BF16))
            p1fml_h = _st.enter_context(SB("p1fml", [128, NHC * 128], BF16))
            p1st_h = _st.enter_context(SB("p1st", [128, 8], F32))
            p1kv_h = _st.enter_context(SB("p1kv", [128, 128], F32))
            p1xc_h = _st.enter_context(SB("p1xc", [128, H // 4], F32))
            p1xw_h = _st.enter_context(SB("p1xw", [128, H // 4], F32))
            p1xe_h = _st.enter_context(SB("p1xe", [128, H // 4], I32))
            p1xd_h = _st.enter_context(SB("p1xd", [128, H // 4], F32))
            wkv32, wkvh, wkvl = wkv32_h[:], wkvh_h[:], wkvl_h[:]
            p1x, p1xn = p1x_h[:], p1xn_h[:]
            p1hi, p1lo, p1tr = p1hi_h[:], p1lo_h[:], p1tr_h[:]
            wpb, wpt = wpb_h[:], wpt_h[:]
            p1fmh, p1fml = p1fmh_h[:], p1fml_h[:]
            p1st, p1kv = p1st_h[:], p1kv_h[:]
            p1xc, p1xw, p1xe = p1xc_h[:], p1xw_h[:], p1xe_h[:]
            p1xd = p1xd_h[:]

            for grp in range(8):
                for j in range(4):
                    hc = grp * 4 + j
                    dma(sq, wkv32[:, j * 256:(j + 1) * 256],
                        wkv_d[hc * 128:(hc + 1) * 128, :], wbuf="wkv32")
                sl = slice(grp * 1024, (grp + 1) * 1024)
                comp(dve, lambda h, sl=sl: h.tensor_copy(
                        wkvh[:, sl], wkv32),
                     wbufs=("wkvh",), rbufs=("wkv32",))
                comp(gp, lambda h, sl=sl: h.tensor_tensor(
                        wkvl[:, sl], wkv32, wkvh[:, sl], OP.subtract),
                     wbufs=("wkvl",), rbufs=("wkv32", "wkvh"))

            for src_d, dst_s, nch, wid in ((wg_d, wgT_s, NIC, H),
                                           (wu_d, wuT_s, NIC, H),
                                           (wd_d, wdT_s, NHC, I_FF)):
                for c in range(nch):
                    dma(gpq, wpb[:, :wid], src_d[c * 128:(c + 1) * 128, :],
                        wbuf="wpb")
                    transpose_batch(
                        wpb[:, :wid], "wpb", wid // 128,
                        lambda g0, cnt: wpt[:, g0 * 128:(g0 + cnt) * 128],
                        "wpt", ident16, "ident16", bf=True)
                    dma(sq, dst_s[c * 128:(c + 1) * 128, :wid],
                        wpt[:, :wid], rbufs=("wpt",))
            for t in range(NT):
                xb = "p1x"
                xt = p1x
                dma(sq, xt, x_d[t * 128:(t + 1) * 128, :], wbuf=xb)
                comp(dve, lambda h, xt=xt: h.reduce_sum(p1st[:, 0:1], xt, AX),
                     wbufs=("p1s1",), rbufs=(xb,))
                for qq in range(4):
                    sl = slice(qq * (H // 4), (qq + 1) * (H // 4))
                    comp(act, lambda h, xt=xt, sl=sl: h.activation(
                            p1xc, xt[:, sl], AF.Square),
                         wbufs=("p1xc",), rbufs=(xb,))
                    comp(dve, lambda h, qq=qq: h.reduce_sum(
                            p1st[:, 1:2] if qq == 0 else p1st[:, 7:8],
                            p1xc, AX),
                         wbufs=("p1s2" if qq == 0 else "p1ve",),
                         rbufs=("p1xc",))
                    if qq > 0:
                        comp(dve, lambda h: h.tensor_tensor(
                                p1st[:, 1:2], p1st[:, 1:2], p1st[:, 7:8],
                                OP.add),
                             wbufs=("p1s2",), rbufs=("p1s2", "p1ve"))
                comp(dve, lambda h: h.tensor_scalar(
                        p1st[:, 2:3], p1st[:, 0:1], 1.0 / H, None, OP.mult),
                     wbufs=("p1mu",), rbufs=("p1s1",))
                comp(dve, lambda h: h.tensor_tensor(
                        p1st[:, 3:4], p1st[:, 2:3], p1st[:, 2:3], OP.mult),
                     wbufs=("p1mu2",), rbufs=("p1mu",))
                comp(dve, lambda h: h.scalar_tensor_tensor(
                        p1st[:, 4:5], p1st[:, 1:2], 1.0 / H, p1st[:, 3:4],
                        OP.mult, OP.subtract),
                     wbufs=("p1var",), rbufs=("p1s2", "p1mu2"))
                comp(dve, lambda h: h.tensor_scalar(
                        p1st[:, 7:8], p1st[:, 4:5], EPS, None, OP.add),
                     wbufs=("p1ve",), rbufs=("p1var",))
                comp(act, lambda h: h.activation(
                        p1st[:, 5:6], p1st[:, 7:8], AF.Sqrt),
                     wbufs=("p1sd",), rbufs=("p1ve",))
                comp(dve, lambda h: h.reciprocal(p1st[:, 6:7], p1st[:, 5:6]),
                     wbufs=("p1istd",), rbufs=("p1sd",))
                comp(dve, lambda h, xt=xt: h.tensor_scalar(
                        p1xn, xt, p1st[:, 2:3], p1st[:, 6:7],
                        OP.subtract, OP.mult),
                     wbufs=("p1xn",), rbufs=(xb, "p1mu", "p1istd"))
                if tap == "st":
                    dma(gpq, dbg_d[t * 128:(t + 1) * 128, 0:8], p1st,
                        rbufs=("p1s1", "p1s2", "p1mu", "p1mu2", "p1var",
                               "p1sd", "p1istd", "p1ve"))
                if tap == "xn":
                    dma(gpq, dbg_d[t * 128:(t + 1) * 128, :], p1xn,
                        rbufs=("p1xn",))
                if tap == "x":
                    dma(gpq, dbg_d[t * 128:(t + 1) * 128, :], xt,
                        rbufs=(xb,))
                comp(dve, lambda h: h.tensor_copy(p1hi, p1xn),
                     wbufs=("p1hi",), rbufs=("p1xn",))
                comp(gp, lambda h: h.tensor_tensor(
                        p1lo, p1xn, p1hi, OP.subtract),
                     wbufs=("p1lo",), rbufs=("p1xn", "p1hi"))
                dma(sq, xnhi_s[t * 128:(t + 1) * 128, :], p1hi,
                    rbufs=("p1hi",))
                dma(sq, xnlo_s[t * 128:(t + 1) * 128, :], p1lo,
                    rbufs=("p1lo",))
                for hhalf in range(4):
                    sl = slice(hhalf * (H // 4), (hhalf + 1) * (H // 4))
                    emit_bf16_dround(p1xn[:, sl], "p1xn", p1xc, "p1xc",
                                     p1xw, "p1xw", p1xd, "p1xd",
                                     p1xe, "p1xe", p1tr[:, sl], "p1tr")
                dma(sq, xf_s[t * 128:(t + 1) * 128, :], p1tr,
                    rbufs=("p1tr",))
                transpose_batch(p1hi, "p1hi", NHC,
                                lambda g0, cnt: p1fmh[:, g0 * 128:
                                                      (g0 + cnt) * 128],
                                "p1fmh", ident16, "ident16", bf=True)
                transpose_batch(p1lo, "p1lo", NHC,
                                lambda g0, cnt: p1fml[:, g0 * 128:
                                                      (g0 + cnt) * 128],
                                "p1fml", ident16, "ident16", bf=True)
                for fmt, dst_s, fb in ((p1fmh, xnhfm_s, "p1fmh"),
                                       (p1fml, xnlfm_s, "p1fml")):
                    dma(gpq,
                        dst_s.rearrange("(hc p) s -> p hc s", p=128)
                        [:, :, t * 128:(t + 1) * 128],
                        fmt.rearrange("p (hc c) -> p hc c", c=128),
                        rbufs=(fb,))
                for oc in range(2):
                    wb = "ps4"
                    seq = [(wkvh, p1fmh, "wkvh", "p1fmh"),
                           (wkvl, p1fmh, "wkvl", "p1fmh"),
                           (wkvh, p1fml, "wkvh", "p1fml")]
                    idx = 0
                    for hc in range(NHC):
                        for si, (wt, mt, wbn, mbn) in enumerate(seq):
                            st = (idx == 0)
                            sp = (idx == NHC * 3 - 1)
                            comp(pe, lambda h, wt=wt, mt=mt, hc=hc, oc=oc,
                                 st=st, sp=sp:
                                 h.matmul(bank(4, 128),
                                          wt[:, hc * 256 + oc * 128:
                                             hc * 256 + oc * 128 + 128],
                                          mt[:, hc * 128:(hc + 1) * 128],
                                          start=st, stop=sp),
                                 wbufs=(wb,),
                                 rbufs=(wbn, mbn))
                            idx += 1
                    if oc == 0:
                        comp(dve, lambda h, t=t: h.tensor_copy(
                                khi[:, t * 128:(t + 1) * 128], bank(4, 128)),
                             wbufs=("khi",), rbufs=(wb,))
                        comp(dve, lambda h, t=t: h.scalar_tensor_tensor(
                                klo[:, t * 128:(t + 1) * 128], bank(4, 128),
                                0.0, khi[:, t * 128:(t + 1) * 128],
                                OP.bypass, OP.subtract),
                             wbufs=("klo",), rbufs=(wb, "khi"))
                    else:
                        comp(dve, lambda h: h.tensor_copy(p1kv, bank(4, 128)),
                             wbufs=("p1kv",), rbufs=(wb,))
                        comp(pe, lambda h: h.transpose(
                                bank(5, 128), p1kv, ident32),
                             wbufs=("ps5",), rbufs=("p1kv", "ident32"))
                        comp(act, lambda h, t=t: h.activation(
                                vhi[:, t * 128:(t + 1) * 128], bank(5, 128),
                                AF.Copy),
                             wbufs=("vhi",), rbufs=("ps5",))
                        comp(dve, lambda h, t=t: h.scalar_tensor_tensor(
                                vlo[:, t * 128:(t + 1) * 128], bank(5, 128),
                                0.0, vhi[:, t * 128:(t + 1) * 128],
                                OP.bypass, OP.subtract),
                             wbufs=("vlo",), rbufs=("ps5", "vhi"))
            if tap == "kv":
                dma(gpq, dbg_d[0:128, 0:S].bitcast(BF16)[:, :S], khi,
                    rbufs=("khi",))
                dma(gpq, dbg_d[128:256, 0:S].bitcast(BF16)[:, :S], vhi,
                    rbufs=("vhi",))
        barrier()

        def phase2_block(b):
            t0 = b * TQ

            def qproj():
                with contextlib.ExitStack() as _s1:
                    p2fmh_h = _s1.enter_context(
                        SB("p2fmh", [128, NHC * TQ], BF16))
                    p2fml_h = _s1.enter_context(
                        SB("p2fml", [128, NHC * TQ], BF16))
                    p2fmh, p2fml = p2fmh_h[:], p2fml_h[:]
                    for src_s, fmt, fb in ((xnhfm_s, p2fmh, "p2fmh"),
                                           (xnlfm_s, p2fml, "p2fml")):
                        fmt3 = fmt.rearrange("p (c m) -> p c m", m=TQ)
                        dma(sq, fmt3,
                            src_s.rearrange("(hc p) s -> p hc s", p=128)
                            [:, :, t0:t0 + TQ],
                            wbuf=fb)
                    barrier()
                    with contextlib.ExitStack() as _s2:
                        wq32_h = _s2.enter_context(
                            SB("wq32", [128, 2 * 512], F32))
                        wqh_h = _s2.enter_context(
                            SB("wqh", [128, 2 * 512], BF16))
                        wql_h = _s2.enter_context(
                            SB("wql", [128, 2 * 512], BF16))
                        q32_h = _s2.enter_context(
                            SB("q32", [128, 2 * TQ], F32))
                        qhib_h = _s2.enter_context(
                            SB("qhib", [128, 2 * TQ], BF16))
                        qlob_h = _s2.enter_context(
                            SB("qlob", [128, 2 * TQ], BF16))
                        wq32_2, wqh2, wql2 = wq32_h[:], wqh_h[:], wql_h[:]
                        q32_2, qhib2, qlob2 = q32_h[:], qhib_h[:], qlob_h[:]
                        for ocg in range(NHC // 4):
                            for hc in range(NHC):
                                par = hc % 2
                                w32 = wq32_2[:, par * 512:(par + 1) * 512]
                                wqh = wqh2[:, par * 512:(par + 1) * 512]
                                wql = wql2[:, par * 512:(par + 1) * 512]
                                w32b, whb, wlb = (f"wq32{par}",
                                                  f"wqh{par}", f"wql{par}")
                                dma(gpq if par == 0 else sq, w32,
                                    wq_d[hc * 128:(hc + 1) * 128,
                                         ocg * 512:(ocg + 1) * 512],
                                    wbuf=w32b)
                                comp(dve, lambda h, wqh=wqh, w32=w32:
                                     h.tensor_copy(wqh, w32),
                                     wbufs=(whb,), rbufs=(w32b,))
                                comp(gp, lambda h, wql=wql, w32=w32,
                                     wqh=wqh:
                                     h.tensor_tensor(wql, w32, wqh,
                                                     OP.subtract),
                                     wbufs=(wlb,), rbufs=(w32b, whb))
                                for j in range(4):
                                    seq = [(wqh, p2fmh, whb, "p2fmh"),
                                           (wql, p2fmh, wlb, "p2fmh"),
                                           (wqh, p2fml, whb, "p2fml")]
                                    for si, (wt, mt, wbn, mbn) in \
                                            enumerate(seq):
                                        st = (hc == 0 and si == 0)
                                        sp = (hc == NHC - 1 and si == 2)
                                        comp(pe, lambda h, wt=wt, mt=mt,
                                             hc=hc, j=j, st=st, sp=sp:
                                             h.matmul(
                                                bank(j, TQ),
                                                wt[:, j * 128:
                                                   (j + 1) * 128],
                                                mt[:, hc * TQ:
                                                   (hc + 1) * TQ],
                                                start=st, stop=sp),
                                             wbufs=(f"ps{j}",),
                                             rbufs=(wbn, mbn))
                            for j in range(4):
                                oc = ocg * 4 + j
                                jp = j % 2
                                qhib = qhib2[:, jp * TQ:(jp + 1) * TQ]
                                qlob = qlob2[:, jp * TQ:(jp + 1) * TQ]
                                qhb, qlb = f"qhib{jp}", f"qlob{jp}"
                                comp(act, lambda h, qhib=qhib, j=j:
                                     h.activation(qhib, bank(j, TQ),
                                                  AF.Copy, scale=scalec),
                                     wbufs=(qhb,),
                                     rbufs=(f"ps{j}", "scalec"))
                                comp(dve, lambda h, qlob=qlob, qhib=qhib,
                                     j=j:
                                     h.scalar_tensor_tensor(
                                        qlob, bank(j, TQ), scalec, qhib,
                                        OP.mult, OP.subtract),
                                     wbufs=(qlb,),
                                     rbufs=(f"ps{j}", qhb, "scalec"))
                                dma(sq, qhi_s[oc * 128:(oc + 1) * 128, :],
                                    qhib, rbufs=(qhb,))
                                dma(sq, qlo_s[oc * 128:(oc + 1) * 128, :],
                                    qlob, rbufs=(qlb,))
            qproj()
            barrier()

            def attention():
                with contextlib.ExitStack() as _s1:
                    biasb_h = _s1.enter_context(SB("biasb", [128, S], I32))
                    s2b_h = _s1.enter_context(
                        SB("s2b", [128, NQS * S], F32))
                    e32_h = _s1.enter_context(
                        SB("e32", [128, NQS * S], F32))
                    mx_h = _s1.enter_context(SB("mx", [128, 2 * NQS], F32))
                    sume_h = _s1.enter_context(SB("sume", [128, NQS], F32))
                    ehiT_h = _s1.enter_context(SB("ehiT", [128, 1024], BF16))
                    eloT_h = _s1.enter_context(SB("eloT", [128, 1024], BF16))
                    repr_h = _s1.enter_context(SB("repr", [1, TQ], F32))
                    reps_h = _s1.enter_context(SB("reps", [128, TQ], F32))
                    ao32_h = _s1.enter_context(SB("ao32", [128, TQ], F32))
                    aohib_h = _s1.enter_context(
                        SB("aohib", [128, 2 * TQ], BF16))
                    aolob_h = _s1.enter_context(
                        SB("aolob", [128, 2 * TQ], BF16))
                    qhib_h = _s1.enter_context(
                        SB("qhibA", [128, 2 * TQ], BF16))
                    qlob_h = _s1.enter_context(
                        SB("qlobA", [128, 2 * TQ], BF16))
                    bias_view = biasb_h[:].bitcast(F32)
                    s2b2, e32_2 = s2b_h[:], e32_h[:]
                    mx, sume = mx_h[:], sume_h[:]
                    ehiT2, eloT2 = ehiT_h[:], eloT_h[:]
                    repr_, reps = repr_h[:], reps_h[:]
                    ao32 = ao32_h[:]
                    aohib2, aolob2 = aohib_h[:], aolob_h[:]
                    qhib2, qlob2 = qhib_h[:], qlob_h[:]
                    pending_evac = []
                    for hh in range(NH):
                        hp = hh % 2
                        qhib = qhib2[:, hp * TQ:(hp + 1) * TQ]
                        qlob = qlob2[:, hp * TQ:(hp + 1) * TQ]
                        qhb, qlb = f"qhibA{hp}", f"qlobA{hp}"
                        aohib = aohib2[:, hp * TQ:(hp + 1) * TQ]
                        aolob = aolob2[:, hp * TQ:(hp + 1) * TQ]
                        ahb, alb = f"aohib{hp}", f"aolob{hp}"
                        dma(sq, qhib, qhi_s[hh * 128:(hh + 1) * 128, :],
                            wbuf=qhb)
                        dma(sq, qlob, qlo_s[hh * 128:(hh + 1) * 128, :],
                            wbuf=qlb)
                        comp(dve, lambda h, hh=hh: h.tensor_scalar(
                                bias_view, jrow2, slopes[hh], None,
                                OP.mult),
                             wbufs=("biasb",), rbufs=("jrow2",))
                        comp(dve, lambda h: h.tensor_tensor(
                                bias_view[:, S - 128:S],
                                bias_view[:, S - 128:S], maskt, OP.add),
                             wbufs=("biasb",), rbufs=("biasb", "maskt"))
                        infos = []
                        for qs in range(NQS):
                            pq = qs
                            q0 = t0 + qs * 128
                            nk = q0 // 128 + 1
                            k_hi = nk * 128
                            infos.append((qs, pq, nk, k_hi,
                                          s2b2[:, pq * S:(pq + 1) * S],
                                          e32_2[:, pq * S:(pq + 1) * S],
                                          f"s2b{pq}", f"e32{pq}"))
                        for qs, pq, nk, k_hi, s2b, e32, s2n, e3n in infos:
                            nkt = (k_hi + 511) // 512
                            for kt in range(nkt):
                                ntk = min(512, k_hi - kt * 512)
                                bk = 2 * (qs % 2) + (kt % 2)
                                wb = f"ps{bk}"
                                seq = [(qhib, khi, qhb, "khi"),
                                       (qlob, khi, qlb, "khi"),
                                       (qhib, klo, qhb, "klo")]
                                for si, (qt, ktt, qbn, kbn) in \
                                        enumerate(seq):
                                    comp(pe, lambda h, qt=qt, ktt=ktt,
                                         qs=qs, kt=kt, ntk=ntk, si=si,
                                         bk=bk:
                                         h.matmul(
                                            bank(bk, ntk),
                                            qt[:, qs * 128:(qs + 1) * 128],
                                            ktt[:, kt * 512:
                                                kt * 512 + ntk],
                                            start=(si == 0),
                                            stop=(si == 2)),
                                         wbufs=(wb,),
                                         rbufs=(qbn, kbn))
                                comp(dve, lambda h, kt=kt, ntk=ntk,
                                     k_hi=k_hi, s2b=s2b, bk=bk:
                                     h.tensor_tensor(
                                        s2b[:, kt * 512:kt * 512 + ntk],
                                        bank(bk, ntk),
                                        bias_view[:, S - k_hi + kt * 512:
                                                  S - k_hi + kt * 512
                                                  + ntk],
                                        OP.add),
                                     wbufs=(s2n,), rbufs=(wb, "biasb"))
                        while pending_evac:
                            pending_evac.pop(0)()
                        for qs, pq, nk, k_hi, s2b, e32, s2n, e3n in infos:
                            comp(dve, lambda h, k_hi=k_hi, s2b=s2b, pq=pq:
                                 h.reduce_max(
                                    mx[:, 2 * pq + 1:2 * pq + 2],
                                    s2b[:, :k_hi], AX, negate=True),
                                 wbufs=(f"mx1_{pq}",), rbufs=(s2n,))
                            comp(act, lambda h, k_hi=k_hi, s2b=s2b,
                                 e32=e32, pq=pq: h.activation(
                                    e32[:, :k_hi], s2b[:, :k_hi], AF.Exp,
                                    bias=mx[:, 2 * pq + 1:2 * pq + 2]),
                                 wbufs=(e3n,), rbufs=(s2n, f"mx1_{pq}"))
                        for qs, pq, nk, k_hi, s2b, e32, s2n, e3n in infos:
                            comp(dve, lambda h, k_hi=k_hi, qs=qs, e32=e32:
                                 h.reduce_sum(
                                    sume[:, qs:qs + 1], e32[:, :k_hi],
                                    AX),
                                 wbufs=("sume",), rbufs=(e3n,))
                            for g in range((nk + 3) // 4):
                                njj = min(4, nk - g * 4)
                                gp2 = g % 2
                                ebk = 4
                                ehiT = ehiT2[:, gp2 * 512:(gp2 + 1) * 512]
                                eloT = eloT2[:, gp2 * 512:(gp2 + 1) * 512]
                                ehn, eln = f"ehiT{gp2}", f"eloT{gp2}"
                                for j in range(njj):
                                    kc = g * 4 + j
                                    comp(pe, lambda h, kc=kc, j=j,
                                         e32=e32, ebk=ebk:
                                         h.transpose(
                                            bank(ebk)[:, j * 128:
                                                      (j + 1) * 128],
                                            e32[:, kc * 128:
                                                (kc + 1) * 128],
                                            ident32),
                                         wbufs=(f"ps{ebk}",),
                                         rbufs=(e3n, "ident32") if j == 0
                                               else (e3n,))
                                comp(act, lambda h, njj=njj, ehiT=ehiT,
                                     ebk=ebk: h.activation(
                                        ehiT[:, :njj * 128],
                                        bank(ebk, njj * 128), AF.Copy),
                                     wbufs=(ehn,), rbufs=(f"ps{ebk}",))
                                comp(dve, lambda h, njj=njj, eloT=eloT,
                                     ehiT=ehiT, ebk=ebk:
                                     h.scalar_tensor_tensor(
                                        eloT[:, :njj * 128],
                                        bank(ebk, njj * 128), 0.0,
                                        ehiT[:, :njj * 128],
                                        OP.bypass, OP.subtract),
                                     wbufs=(eln,), rbufs=(f"ps{ebk}", ehn))
                                for j in range(njj):
                                    kc = g * 4 + j
                                    seq = [(vhi, ehiT, "vhi", ehn),
                                           (vlo, ehiT, "vlo", ehn),
                                           (vhi, eloT, "vhi", eln)]
                                    for si, (vt, et, vbn, ebn) in \
                                            enumerate(seq):
                                        st = (kc == 0 and si == 0)
                                        sp = (kc == nk - 1 and si == 2)
                                        comp(pe, lambda h, vt=vt, et=et,
                                             kc=kc, j=j, qs=qs, st=st,
                                             sp=sp:
                                             h.matmul(
                                                bank(5)[:, qs * 128:
                                                        (qs + 1) * 128],
                                                vt[:, kc * 128:
                                                   (kc + 1) * 128],
                                                et[:, j * 128:
                                                   (j + 1) * 128],
                                                start=st, stop=sp),
                                             wbufs=("ps5",),
                                             rbufs=(vbn, ebn))
                        def make_evac(hh=hh, aohib=aohib, aolob=aolob,
                                      ahb=ahb, alb=alb):
                            def evac():
                                for qs in range(NQS):
                                    comp(pe, lambda h, qs=qs: h.transpose(
                                            bank(1)[0:1, qs * 128:
                                                    (qs + 1) * 128],
                                            sume[:, qs:qs + 1], ident32),
                                         wbufs=("ps1",),
                                         rbufs=("sume", "ident32"))
                                comp(dve, lambda h: h.reciprocal(
                                        repr_[:1, :], bank(1, TQ)[0:1, :]),
                                     wbufs=("repr",), rbufs=("ps1",))
                                replicate_row(repr_, "repr", TQ, reps,
                                              "reps", bank_i=0)
                                comp(dve, lambda h: h.scalar_tensor_tensor(
                                        ao32, bank(5, TQ), 0.0, reps,
                                        OP.bypass, OP.mult),
                                     wbufs=("ao32",), rbufs=("ps5", "reps"))
                                comp(dve, lambda h:
                                     h.tensor_copy(aohib, ao32),
                                     wbufs=(ahb,), rbufs=("ao32",))
                                comp(gp, lambda h:
                                     h.tensor_tensor(aolob, ao32, aohib,
                                                     OP.subtract),
                                     wbufs=(alb,), rbufs=("ao32", ahb))
                                dma(sq, aohi_s[hh * 128:(hh + 1) * 128, :],
                                    aohib, rbufs=(ahb,))
                                dma(sq, aolo_s[hh * 128:(hh + 1) * 128, :],
                                    aolob, rbufs=(alb,))
                            return evac
                        pending_evac.append(make_evac())

                    while pending_evac:
                        pending_evac.pop(0)()
            attention()
            barrier()

            def wo():
                with contextlib.ExitStack() as _s1:
                    aofh_h = _s1.enter_context(
                        SB("aofh", [128, NHC * TQ], BF16))
                    aofl_h = _s1.enter_context(
                        SB("aofl", [128, NHC * TQ], BF16))
                    wq32_h = _s1.enter_context(
                        SB("wo32", [128, 2 * 512], F32))
                    wqh_h = _s1.enter_context(
                        SB("woh", [128, 2 * 512], BF16))
                    wql_h = _s1.enter_context(
                        SB("wol", [128, 2 * 512], BF16))
                    at32_h = _s1.enter_context(SB("at32", [128, 2 * TQ], F32))
                    aofh, aofl = aofh_h[:], aofl_h[:]
                    wq32_2, wqh2, wql2 = wq32_h[:], wqh_h[:], wql_h[:]
                    at32_2 = at32_h[:]
                    for hc in range(NHC):
                        dma(sq, aofh[:, hc * TQ:(hc + 1) * TQ],
                            aohi_s[hc * 128:(hc + 1) * 128, :], wbuf="aofh")
                        dma(sq, aofl[:, hc * TQ:(hc + 1) * TQ],
                            aolo_s[hc * 128:(hc + 1) * 128, :], wbuf="aofl")
                    for ocg in range(NHC // 4):
                        for hc in range(NHC):
                            par = hc % 2
                            w32 = wq32_2[:, par * 512:(par + 1) * 512]
                            wqh = wqh2[:, par * 512:(par + 1) * 512]
                            wql = wql2[:, par * 512:(par + 1) * 512]
                            w32b, whb, wlb = (f"wo32{par}", f"woh{par}",
                                              f"wol{par}")
                            dma(gpq if par == 0 else sq, w32,
                                wo_d[hc * 128:(hc + 1) * 128,
                                     ocg * 512:(ocg + 1) * 512],
                                wbuf=w32b)
                            comp(dve, lambda h, wqh=wqh, w32=w32:
                                 h.tensor_copy(wqh, w32),
                                 wbufs=(whb,), rbufs=(w32b,))
                            comp(gp, lambda h, wql=wql, w32=w32, wqh=wqh:
                                 h.tensor_tensor(wql, w32, wqh,
                                                 OP.subtract),
                                 wbufs=(wlb,), rbufs=(w32b, whb))
                            for j in range(4):
                                seq = [(wqh, aofh, whb, "aofh"),
                                       (wql, aofh, wlb, "aofh"),
                                       (wqh, aofl, whb, "aofl")]
                                for si, (wt, mt, wbn, mbn) in \
                                        enumerate(seq):
                                    st = (hc == 0 and si == 0)
                                    sp = (hc == NHC - 1 and si == 2)
                                    comp(pe, lambda h, wt=wt, mt=mt, hc=hc,
                                         j=j, st=st, sp=sp:
                                         h.matmul(
                                            bank(j, TQ),
                                            wt[:, j * 128:(j + 1) * 128],
                                            mt[:, hc * TQ:(hc + 1) * TQ],
                                            start=st, stop=sp),
                                         wbufs=(f"ps{j}",),
                                         rbufs=(wbn, mbn))
                        for j in range(4):
                            oc = ocg * 4 + j
                            jp = j % 2
                            at32 = at32_2[:, jp * TQ:(jp + 1) * TQ]
                            atb = f"at32{jp}"
                            comp(act, lambda h, at32=at32, j=j:
                                 h.activation(at32, bank(j, TQ), AF.Copy),
                                 wbufs=(atb,), rbufs=(f"ps{j}",))
                            dma(sq, attn_s[oc * 128:(oc + 1) * 128,
                                           t0:t0 + TQ],
                                at32, rbufs=(atb,))
                            if tap == "attn" and b == 0:
                                dma(gpq, dbg_d[oc * 128:(oc + 1) * 128,
                                               :TQ],
                                    at32, rbufs=(atb,))
            wo()
            barrier()

        for b in range(NB):
            phase2_block(b)
        barrier()

        with contextlib.ExitStack() as _st:
            qgT_h = _st.enter_context(SB("qgT", [128, NHC * TQ3], BF16))
            quT_h = _st.enter_context(SB("quT", [128, NHC * TQ3], BF16))
            intm_h = _st.enter_context(SB("intm", [128, NQS3 * I_FF], BF16))
            qiT_h = _st.enter_context(SB("qiT", [128, NIC * TQ3], BF16))
            stat_h = _st.enter_context(SB("stat", [128, 4], F32))
            sccol_h = _st.enter_context(SB("sccol", [128, 3 * NQS3], F32))
            scrow_h = _st.enter_context(SB("scrow", [1, 3 * TQ3], F32))
            sgrep_h = _st.enter_context(SB("sgrep", [128, TQ3], F32))
            surep_h = _st.enter_context(SB("surep", [128, TQ3], F32))
            sirep_h = _st.enter_context(SB("sirep", [128, TQ3], F32))
            qgT, quT = qgT_h[:], quT_h[:]
            intm, qiT = intm_h[:], qiT_h[:]
            stat, sccol, scrow = stat_h[:], sccol_h[:], scrow_h[:]
            sgrep, surep, sirep = sgrep_h[:], surep_h[:], sirep_h[:]
            qgT3 = qgT.rearrange("p (c m) -> p c m", m=TQ3)
            quT3 = quT.rearrange("p (c m) -> p c m", m=TQ3)
            intm3 = intm.rearrange("p (c m) -> p c m", m=I_FF)
            qiT3 = qiT.rearrange("p (c m) -> p c m", m=TQ3)

            def phase3_block(w):
                t0 = w * TQ3
                with contextlib.ExitStack() as _st:
                    xfb_h = _st.enter_context(SB("xfb", [128, 2 * H], BF16))
                    xsb_h = _st.enter_context(SB("xsb", [128, H // 2], F32))
                    tpb_h = _st.enter_context(SB("tpb", [128, H // 2], F32))
                    qrb_h = _st.enter_context(SB("qrb", [128, H // 2], F32))
                    qtm_h = _st.enter_context(SB("qtm", [128, H], BF16))
                    invgr_h = _st.enter_context(SB("invgr", [128, H], F32))
                    xfb2, xsb, tpb = xfb_h[:], xsb_h[:], tpb_h[:]
                    qrb, qtm = qrb_h[:], qtm_h[:]
                    invgr = invgr_h[:]
                    for branch, (invsrc, invsb, qT3, qTb) in enumerate((
                            (invg_r, "invg_r", qgT3, "qgT"),
                            (invu_r, "invu_r", quT3, "quT"))):
                        invt, invb = invgr, "invgr"
                        replicate_row(invsrc, invsb, H, invgr, "invgr")
                        for tt in range(NQS3):
                            xp = (branch * NQS3 + tt) % 2
                            xfb = xfb2[:, xp * H:(xp + 1) * H]
                            xfn = f"xfb{xp}"
                            dma(sq, xfb,
                                xf_s[t0 + tt * 128:t0 + (tt + 1) * 128, :],
                                wbuf=xfn)
                            for hf in range(2):
                                sl = slice(hf * (H // 2), (hf + 1) * (H // 2))
                                comp(dve, lambda h, invt=invt, sl=sl,
                                     xfb=xfb:
                                     h.tensor_tensor(
                                        xsb, xfb[:, sl], invt[:, sl],
                                        OP.mult),
                                     wbufs=("xsb",), rbufs=(xfn, invb))
                                comp(dve, lambda h, hf=hf: h.reduce_max(
                                        stat[:, hf:hf + 1], xsb, AX,
                                        apply_absolute_value=True),
                                     wbufs=(f"st{hf}",), rbufs=("xsb",))
                            comp(dve, lambda h: h.tensor_tensor(
                                    stat[:, 0:1], stat[:, 0:1],
                                    stat[:, 1:2], OP.max),
                                 wbufs=("st0",), rbufs=("st0", "st1"))
                            comp(dve, lambda h: h.tensor_scalar(
                                    stat[:, 1:2], stat[:, 0:1], inv127c,
                                    1e-8, OP.mult, OP.max),
                                 wbufs=("st1",), rbufs=("st0", "inv127c"))
                            comp(dve, lambda h: h.reciprocal(
                                    stat[:, 2:3], stat[:, 1:2]),
                                 wbufs=("st2",), rbufs=("st1",))
                            comp(dve, lambda h, branch=branch, tt=tt:
                                 h.tensor_copy(
                                    sccol[:, branch * NQS3 + tt:
                                          branch * NQS3 + tt + 1],
                                    stat[:, 1:2]),
                                 wbufs=("sccol",), rbufs=("st1",))
                            for hf in range(2):
                                sl = slice(hf * (H // 2), (hf + 1) * (H // 2))
                                comp(dve, lambda h, invt=invt, sl=sl,
                                     xfb=xfb:
                                     h.tensor_tensor(
                                        xsb, xfb[:, sl], invt[:, sl],
                                        OP.mult),
                                     wbufs=("xsb",), rbufs=(xfn, invb))
                                comp(dve, lambda h: h.tensor_scalar(
                                        tpb, xsb, stat[:, 2:3], magc,
                                        OP.mult, OP.add),
                                     wbufs=("tpb",),
                                     rbufs=("xsb", "st2", "magc"))
                                comp(dve, lambda h: h.tensor_scalar(
                                        qrb, tpb, magc, None, OP.subtract),
                                     wbufs=("qrb",), rbufs=("tpb", "magc"))
                                comp(dve, lambda h, sl=sl: h.tensor_scalar(
                                        qtm[:, sl], qrb, -127.0, 127.0,
                                        OP.max, OP.min),
                                     wbufs=("qtm",), rbufs=("qrb",))
                            transpose_batch(
                                qtm, "qtm", NHC,
                                lambda g0, cnt, tt=tt, qT3=qT3:
                                    qT3[:, g0:g0 + cnt,
                                        tt * 128:(tt + 1) * 128],
                                qTb, ident16, "ident16", bf=True)
                barrier()
                for branch, (rep, repb) in enumerate(((sgrep, "sgrep"),
                                                      (surep, "surep"))):
                    for tt in range(NQS3):
                        comp(pe, lambda h, branch=branch, tt=tt: h.transpose(
                                bank(1)[0:1, tt * 128:(tt + 1) * 128],
                                sccol[:, branch * NQS3 + tt:
                                      branch * NQS3 + tt + 1], ident32),
                             wbufs=("ps1",),
                             rbufs=("sccol", "ident32"))
                    comp(dve, lambda h, branch=branch: h.tensor_copy(
                            scrow[:1, branch * TQ3:(branch + 1) * TQ3],
                            bank(1, TQ3)[0:1, :]),
                         wbufs=("scrow",), rbufs=("ps1",))
                    replicate_row(scrow[:1, branch * TQ3:
                                        (branch + 1) * TQ3],
                                  "scrow", TQ3, rep, repb)

                with contextlib.ExitStack() as _st:
                    wgT_h = _st.enter_context(SB("wgT", [128, 2 * H], BF16))
                    s1b_h = _st.enter_context(SB("s1b", [128, NIC * TQ3], F32))
                    gtmp_h = _st.enter_context(SB("gtmp", [128, 2 * TQ3], F32))
                    utmp_h = _st.enter_context(SB("utmp", [128, TQ3], F32))
                    ibg_h = _st.enter_context(SB("ibg", [128, TQ3], BF16))
                    vcb_h = _st.enter_context(SB("vcb", [128, TQ3], F32))
                    dfb_h = _st.enter_context(SB("dfb", [128, TQ3], F32))
                    eib_h = _st.enter_context(SB("eib", [128, TQ3], I32))
                    wgT2 = wgT_h[:]
                    s1b = s1b_h[:]
                    gtmp2 = gtmp_h[:]
                    utmp, ibg = utmp_h[:], ibg_h[:]
                    vcb, dfb, eib = vcb_h[:], dfb_h[:], eib_h[:]
                    for ic in range(NIC):
                        par = ic % 2
                        wgT = wgT2[:, par * H:(par + 1) * H]
                        wgb = f"wgT{par}"
                        bk = 0 if par == 0 else 2
                        dma(gpq if par == 0 else sq, wgT,
                            wgT_s[ic * 128:(ic + 1) * 128, :], wbuf=wgb)
                        for hc in range(NHC):
                            comp(pe, lambda h, hc=hc, wgT=wgT, bk=bk:
                                 h.matmul(
                                    bank(bk, TQ3),
                                    wgT[:, hc * 128:(hc + 1) * 128],
                                    qgT[:, hc * TQ3:(hc + 1) * TQ3],
                                    start=(hc == 0), stop=(hc == NHC - 1)),
                                 wbufs=(f"ps{bk}",),
                                 rbufs=(wgb, "qgT"))
                        gtp = gtmp2[:, par * TQ3:(par + 1) * TQ3]
                        gtn = f"gtmp{par}"
                        comp(dve, lambda h, ic=ic, bk=bk, gtp=gtp:
                             h.scalar_tensor_tensor(
                                gtp, bank(bk, TQ3), swg_c[:, ic:ic + 1],
                                sgrep, OP.mult, OP.mult),
                             wbufs=(gtn,),
                             rbufs=(f"ps{bk}", "swg", "sgrep"))
                        comp(act, lambda h, ic=ic, gtp=gtp: h.activation(
                                s1b[:, ic * TQ3:(ic + 1) * TQ3], gtp,
                                AF.Silu),
                             wbufs=("s1b",), rbufs=(gtn,))
                    for ic in range(NIC):
                        par = ic % 2
                        wgT = wgT2[:, par * H:(par + 1) * H]
                        wgb = f"wgT{par}"
                        bk = 0 if par == 0 else 2
                        gtp = gtmp2[:, par * TQ3:(par + 1) * TQ3]
                        gtn = f"gtmp{par}"
                        dma(gpq if par == 0 else sq, wgT,
                            wuT_s[ic * 128:(ic + 1) * 128, :], wbuf=wgb)
                        for hc in range(NHC):
                            comp(pe, lambda h, hc=hc, wgT=wgT, bk=bk:
                                 h.matmul(
                                    bank(bk, TQ3),
                                    wgT[:, hc * 128:(hc + 1) * 128],
                                    quT[:, hc * TQ3:(hc + 1) * TQ3],
                                    start=(hc == 0), stop=(hc == NHC - 1)),
                                 wbufs=(f"ps{bk}",),
                                 rbufs=(wgb, "quT"))
                        comp(dve, lambda h, ic=ic, bk=bk:
                             h.scalar_tensor_tensor(
                                utmp, bank(bk, TQ3), swu_c[:, ic:ic + 1],
                                surep, OP.mult, OP.mult),
                             wbufs=("utmp",),
                             rbufs=(f"ps{bk}", "swu", "surep"))
                        comp(dve, lambda h, ic=ic: h.tensor_tensor(
                                gtp, utmp,
                                s1b[:, ic * TQ3:(ic + 1) * TQ3],
                                OP.mult),
                             wbufs=(gtn,), rbufs=("utmp", "s1b"))
                        comp(dve, lambda h: h.tensor_scalar(
                                vcb, gtp, -10.0, 10.0, OP.max, OP.min),
                             wbufs=("vcb",), rbufs=(gtn,))
                        emit_bf16_dround(vcb, "vcb", gtp, gtn,
                                         utmp, "utmp", dfb, "dfb",
                                         eib, "eib", ibg, "ibg")
                        transpose_batch(
                            ibg, "ibg", NQS3,
                            lambda g0, cnt, ic=ic:
                                intm3[:, g0:g0 + cnt, ic * 128:
                                      (ic + 1) * 128],
                            "intm", ident16, "ident16", bf=True)
                barrier()
                with contextlib.ExitStack() as _st:
                    xsi_h = _st.enter_context(SB("xsi", [128, I_FF], F32))
                    tpi_h = _st.enter_context(SB("tpi", [128, I_FF], F32))
                    qri_h = _st.enter_context(SB("qri", [128, I_FF], F32))
                    qtmi_h = _st.enter_context(SB("qtmi", [128, I_FF], BF16))
                    invir_h = _st.enter_context(SB("invir", [128, I_FF], F32))
                    xsi, tpi = xsi_h[:], tpi_h[:]
                    qri, qtmi = qri_h[:], qtmi_h[:]
                    invir = invir_h[:]
                    replicate_row(invi_r, "invi_r", I_FF, invir, "invir")
                    for tt in range(NQS3):
                        comp(dve, lambda h, tt=tt: h.tensor_tensor(
                                xsi, intm[:, tt * I_FF:(tt + 1) * I_FF],
                                invir, OP.mult),
                             wbufs=("xsi",), rbufs=("intm", "invir"))
                        comp(dve, lambda h: h.reduce_max(
                                stat[:, 0:1], xsi, AX,
                                apply_absolute_value=True),
                             wbufs=("st0",), rbufs=("xsi",))
                        comp(dve, lambda h: h.tensor_scalar(
                                stat[:, 1:2], stat[:, 0:1], inv127c,
                                1e-8, OP.mult, OP.max),
                             wbufs=("st1",), rbufs=("st0", "inv127c"))
                        comp(dve, lambda h: h.reciprocal(
                                stat[:, 2:3], stat[:, 1:2]),
                             wbufs=("st2",), rbufs=("st1",))
                        comp(dve, lambda h, tt=tt: h.tensor_copy(
                                sccol[:, 2 * NQS3 + tt:2 * NQS3 + tt + 1],
                                stat[:, 1:2]),
                             wbufs=("sccol",), rbufs=("st1",))
                        comp(dve, lambda h: h.tensor_scalar(
                                tpi, xsi, stat[:, 2:3], magc,
                                OP.mult, OP.add),
                             wbufs=("tpi",), rbufs=("xsi", "st2", "magc"))
                        comp(dve, lambda h: h.tensor_scalar(
                                qri, tpi, magc, None, OP.subtract),
                             wbufs=("qri",), rbufs=("tpi", "magc"))
                        comp(dve, lambda h: h.tensor_scalar(
                                qtmi, qri, -127.0, 127.0,
                                OP.max, OP.min),
                             wbufs=("qtmi",), rbufs=("qri",))
                        transpose_batch(
                            qtmi, "qtmi", NIC,
                            lambda g0, cnt, tt=tt:
                                qiT3[:, g0:g0 + cnt,
                                     tt * 128:(tt + 1) * 128],
                            "qiT", ident16, "ident16", bf=True)
                    for tt in range(NQS3):
                        comp(pe, lambda h, tt=tt: h.transpose(
                                bank(1)[0:1, tt * 128:(tt + 1) * 128],
                                sccol[:, 2 * NQS3 + tt:2 * NQS3 + tt + 1],
                                ident32),
                             wbufs=("ps1",),
                             rbufs=("sccol", "ident32"))
                    comp(dve, lambda h: h.tensor_copy(
                            scrow[:1, 2 * TQ3:3 * TQ3], bank(1, TQ3)[0:1, :]),
                         wbufs=("scrow",), rbufs=("ps1",))
                    replicate_row(scrow[:1, 2 * TQ3:3 * TQ3], "scrow",
                                  TQ3, sirep, "sirep")

                barrier()
                with contextlib.ExitStack() as _st:
                    wdT_h = _st.enter_context(SB("wdT", [128, 2 * I_FF], BF16))
                    dtmp_h = _st.enter_context(SB("dtmp", [128, 2 * TQ3], F32))
                    attnb_h = _st.enter_context(SB("attnb", [128, 2 * TQ3], F32))
                    xcol_h = _st.enter_context(SB("xcol", [128, 256], F32))
                    youtb_h = _st.enter_context(SB("youtb", [128, 256], F32))
                    wdT2 = wdT_h[:]
                    dtmp2, attnb2 = dtmp_h[:], attnb_h[:]
                    xcol2, youtb2 = xcol_h[:], youtb_h[:]
                    for hc in range(NHC):
                        par = hc % 2
                        wdT = wdT2[:, par * I_FF:(par + 1) * I_FF]
                        wdb = f"wdT{par}"
                        dtmp = dtmp2[:, par * TQ3:(par + 1) * TQ3]
                        attnb = attnb2[:, par * TQ3:(par + 1) * TQ3]
                        dtb, anb = f"dtmp{par}", f"attnb{par}"
                        bk = 0 if par == 0 else 2
                        dma(gpq if par == 0 else sq, wdT,
                            wdT_s[hc * 128:(hc + 1) * 128, :], wbuf=wdb)
                        for ic in range(NIC):
                            comp(pe, lambda h, ic=ic, wdT=wdT, bk=bk:
                                 h.matmul(
                                    bank(bk, TQ3),
                                    wdT[:, ic * 128:(ic + 1) * 128],
                                    qiT[:, ic * TQ3:(ic + 1) * TQ3],
                                    start=(ic == 0), stop=(ic == NIC - 1)),
                                 wbufs=(f"ps{bk}",),
                                 rbufs=(wdb, "qiT"))
                        comp(dve, lambda h, hc=hc, bk=bk, dtmp=dtmp:
                             h.scalar_tensor_tensor(
                                dtmp, bank(bk, TQ3), swd_c[:, hc:hc + 1],
                                sirep, OP.mult, OP.mult),
                             wbufs=(dtb,),
                             rbufs=(f"ps{bk}", "swd", "sirep"))
                        dma(sq, attnb,
                            attn_s[hc * 128:(hc + 1) * 128, t0:t0 + TQ3],
                            wbuf=anb)
                        comp(dve, lambda h, dtmp=dtmp, attnb=attnb:
                             h.tensor_tensor(dtmp, dtmp, attnb, OP.add),
                             wbufs=(dtb,), rbufs=(dtb, anb))
                        for j in range(NQS3):
                            jp = j % 2
                            xcol = xcol2[:, jp * 128:(jp + 1) * 128]
                            youtb = youtb2[:, jp * 128:(jp + 1) * 128]
                            xcb, ytb = f"xcol{jp}", f"youtb{jp}"
                            tbk = 1 if jp == 0 else 3
                            comp(pe, lambda h, j=j, dtmp=dtmp, tbk=tbk:
                                 h.transpose(
                                    bank(tbk)[:, 0:128],
                                    dtmp[:, j * 128:(j + 1) * 128],
                                    ident32),
                                 wbufs=(f"ps{tbk}",),
                                 rbufs=(dtb, "ident32"))
                            dma(sq, xcol,
                                x_d[t0 + j * 128:t0 + (j + 1) * 128,
                                    hc * 128:(hc + 1) * 128],
                                wbuf=xcb)
                            comp(dve, lambda h, youtb=youtb, xcol=xcol,
                                 tbk=tbk:
                                 h.scalar_tensor_tensor(
                                    youtb, bank(tbk, 128), 0.0, xcol,
                                    OP.bypass, OP.add),
                                 wbufs=(ytb,), rbufs=(f"ps{tbk}", xcb))
                            dma(gpq, y_d[t0 + j * 128:t0 + (j + 1) * 128,
                                        hc * 128:(hc + 1) * 128],
                                youtb, rbufs=(ytb,))
                barrier()


            for w in range(NB3):
                phase3_block(w)
        for ds in dmasems.values():
            if ds.n > 0:
                prog["sq"].append(("w", ds.sem, ds.n))

        def play(h, key):
            for item in prog[key]:
                if item[0] == "w":
                    h.wait_ge(item[1], item[2])
                else:
                    item[1](h).then_inc(item[2], item[3])

        @block.tensor
        def _(h):
            play(h, "pe")

        @block.vector
        def _(h):
            play(h, "dve")

        @block.scalar
        def _(h):
            play(h, "act")

        @block.gpsimd
        def _(h):
            play(h, "gp")

        @block.sync
        def _(h):
            play(h, "sq")

    return nc


def _consts(S):
    id32 = np.eye(128, dtype=np.float32)
    id16 = np.eye(128).astype(ml_dtypes.bfloat16)
    ones1 = np.ones((1, 128), dtype=np.float32)
    c = np.arange(S, dtype=np.float64)
    p = np.arange(128, dtype=np.float64)
    jrow2 = (c[None, :] - (S - 128) - p[:, None]).astype(np.float32)
    mask = np.where(np.arange(128)[None, :] > np.arange(128)[:, None],
                    np.float32(NEG_BIG), np.float32(0.0)).astype(np.float32)
    return id32, id16, ones1, jrow2, mask


def kernel(L0_attn_hidden_states, L0_attn_Wq, L0_attn_Wkv, L0_attn_Wo,
           L0_ffn_w_gate, L0_ffn_s_wgate, L0_ffn_w_up, L0_ffn_s_wup,
           L0_ffn_w_down, L0_ffn_s_wdown, L0_ffn_inv_gate, L0_ffn_inv_up,
           L0_ffn_inv_inter):
    x = np.ascontiguousarray(L0_attn_hidden_states)
    Bb, S, _ = x.shape
    nc = build_nc(S=S, TQ=min(512, S))
    id32, id16, ones1, jrow2, mask = _consts(S)
    base = {
        "Wq": np.ascontiguousarray(L0_attn_Wq),
        "Wkv": np.ascontiguousarray(L0_attn_Wkv),
        "Wo": np.ascontiguousarray(L0_attn_Wo),
        "w_gate": np.ascontiguousarray(L0_ffn_w_gate),
        "w_up": np.ascontiguousarray(L0_ffn_w_up),
        "w_down": np.ascontiguousarray(L0_ffn_w_down),
        "s_wgate": np.ascontiguousarray(L0_ffn_s_wgate),
        "s_wup": np.ascontiguousarray(L0_ffn_s_wup),
        "s_wdown": np.ascontiguousarray(L0_ffn_s_wdown),
        "inv_gate": np.ascontiguousarray(
            np.asarray(L0_ffn_inv_gate).reshape(1, -1)),
        "inv_up": np.ascontiguousarray(
            np.asarray(L0_ffn_inv_up).reshape(1, -1)),
        "inv_inter": np.ascontiguousarray(
            np.asarray(L0_ffn_inv_inter).reshape(1, -1)),
        "ident32": id32, "ident16": id16, "ones1": ones1,
        "jrow2": jrow2, "maskt": mask,
    }
    in_maps = [dict(base, x=np.ascontiguousarray(x[b])) for b in range(Bb)]
    res = run_bass_kernel_spmd(nc, in_maps, core_ids=list(range(Bb)))
    return np.stack([np.asarray(res.results[b]["y"]) for b in range(Bb)],
                    axis=0)


# version 10 (final)
# speedup vs baseline: 1.1308x; 1.0663x over previous
import contextlib
import math
import numpy as np
import ml_dtypes
import concourse.bass as bass
import concourse.mybir as mybir
from concourse.bass_utils import run_bass_kernel_spmd

F32 = mybir.dt.float32
BF16 = mybir.dt.bfloat16
I16 = mybir.dt.int16
I32 = mybir.dt.int32
AX = mybir.AxisListType.X
OP = mybir.AluOpType
AF = mybir.ActivationFunctionType

H = 4096
NH = 32
HD = 128
I_FF = 2048
EPS = 1e-5
MAGIC = 12582912.0
INV127 = 1.0 / 127.0
SCALE = float(HD) ** -0.5
NEG_BIG = -1.0e38
NHC = H // 128
NIC = I_FF // 128


def _slopes32():
    start = 2.0 ** (-(2.0 ** (-(math.log2(NH) - 3))))
    return [float(np.float32(start ** (i + 1))) for i in range(NH)]


class REng:

    def __init__(self, prog, key, sem, default_inc, skip_self=True):
        self.prog = prog
        self.key = key
        self.sem = sem
        self.n = 0
        self.default_inc = default_inc
        self.skip_self = skip_self
        self.waited = {}

    def wait(self, dep):
        if dep is None:
            return
        eng, val = dep
        if val <= 0:
            return
        if eng is self and self.skip_self:
            return
        k = id(eng.sem)
        if self.waited.get(k, -1) >= val:
            return
        self.prog[self.key].append(("w", eng.sem, val))
        self.waited[k] = val

    def run(self, fn, deps=(), inc=None):
        for d in deps:
            self.wait(d)
        if inc is None:
            inc = self.default_inc
        self.prog[self.key].append(("e", fn, self.sem, inc))
        self.n += inc
        return (self, self.n)

    def run_raw(self, fn, dsem):
        self.prog[self.key].append(("e", fn, dsem.sem, 16))
        dsem.n += 16
        return None


class DmaSem:

    def __init__(self, sem):
        self.sem = sem
        self.n = 0


class Buf:
    def __init__(self):
        self.wdep = None
        self.rdeps = []

    def write_deps(self):
        d = list(self.rdeps)
        if self.wdep is not None:
            d.append(self.wdep)
        return d

    def wrote(self, dep):
        self.wdep = dep
        self.rdeps = []

    def read_deps(self):
        return [self.wdep] if self.wdep is not None else []

    def read(self, dep):
        self.rdeps.append(dep)


def build_nc(S=2048, TQ=512, debug=False, tap=None):
    assert S % 128 == 0 and TQ % 128 == 0 and S % TQ == 0 and TQ <= 512
    NT = S // 128
    NB = S // TQ
    NQS = TQ // 128
    TQ3 = min(256, TQ)
    NB3 = S // TQ3
    NQS3 = TQ3 // 128

    nc = bass.Bass("TRN2", target_bir_lowering=False, debug=debug,
                   detect_race_conditions=False)

    x_d = nc.dram_tensor("x", [S, H], F32, kind="ExternalInput").ap()
    wq_d = nc.dram_tensor("Wq", [H, H], F32, kind="ExternalInput").ap()
    wkv_d = nc.dram_tensor("Wkv", [H, 2 * HD], F32, kind="ExternalInput").ap()
    wo_d = nc.dram_tensor("Wo", [H, H], F32, kind="ExternalInput").ap()
    wg_d = nc.dram_tensor("w_gate", [I_FF, H], F32, kind="ExternalInput").ap()
    wu_d = nc.dram_tensor("w_up", [I_FF, H], F32, kind="ExternalInput").ap()
    wd_d = nc.dram_tensor("w_down", [H, I_FF], F32, kind="ExternalInput").ap()
    swg_d = nc.dram_tensor("s_wgate", [I_FF], F32, kind="ExternalInput").ap()
    swu_d = nc.dram_tensor("s_wup", [I_FF], F32, kind="ExternalInput").ap()
    swd_d = nc.dram_tensor("s_wdown", [H], F32, kind="ExternalInput").ap()
    invg_d = nc.dram_tensor("inv_gate", [1, H], F32, kind="ExternalInput").ap()
    invu_d = nc.dram_tensor("inv_up", [1, H], F32, kind="ExternalInput").ap()
    invi_d = nc.dram_tensor("inv_inter", [1, I_FF], F32,
                            kind="ExternalInput").ap()
    id32_d = nc.dram_tensor("ident32", [128, 128], F32,
                            kind="ExternalInput").ap()
    id16_d = nc.dram_tensor("ident16", [128, 128], BF16,
                            kind="ExternalInput").ap()
    ones_d = nc.dram_tensor("ones1", [1, 128], F32, kind="ExternalInput").ap()
    jrow_d = nc.dram_tensor("jrow2", [128, S], F32, kind="ExternalInput").ap()
    mask_d = nc.dram_tensor("maskt", [128, 128], F32,
                            kind="ExternalInput").ap()

    y_d = nc.dram_tensor("y", [S, H], F32, kind="ExternalOutput").ap()
    dbg_d = (nc.dram_tensor("dbg", [S, H], F32, kind="ExternalOutput").ap()
             if tap else None)

    xnhi_s = nc.dram_tensor("xnhi_s", [S, H], BF16, kind="Internal").ap()
    xnlo_s = nc.dram_tensor("xnlo_s", [S, H], BF16, kind="Internal").ap()
    xf_s = nc.dram_tensor("xf_s", [S, H], BF16, kind="Internal").ap()
    qhi_s = nc.dram_tensor("qhi_s", [H, TQ], BF16, kind="Internal").ap()
    qlo_s = nc.dram_tensor("qlo_s", [H, TQ], BF16, kind="Internal").ap()
    aohi_s = nc.dram_tensor("aohi_s", [H, TQ], BF16, kind="Internal").ap()
    aolo_s = nc.dram_tensor("aolo_s", [H, TQ], BF16, kind="Internal").ap()
    attn_s = nc.dram_tensor("attn_s", [H, S], F32, kind="Internal").ap()
    xnhfm_s = nc.dram_tensor("xnhfm_s", [NHC * 128, S], BF16,
                             kind="Internal").ap()
    xnlfm_s = nc.dram_tensor("xnlfm_s", [NHC * 128, S], BF16,
                             kind="Internal").ap()
    wgT_s = nc.dram_tensor("wgT_s", [NIC * 128, H], BF16, kind="Internal").ap()
    wuT_s = nc.dram_tensor("wuT_s", [NIC * 128, H], BF16, kind="Internal").ap()
    wdT_s = nc.dram_tensor("wdT_s", [NHC * 128, I_FF], BF16,
                           kind="Internal").ap()

    slopes = _slopes32()

    _semstack = contextlib.ExitStack()
    with (
        _semstack,
        nc.semaphore("m_pe") as sem_pe,
        nc.semaphore("m_dve") as sem_dve,
        nc.semaphore("m_act") as sem_act,
        nc.semaphore("m_gp") as sem_gp,
        nc.semaphore("m_gpd") as sem_gpd,
        nc.semaphore("m_sq") as sem_sq,
        nc.semaphore("m_aq") as sem_aq,
        nc.Block() as block,
    ):
        prog = {"pe": [], "dve": [], "act": [], "gp": [], "sq": []}
        pe = REng(prog, "pe", sem_pe, 1)
        dve = REng(prog, "dve", sem_dve, 1, skip_self=False)
        act = REng(prog, "act", sem_act, 1, skip_self=False)
        gp = REng(prog, "gp", sem_gp, 1,
                  skip_self=False)
        gpq = REng(prog, "gp", sem_gpd, 16,
                   skip_self=False)
        sq = REng(prog, "sq", sem_sq, 16,
                  skip_self=False)
        aq = REng(prog, "act", sem_aq, 16,
                  skip_self=False)

        _names = {}

        def SB(name, shape, dt):
            i = _names.get(name, 0)
            _names[name] = i + 1
            return nc.sbuf_tensor(f"sb_{name}_{i}", shape, dt)

        bufs = {}

        def B(name):
            if name not in bufs:
                bufs[name] = Buf()
            return bufs[name]

        dmasems = {}

        def DS(key):
            if key not in dmasems:
                dmasems[key] = DmaSem(_semstack.enter_context(
                    nc.semaphore(f"ds_{key}")))
            return dmasems[key]

        def dma(q, dst_ap, src_ap, wbuf=None, rbufs=(), extra=(),
                slow=False):
            deps = list(extra)
            if wbuf is not None:
                deps += B(wbuf).write_deps()
            for rb in rbufs:
                deps += B(rb).read_deps()
            key = wbuf if wbuf is not None else (rbufs[0] if rbufs
                                                 else "misc")
            ds = DS(key)
            for dd in deps:
                q.wait(dd)
            inst = q.run_raw(lambda h, dst=dst_ap, src=src_ap, slow=slow:
                             h.dma_start(out=dst, in_=src,
                                         allow_slow_non_contiguous=slow),
                             ds)
            d = (ds, ds.n)
            if wbuf is not None:
                B(wbuf).wrote(d)
            for rb in rbufs:
                B(rb).read(d)
            return d

        def comp(eng, fn, wbufs=(), rbufs=(), extra=()):
            deps = list(extra)
            for wb in wbufs:
                deps += B(wb).write_deps()
            for rb in rbufs:
                deps += B(rb).read_deps()
            d = eng.run(fn, deps, inc=1)
            for wb in wbufs:
                B(wb).wrote(d)
            for rb in rbufs:
                B(rb).read(d)
            return d

        def barrier():
            engs = (pe, dve, act, gp, gpq, sq)
            marks = [(e, e.n) for e in engs]
            marks += [(ds, ds.n) for ds in dmasems.values() if ds.n > 0]
            for e in engs:
                for m in marks:
                    e.wait(m)

        ps = nc.alloc_psum_tensor("ps", [128, 3072], F32)
        ps_ap = ps.ap()
        psb = nc.alloc_psum_tensor("psb", [128, 2048], BF16)
        psb_ap = psb.ap()

        def bank(i, w=512):
            return ps_ap[:, i * 512:i * 512 + w]

        def bank_bf(i, w=1024):
            return psb_ap[:, i * 1024:i * 1024 + w]

        def sb(name, shape, dt):
            return nc.alloc_sbuf_tensor("sbr_" + name, shape, dt).ap()

        ident32 = sb("ident32", [128, 128], F32)
        ident16 = sb("ident16", [128, 128], BF16)
        ones1 = sb("ones1", [1, 128], F32)
        jrow2 = sb("jrow2", [128, S], F32)
        maskt = sb("maskt", [128, 128], F32)
        khi = sb("khi", [128, S], BF16)
        klo = sb("klo", [128, S], BF16)
        vhi = sb("vhi", [128, S], BF16)
        vlo = sb("vlo", [128, S], BF16)
        swg_c = sb("swg_c", [128, NIC], F32)
        swu_c = sb("swu_c", [128, NIC], F32)
        swd_c = sb("swd_c", [128, NHC], F32)
        invg_r = sb("invg_r", [1, H], F32)
        invu_r = sb("invu_r", [1, H], F32)
        invi_r = sb("invi_r", [1, I_FF], F32)

        c127 = sb("c127", [128, 1], F32)
        inv127c = sb("inv127c", [128, 1], F32)
        c128 = sb("c128", [128, 1], F32)
        scalec = sb("scalec", [128, 1], F32)
        comp(dve, lambda h: h.memset(c127, 127.0), wbufs=("c127",))
        comp(dve, lambda h: h.reciprocal(inv127c, c127),
             wbufs=("inv127c",), rbufs=("c127",))
        comp(dve, lambda h: h.memset(c128, 128.0), wbufs=("c128",))
        comp(act, lambda h: h.activation(c128, c128, AF.Sqrt),
             wbufs=("c128",), rbufs=("c128",))
        comp(dve, lambda h: h.reciprocal(scalec, c128),
             wbufs=("scalec",), rbufs=("c128",))
        m3c = sb("m3c", [128, 1], F32)
        magc = sb("magc", [128, 1], F32)
        comp(dve, lambda h: h.memset(m3c, 1.0), wbufs=("m3c",))
        comp(dve, lambda h: h.tensor_single_scalar(
                m3c, m3c, float(2.0 ** -30), OP.add),
             wbufs=("m3c",), rbufs=("m3c",))
        comp(dve, lambda h: h.tensor_single_scalar(
                m3c, m3c, -1.0, OP.add),
             wbufs=("m3c",), rbufs=("m3c",))
        comp(dve, lambda h: h.tensor_single_scalar(
                m3c, m3c, float(1.5 * 2.0 ** 82), OP.mult),
             wbufs=("m3c",), rbufs=("m3c",))
        comp(dve, lambda h: h.tensor_single_scalar(
                magc, m3c, 12582912.0, OP.add),
             wbufs=("magc",), rbufs=("m3c",))
        dma(sq, ident32, id32_d, wbuf="ident32")
        dma(sq, ident16, id16_d, wbuf="ident16")
        dma(sq, ones1, ones_d, wbuf="ones1")
        dma(sq, jrow2, jrow_d, wbuf="jrow2")
        dma(sq, maskt, mask_d, wbuf="maskt")
        dma(sq, swg_c, swg_d.rearrange("(c p) -> p c", p=128), wbuf="swg", slow=True)
        dma(sq, swu_c, swu_d.rearrange("(c p) -> p c", p=128), wbuf="swu", slow=True)
        dma(sq, swd_c, swd_d.rearrange("(c p) -> p c", p=128), wbuf="swd", slow=True)
        dma(sq, invg_r, invg_d, wbuf="invg_r")
        dma(sq, invu_r, invu_d, wbuf="invu_r")
        dma(sq, invi_r, invi_d, wbuf="invi_r")

        def replicate_row(row_ap, row_buf, n, dst_ap, dst_buf, bank_i=2):
            for c0 in range(0, n, 512):
                w = min(512, n - c0)
                comp(pe, lambda h, row_ap=row_ap, c0=c0, w=w, bank_i=bank_i:
                     h.matmul(bank(bank_i, w), ones1[:1, :],
                              row_ap[:1, c0:c0 + w], start=True, stop=True),
                     wbufs=(f"ps{bank_i}",), rbufs=("ones1", row_buf))
                comp(dve, lambda h, dst_ap=dst_ap, c0=c0, w=w, bank_i=bank_i:
                     h.tensor_copy(dst_ap[:, c0:c0 + w], bank(bank_i, w)),
                     wbufs=(dst_buf,), rbufs=(f"ps{bank_i}",))

        def transpose_batch(src_ap, src_buf, n_tiles, dst_fn, dst_buf,
                            ident, ident_buf, bf, banks=(4, 5)):
            per = 8 if bf else 4
            g0 = 0
            while g0 < n_tiles:
                cnt = min(per, n_tiles - g0)
                if bf:
                    bi = (g0 // per) % 2
                    wb = f"psb{bi}"
                else:
                    bi = banks[(g0 // per) % len(banks)]
                    wb = f"ps{bi}"
                for j in range(cnt):
                    tl = g0 + j
                    out_ap = (bank_bf(bi)[:, j * 128:(j + 1) * 128] if bf
                              else bank(bi)[:, j * 128:(j + 1) * 128])
                    comp(pe, lambda h, out_ap=out_ap, tl=tl:
                         h.transpose(out_ap,
                                     src_ap[:, tl * 128:(tl + 1) * 128],
                                     ident),
                         wbufs=(wb,),
                         rbufs=(src_buf, ident_buf) if j == 0
                               else (src_buf,))
                dst_ap = dst_fn(g0, cnt)
                src2 = (bank_bf(bi)[:, :cnt * 128] if bf
                        else bank(bi)[:, :cnt * 128])
                comp(dve, lambda h, dst_ap=dst_ap, src2=src2:
                     h.tensor_copy(dst_ap, src2),
                     wbufs=(dst_buf,), rbufs=(wb,))
                g0 += cnt

        def emit_bf16_dround(V, Vbuf, A, Abuf, Bv, Bbuf, D, Dbuf, E, Ebuf,
                             O, Obuf):
            EXPM = 0x7F800000
            comp(dve, lambda h: h.tensor_single_scalar(
                    E, V.bitcast(I32), EXPM, OP.bitwise_and),
                 wbufs=(Ebuf,), rbufs=(Vbuf,))
            comp(dve, lambda h: h.tensor_single_scalar(
                    E, E, float(23 * 2 ** 23), OP.subtract),
                 wbufs=(Ebuf,), rbufs=(Ebuf,))
            comp(dve, lambda h: h.tensor_scalar(
                    Bv, E.bitcast(F32), m3c, None, OP.mult),
                 wbufs=(Bbuf,), rbufs=(Ebuf, "m3c"))
            comp(dve, lambda h: h.tensor_tensor(A, V, Bv, OP.add),
                 wbufs=(Abuf,), rbufs=(Vbuf, Bbuf))
            comp(dve, lambda h: h.tensor_tensor(A, A, Bv, OP.subtract),
                 wbufs=(Abuf,), rbufs=(Abuf, Bbuf))
            comp(dve, lambda h: h.tensor_single_scalar(
                    E, A.bitcast(I32), EXPM, OP.bitwise_and),
                 wbufs=(Ebuf,), rbufs=(Abuf,))
            comp(dve, lambda h: h.tensor_single_scalar(
                    E, E, float(7 * 2 ** 23), OP.subtract),
                 wbufs=(Ebuf,), rbufs=(Ebuf,))
            comp(dve, lambda h: h.tensor_scalar(
                    Bv, E.bitcast(F32), m3c, None, OP.mult),
                 wbufs=(Bbuf,), rbufs=(Ebuf, "m3c"))
            comp(dve, lambda h: h.tensor_tensor(A, A, Bv, OP.add),
                 wbufs=(Abuf,), rbufs=(Abuf, Bbuf))
            comp(dve, lambda h: h.tensor_tensor(O, A, Bv, OP.subtract),
                 wbufs=(Obuf,), rbufs=(Abuf, Bbuf))

        with contextlib.ExitStack() as _st:
            wkv32_h = _st.enter_context(SB("wkv32", [128, NHC * 32], F32))
            wkvh_h = _st.enter_context(SB("wkvh", [128, NHC * 256], BF16))
            wkvl_h = _st.enter_context(SB("wkvl", [128, NHC * 256], BF16))
            p1x_h = _st.enter_context(SB("p1x", [128, H], F32))
            p1xn_h = _st.enter_context(SB("p1xn", [128, H], F32))
            p1hi_h = _st.enter_context(SB("p1hi", [128, H], BF16))
            p1lo_h = _st.enter_context(SB("p1lo", [128, H], BF16))
            p1tr_h = _st.enter_context(SB("p1tr", [128, H], BF16))
            wpb_h = _st.enter_context(SB("wpb", [128, H // 2], BF16))
            wpt_h = _st.enter_context(SB("wpt", [128, H], BF16))
            wpb2_h = _st.enter_context(SB("wpb2", [128, H // 2], BF16))
            p1fmh_h = _st.enter_context(SB("p1fmh", [128, NHC * 128], BF16))
            p1fml_h = _st.enter_context(SB("p1fml", [128, NHC * 128], BF16))
            p1st_h = _st.enter_context(SB("p1st", [128, 8], F32))
            p1kv_h = _st.enter_context(SB("p1kv", [128, 128], F32))
            p1xc_h = _st.enter_context(SB("p1xc", [128, H // 4], F32))
            p1xw_h = _st.enter_context(SB("p1xw", [128, H // 4], F32))
            p1xe_h = _st.enter_context(SB("p1xe", [128, H // 4], I32))
            p1xd_h = _st.enter_context(SB("p1xd", [128, H // 4], F32))
            wkv32, wkvh, wkvl = wkv32_h[:], wkvh_h[:], wkvl_h[:]
            p1x, p1xn = p1x_h[:], p1xn_h[:]
            p1hi, p1lo, p1tr = p1hi_h[:], p1lo_h[:], p1tr_h[:]
            wpb, wpt = wpb_h[:], wpt_h[:]
            wpb2 = wpb2_h[:]
            p1fmh, p1fml = p1fmh_h[:], p1fml_h[:]
            p1st, p1kv = p1st_h[:], p1kv_h[:]
            p1xc, p1xw, p1xe = p1xc_h[:], p1xw_h[:], p1xe_h[:]
            p1xd = p1xd_h[:]

            for grp in range(8):
                for j in range(4):
                    hc = grp * 4 + j
                    dma(sq, wkv32[:, j * 256:(j + 1) * 256],
                        wkv_d[hc * 128:(hc + 1) * 128, :], wbuf="wkv32")
                sl = slice(grp * 1024, (grp + 1) * 1024)
                comp(dve, lambda h, sl=sl: h.tensor_copy(
                        wkvh[:, sl], wkv32),
                     wbufs=("wkvh",), rbufs=("wkv32",))
                comp(gp, lambda h, sl=sl: h.tensor_tensor(
                        wkvl[:, sl], wkv32, wkvh[:, sl], OP.subtract),
                     wbufs=("wkvl",), rbufs=("wkv32", "wkvh"))

            for src_d, dst_s, nch, wid in ((wg_d, wgT_s, NIC, H),
                                           (wu_d, wuT_s, NIC, H),
                                           (wd_d, wdT_s, NHC, I_FF)):
                nhf = wid // (H // 2)
                hw = H // 2
                for c in range(nch):
                    for hf in range(nhf):
                        p = (c * nhf + hf) % 2
                        wpbc = wpb if p == 0 else wpb2
                        wpbn = "wpb" if p == 0 else "wpb2"
                        c0 = hf * hw
                        dma(gpq, wpbc[:, :hw],
                            src_d[c * 128:(c + 1) * 128, c0:c0 + hw],
                            wbuf=wpbn)
                        transpose_batch(
                            wpbc[:, :hw], wpbn, hw // 128,
                            lambda g0, cnt, c0=c0:
                                wpt[:, c0 + g0 * 128:c0 + (g0 + cnt) * 128],
                            "wpt", ident16, "ident16", bf=True)
                        dma(aq,
                            dst_s[c * 128:(c + 1) * 128, c0:c0 + hw],
                            wpt[:, c0:c0 + hw], rbufs=("wpt",))
            for t in range(NT):
                xb = "p1x"
                xt = p1x
                dma(sq, xt, x_d[t * 128:(t + 1) * 128, :], wbuf=xb)
                comp(dve, lambda h, xt=xt: h.reduce_sum(p1st[:, 0:1], xt, AX),
                     wbufs=("p1s1",), rbufs=(xb,))
                for qq in range(4):
                    sl = slice(qq * (H // 4), (qq + 1) * (H // 4))
                    comp(act, lambda h, xt=xt, sl=sl: h.activation(
                            p1xc, xt[:, sl], AF.Square),
                         wbufs=("p1xc",), rbufs=(xb,))
                    comp(dve, lambda h, qq=qq: h.reduce_sum(
                            p1st[:, 1:2] if qq == 0 else p1st[:, 7:8],
                            p1xc, AX),
                         wbufs=("p1s2" if qq == 0 else "p1ve",),
                         rbufs=("p1xc",))
                    if qq > 0:
                        comp(dve, lambda h: h.tensor_tensor(
                                p1st[:, 1:2], p1st[:, 1:2], p1st[:, 7:8],
                                OP.add),
                             wbufs=("p1s2",), rbufs=("p1s2", "p1ve"))
                comp(dve, lambda h: h.tensor_scalar(
                        p1st[:, 2:3], p1st[:, 0:1], 1.0 / H, None, OP.mult),
                     wbufs=("p1mu",), rbufs=("p1s1",))
                comp(dve, lambda h: h.tensor_tensor(
                        p1st[:, 3:4], p1st[:, 2:3], p1st[:, 2:3], OP.mult),
                     wbufs=("p1mu2",), rbufs=("p1mu",))
                comp(dve, lambda h: h.scalar_tensor_tensor(
                        p1st[:, 4:5], p1st[:, 1:2], 1.0 / H, p1st[:, 3:4],
                        OP.mult, OP.subtract),
                     wbufs=("p1var",), rbufs=("p1s2", "p1mu2"))
                comp(dve, lambda h: h.tensor_scalar(
                        p1st[:, 7:8], p1st[:, 4:5], EPS, None, OP.add),
                     wbufs=("p1ve",), rbufs=("p1var",))
                comp(act, lambda h: h.activation(
                        p1st[:, 5:6], p1st[:, 7:8], AF.Sqrt),
                     wbufs=("p1sd",), rbufs=("p1ve",))
                comp(dve, lambda h: h.reciprocal(p1st[:, 6:7], p1st[:, 5:6]),
                     wbufs=("p1istd",), rbufs=("p1sd",))
                comp(dve, lambda h, xt=xt: h.tensor_scalar(
                        p1xn, xt, p1st[:, 2:3], p1st[:, 6:7],
                        OP.subtract, OP.mult),
                     wbufs=("p1xn",), rbufs=(xb, "p1mu", "p1istd"))
                if tap == "st":
                    dma(gpq, dbg_d[t * 128:(t + 1) * 128, 0:8], p1st,
                        rbufs=("p1s1", "p1s2", "p1mu", "p1mu2", "p1var",
                               "p1sd", "p1istd", "p1ve"))
                if tap == "xn":
                    dma(gpq, dbg_d[t * 128:(t + 1) * 128, :], p1xn,
                        rbufs=("p1xn",))
                if tap == "x":
                    dma(gpq, dbg_d[t * 128:(t + 1) * 128, :], xt,
                        rbufs=(xb,))
                comp(dve, lambda h: h.tensor_copy(p1hi, p1xn),
                     wbufs=("p1hi",), rbufs=("p1xn",))
                comp(gp, lambda h: h.tensor_tensor(
                        p1lo, p1xn, p1hi, OP.subtract),
                     wbufs=("p1lo",), rbufs=("p1xn", "p1hi"))
                dma(sq, xnhi_s[t * 128:(t + 1) * 128, :], p1hi,
                    rbufs=("p1hi",))
                dma(sq, xnlo_s[t * 128:(t + 1) * 128, :], p1lo,
                    rbufs=("p1lo",))
                for hhalf in range(4):
                    sl = slice(hhalf * (H // 4), (hhalf + 1) * (H // 4))
                    emit_bf16_dround(p1xn[:, sl], "p1xn", p1xc, "p1xc",
                                     p1xw, "p1xw", p1xd, "p1xd",
                                     p1xe, "p1xe", p1tr[:, sl], "p1tr")
                dma(sq, xf_s[t * 128:(t + 1) * 128, :], p1tr,
                    rbufs=("p1tr",))
                transpose_batch(p1hi, "p1hi", NHC,
                                lambda g0, cnt: p1fmh[:, g0 * 128:
                                                      (g0 + cnt) * 128],
                                "p1fmh", ident16, "ident16", bf=True)
                transpose_batch(p1lo, "p1lo", NHC,
                                lambda g0, cnt: p1fml[:, g0 * 128:
                                                      (g0 + cnt) * 128],
                                "p1fml", ident16, "ident16", bf=True)
                for fmt, dst_s, fb in ((p1fmh, xnhfm_s, "p1fmh"),
                                       (p1fml, xnlfm_s, "p1fml")):
                    dma(gpq,
                        dst_s.rearrange("(hc p) s -> p hc s", p=128)
                        [:, :, t * 128:(t + 1) * 128],
                        fmt.rearrange("p (hc c) -> p hc c", c=128),
                        rbufs=(fb,))
                for oc in range(2):
                    wb = "ps4"
                    seq = [(wkvh, p1fmh, "wkvh", "p1fmh"),
                           (wkvl, p1fmh, "wkvl", "p1fmh"),
                           (wkvh, p1fml, "wkvh", "p1fml")]
                    idx = 0
                    for hc in range(NHC):
                        for si, (wt, mt, wbn, mbn) in enumerate(seq):
                            st = (idx == 0)
                            sp = (idx == NHC * 3 - 1)
                            comp(pe, lambda h, wt=wt, mt=mt, hc=hc, oc=oc,
                                 st=st, sp=sp:
                                 h.matmul(bank(4, 128),
                                          wt[:, hc * 256 + oc * 128:
                                             hc * 256 + oc * 128 + 128],
                                          mt[:, hc * 128:(hc + 1) * 128],
                                          start=st, stop=sp),
                                 wbufs=(wb,),
                                 rbufs=(wbn, mbn))
                            idx += 1
                    if oc == 0:
                        comp(dve, lambda h, t=t: h.tensor_copy(
                                khi[:, t * 128:(t + 1) * 128], bank(4, 128)),
                             wbufs=("khi",), rbufs=(wb,))
                        comp(dve, lambda h, t=t: h.scalar_tensor_tensor(
                                klo[:, t * 128:(t + 1) * 128], bank(4, 128),
                                0.0, khi[:, t * 128:(t + 1) * 128],
                                OP.bypass, OP.subtract),
                             wbufs=("klo",), rbufs=(wb, "khi"))
                    else:
                        comp(dve, lambda h: h.tensor_copy(p1kv, bank(4, 128)),
                             wbufs=("p1kv",), rbufs=(wb,))
                        comp(pe, lambda h: h.transpose(
                                bank(5, 128), p1kv, ident32),
                             wbufs=("ps5",), rbufs=("p1kv", "ident32"))
                        comp(act, lambda h, t=t: h.activation(
                                vhi[:, t * 128:(t + 1) * 128], bank(5, 128),
                                AF.Copy),
                             wbufs=("vhi",), rbufs=("ps5",))
                        comp(dve, lambda h, t=t: h.scalar_tensor_tensor(
                                vlo[:, t * 128:(t + 1) * 128], bank(5, 128),
                                0.0, vhi[:, t * 128:(t + 1) * 128],
                                OP.bypass, OP.subtract),
                             wbufs=("vlo",), rbufs=("ps5", "vhi"))
            if tap == "kv":
                dma(gpq, dbg_d[0:128, 0:S].bitcast(BF16)[:, :S], khi,
                    rbufs=("khi",))
                dma(gpq, dbg_d[128:256, 0:S].bitcast(BF16)[:, :S], vhi,
                    rbufs=("vhi",))
        barrier()

        def phase2_block(b):
            t0 = b * TQ

            def qproj():
                with contextlib.ExitStack() as _s1:
                    p2fmh_h = _s1.enter_context(
                        SB("p2fmh", [128, NHC * TQ], BF16))
                    p2fml_h = _s1.enter_context(
                        SB("p2fml", [128, NHC * TQ], BF16))
                    p2fmh, p2fml = p2fmh_h[:], p2fml_h[:]
                    for src_s, fmt, fb in ((xnhfm_s, p2fmh, "p2fmh"),
                                           (xnlfm_s, p2fml, "p2fml")):
                        fmt3 = fmt.rearrange("p (c m) -> p c m", m=TQ)
                        dma(sq, fmt3,
                            src_s.rearrange("(hc p) s -> p hc s", p=128)
                            [:, :, t0:t0 + TQ],
                            wbuf=fb)
                    barrier()
                    with contextlib.ExitStack() as _s2:
                        wq32_h = _s2.enter_context(
                            SB("wq32", [128, 3 * 512], F32))
                        wqh_h = _s2.enter_context(
                            SB("wqh", [128, 3 * 512], BF16))
                        wql_h = _s2.enter_context(
                            SB("wql", [128, 3 * 512], BF16))
                        q32_h = _s2.enter_context(
                            SB("q32", [128, 2 * TQ], F32))
                        qhib_h = _s2.enter_context(
                            SB("qhib", [128, 2 * TQ], BF16))
                        qlob_h = _s2.enter_context(
                            SB("qlob", [128, 2 * TQ], BF16))
                        wq32_2, wqh2, wql2 = wq32_h[:], wqh_h[:], wql_h[:]
                        q32_2, qhib2, qlob2 = q32_h[:], qhib_h[:], qlob_h[:]
                        for ocg in range(NHC // 4):
                            for hc in range(NHC):
                                par = hc % 3
                                w32 = wq32_2[:, par * 512:(par + 1) * 512]
                                wqh = wqh2[:, par * 512:(par + 1) * 512]
                                wql = wql2[:, par * 512:(par + 1) * 512]
                                w32b, whb, wlb = (f"wq32{par}",
                                                  f"wqh{par}", f"wql{par}")
                                dma((gpq, sq, aq)[par], w32,
                                    wq_d[hc * 128:(hc + 1) * 128,
                                         ocg * 512:(ocg + 1) * 512],
                                    wbuf=w32b)
                                comp(dve, lambda h, wqh=wqh, w32=w32:
                                     h.tensor_copy(wqh, w32),
                                     wbufs=(whb,), rbufs=(w32b,))
                                comp(gp, lambda h, wql=wql, w32=w32,
                                     wqh=wqh:
                                     h.tensor_tensor(wql, w32, wqh,
                                                     OP.subtract),
                                     wbufs=(wlb,), rbufs=(w32b, whb))
                                for j in range(4):
                                    seq = [(wqh, p2fmh, whb, "p2fmh"),
                                           (wql, p2fmh, wlb, "p2fmh"),
                                           (wqh, p2fml, whb, "p2fml")]
                                    for si, (wt, mt, wbn, mbn) in \
                                            enumerate(seq):
                                        st = (hc == 0 and si == 0)
                                        sp = (hc == NHC - 1 and si == 2)
                                        comp(pe, lambda h, wt=wt, mt=mt,
                                             hc=hc, j=j, st=st, sp=sp:
                                             h.matmul(
                                                bank(j, TQ),
                                                wt[:, j * 128:
                                                   (j + 1) * 128],
                                                mt[:, hc * TQ:
                                                   (hc + 1) * TQ],
                                                start=st, stop=sp),
                                             wbufs=(f"ps{j}",),
                                             rbufs=(wbn, mbn))
                            for j in range(4):
                                oc = ocg * 4 + j
                                jp = j % 2
                                qhib = qhib2[:, jp * TQ:(jp + 1) * TQ]
                                qlob = qlob2[:, jp * TQ:(jp + 1) * TQ]
                                qhb, qlb = f"qhib{jp}", f"qlob{jp}"
                                comp(act, lambda h, qhib=qhib, j=j:
                                     h.activation(qhib, bank(j, TQ),
                                                  AF.Copy, scale=scalec),
                                     wbufs=(qhb,),
                                     rbufs=(f"ps{j}", "scalec"))
                                comp(dve, lambda h, qlob=qlob, qhib=qhib,
                                     j=j:
                                     h.scalar_tensor_tensor(
                                        qlob, bank(j, TQ), scalec, qhib,
                                        OP.mult, OP.subtract),
                                     wbufs=(qlb,),
                                     rbufs=(f"ps{j}", qhb, "scalec"))
                                dma(sq, qhi_s[oc * 128:(oc + 1) * 128, :],
                                    qhib, rbufs=(qhb,))
                                dma(sq, qlo_s[oc * 128:(oc + 1) * 128, :],
                                    qlob, rbufs=(qlb,))
            qproj()
            barrier()

            def attention():
                with contextlib.ExitStack() as _s1:
                    biasb_h = _s1.enter_context(SB("biasb", [128, S], I32))
                    s2b_h = _s1.enter_context(
                        SB("s2b", [128, NQS * S], F32))
                    e32_h = _s1.enter_context(
                        SB("e32", [128, NQS * S], F32))
                    mx_h = _s1.enter_context(SB("mx", [128, 2 * NQS], F32))
                    sume_h = _s1.enter_context(SB("sume", [128, NQS], F32))
                    ehiT_h = _s1.enter_context(SB("ehiT", [128, 1024], BF16))
                    eloT_h = _s1.enter_context(SB("eloT", [128, 1024], BF16))
                    repr_h = _s1.enter_context(SB("repr", [1, TQ], F32))
                    reps_h = _s1.enter_context(SB("reps", [128, TQ], F32))
                    ao32_h = _s1.enter_context(SB("ao32", [128, TQ], F32))
                    aohib_h = _s1.enter_context(
                        SB("aohib", [128, 2 * TQ], BF16))
                    aolob_h = _s1.enter_context(
                        SB("aolob", [128, 2 * TQ], BF16))
                    qhib_h = _s1.enter_context(
                        SB("qhibA", [128, 2 * TQ], BF16))
                    qlob_h = _s1.enter_context(
                        SB("qlobA", [128, 2 * TQ], BF16))
                    bias_view = biasb_h[:].bitcast(F32)
                    s2b2, e32_2 = s2b_h[:], e32_h[:]
                    mx, sume = mx_h[:], sume_h[:]
                    ehiT2, eloT2 = ehiT_h[:], eloT_h[:]
                    repr_, reps = repr_h[:], reps_h[:]
                    ao32 = ao32_h[:]
                    aohib2, aolob2 = aohib_h[:], aolob_h[:]
                    qhib2, qlob2 = qhib_h[:], qlob_h[:]
                    pending_evac = []
                    for hh in range(NH):
                        hp = hh % 2
                        qhib = qhib2[:, hp * TQ:(hp + 1) * TQ]
                        qlob = qlob2[:, hp * TQ:(hp + 1) * TQ]
                        qhb, qlb = f"qhibA{hp}", f"qlobA{hp}"
                        aohib = aohib2[:, hp * TQ:(hp + 1) * TQ]
                        aolob = aolob2[:, hp * TQ:(hp + 1) * TQ]
                        ahb, alb = f"aohib{hp}", f"aolob{hp}"
                        dma(sq, qhib, qhi_s[hh * 128:(hh + 1) * 128, :],
                            wbuf=qhb)
                        dma(sq, qlob, qlo_s[hh * 128:(hh + 1) * 128, :],
                            wbuf=qlb)
                        comp(dve, lambda h, hh=hh: h.tensor_scalar(
                                bias_view, jrow2, slopes[hh], None,
                                OP.mult),
                             wbufs=("biasb",), rbufs=("jrow2",))
                        comp(dve, lambda h: h.tensor_tensor(
                                bias_view[:, S - 128:S],
                                bias_view[:, S - 128:S], maskt, OP.add),
                             wbufs=("biasb",), rbufs=("biasb", "maskt"))
                        infos = []
                        for qs in range(NQS):
                            pq = qs
                            q0 = t0 + qs * 128
                            nk = q0 // 128 + 1
                            k_hi = nk * 128
                            infos.append((qs, pq, nk, k_hi,
                                          s2b2[:, pq * S:(pq + 1) * S],
                                          e32_2[:, pq * S:(pq + 1) * S],
                                          f"s2b{pq}", f"e32{pq}"))
                        for qs, pq, nk, k_hi, s2b, e32, s2n, e3n in infos:
                            nkt = (k_hi + 511) // 512
                            for kt in range(nkt):
                                ntk = min(512, k_hi - kt * 512)
                                bk = 2 * (qs % 2) + (kt % 2)
                                wb = f"ps{bk}"
                                seq = [(qhib, khi, qhb, "khi"),
                                       (qlob, khi, qlb, "khi"),
                                       (qhib, klo, qhb, "klo")]
                                for si, (qt, ktt, qbn, kbn) in \
                                        enumerate(seq):
                                    comp(pe, lambda h, qt=qt, ktt=ktt,
                                         qs=qs, kt=kt, ntk=ntk, si=si,
                                         bk=bk:
                                         h.matmul(
                                            bank(bk, ntk),
                                            qt[:, qs * 128:(qs + 1) * 128],
                                            ktt[:, kt * 512:
                                                kt * 512 + ntk],
                                            start=(si == 0),
                                            stop=(si == 2)),
                                         wbufs=(wb,),
                                         rbufs=(qbn, kbn))
                                comp(dve, lambda h, kt=kt, ntk=ntk,
                                     k_hi=k_hi, s2b=s2b, bk=bk:
                                     h.tensor_tensor(
                                        s2b[:, kt * 512:kt * 512 + ntk],
                                        bank(bk, ntk),
                                        bias_view[:, S - k_hi + kt * 512:
                                                  S - k_hi + kt * 512
                                                  + ntk],
                                        OP.add),
                                     wbufs=(s2n,), rbufs=(wb, "biasb"))
                        while pending_evac:
                            pending_evac.pop(0)()
                        for qs, pq, nk, k_hi, s2b, e32, s2n, e3n in infos:
                            comp(dve, lambda h, k_hi=k_hi, s2b=s2b, pq=pq:
                                 h.reduce_max(
                                    mx[:, 2 * pq + 1:2 * pq + 2],
                                    s2b[:, :k_hi], AX, negate=True),
                                 wbufs=(f"mx1_{pq}",), rbufs=(s2n,))
                            comp(act, lambda h, k_hi=k_hi, s2b=s2b,
                                 e32=e32, pq=pq: h.activation(
                                    e32[:, :k_hi], s2b[:, :k_hi], AF.Exp,
                                    bias=mx[:, 2 * pq + 1:2 * pq + 2]),
                                 wbufs=(e3n,), rbufs=(s2n, f"mx1_{pq}"))
                        for qs, pq, nk, k_hi, s2b, e32, s2n, e3n in infos:
                            comp(dve, lambda h, k_hi=k_hi, qs=qs, e32=e32:
                                 h.reduce_sum(
                                    sume[:, qs:qs + 1], e32[:, :k_hi],
                                    AX),
                                 wbufs=("sume",), rbufs=(e3n,))
                            for g in range((nk + 3) // 4):
                                njj = min(4, nk - g * 4)
                                gp2 = g % 2
                                ebk = 4
                                ehiT = ehiT2[:, gp2 * 512:(gp2 + 1) * 512]
                                eloT = eloT2[:, gp2 * 512:(gp2 + 1) * 512]
                                ehn, eln = f"ehiT{gp2}", f"eloT{gp2}"
                                for j in range(njj):
                                    kc = g * 4 + j
                                    comp(pe, lambda h, kc=kc, j=j,
                                         e32=e32, ebk=ebk:
                                         h.transpose(
                                            bank(ebk)[:, j * 128:
                                                      (j + 1) * 128],
                                            e32[:, kc * 128:
                                                (kc + 1) * 128],
                                            ident32),
                                         wbufs=(f"ps{ebk}",),
                                         rbufs=(e3n, "ident32") if j == 0
                                               else (e3n,))
                                comp(act, lambda h, njj=njj, ehiT=ehiT,
                                     ebk=ebk: h.activation(
                                        ehiT[:, :njj * 128],
                                        bank(ebk, njj * 128), AF.Copy),
                                     wbufs=(ehn,), rbufs=(f"ps{ebk}",))
                                comp(dve, lambda h, njj=njj, eloT=eloT,
                                     ehiT=ehiT, ebk=ebk:
                                     h.scalar_tensor_tensor(
                                        eloT[:, :njj * 128],
                                        bank(ebk, njj * 128), 0.0,
                                        ehiT[:, :njj * 128],
                                        OP.bypass, OP.subtract),
                                     wbufs=(eln,), rbufs=(f"ps{ebk}", ehn))
                                for j in range(njj):
                                    kc = g * 4 + j
                                    seq = [(vhi, ehiT, "vhi", ehn),
                                           (vlo, ehiT, "vlo", ehn),
                                           (vhi, eloT, "vhi", eln)]
                                    for si, (vt, et, vbn, ebn) in \
                                            enumerate(seq):
                                        st = (kc == 0 and si == 0)
                                        sp = (kc == nk - 1 and si == 2)
                                        comp(pe, lambda h, vt=vt, et=et,
                                             kc=kc, j=j, qs=qs, st=st,
                                             sp=sp:
                                             h.matmul(
                                                bank(5)[:, qs * 128:
                                                        (qs + 1) * 128],
                                                vt[:, kc * 128:
                                                   (kc + 1) * 128],
                                                et[:, j * 128:
                                                   (j + 1) * 128],
                                                start=st, stop=sp),
                                             wbufs=("ps5",),
                                             rbufs=(vbn, ebn))
                        def make_evac(hh=hh, aohib=aohib, aolob=aolob,
                                      ahb=ahb, alb=alb):
                            def evac():
                                for qs in range(NQS):
                                    comp(pe, lambda h, qs=qs: h.transpose(
                                            bank(1)[0:1, qs * 128:
                                                    (qs + 1) * 128],
                                            sume[:, qs:qs + 1], ident32),
                                         wbufs=("ps1",),
                                         rbufs=("sume", "ident32"))
                                comp(dve, lambda h: h.reciprocal(
                                        repr_[:1, :], bank(1, TQ)[0:1, :]),
                                     wbufs=("repr",), rbufs=("ps1",))
                                replicate_row(repr_, "repr", TQ, reps,
                                              "reps", bank_i=0)
                                comp(dve, lambda h: h.scalar_tensor_tensor(
                                        ao32, bank(5, TQ), 0.0, reps,
                                        OP.bypass, OP.mult),
                                     wbufs=("ao32",), rbufs=("ps5", "reps"))
                                comp(dve, lambda h:
                                     h.tensor_copy(aohib, ao32),
                                     wbufs=(ahb,), rbufs=("ao32",))
                                comp(gp, lambda h:
                                     h.tensor_tensor(aolob, ao32, aohib,
                                                     OP.subtract),
                                     wbufs=(alb,), rbufs=("ao32", ahb))
                                dma(sq, aohi_s[hh * 128:(hh + 1) * 128, :],
                                    aohib, rbufs=(ahb,))
                                dma(sq, aolo_s[hh * 128:(hh + 1) * 128, :],
                                    aolob, rbufs=(alb,))
                            return evac
                        pending_evac.append(make_evac())

                    while pending_evac:
                        pending_evac.pop(0)()
            attention()
            barrier()

            def wo():
                with contextlib.ExitStack() as _s1:
                    aofh_h = _s1.enter_context(
                        SB("aofh", [128, NHC * TQ], BF16))
                    aofl_h = _s1.enter_context(
                        SB("aofl", [128, NHC * TQ], BF16))
                    wq32_h = _s1.enter_context(
                        SB("wo32", [128, 3 * 512], F32))
                    wqh_h = _s1.enter_context(
                        SB("woh", [128, 3 * 512], BF16))
                    wql_h = _s1.enter_context(
                        SB("wol", [128, 3 * 512], BF16))
                    at32_h = _s1.enter_context(SB("at32", [128, 2 * TQ], F32))
                    aofh, aofl = aofh_h[:], aofl_h[:]
                    wq32_2, wqh2, wql2 = wq32_h[:], wqh_h[:], wql_h[:]
                    at32_2 = at32_h[:]
                    for hc in range(NHC):
                        dma(sq, aofh[:, hc * TQ:(hc + 1) * TQ],
                            aohi_s[hc * 128:(hc + 1) * 128, :], wbuf="aofh")
                        dma(sq, aofl[:, hc * TQ:(hc + 1) * TQ],
                            aolo_s[hc * 128:(hc + 1) * 128, :], wbuf="aofl")
                    for ocg in range(NHC // 4):
                        for hc in range(NHC):
                            par = hc % 3
                            w32 = wq32_2[:, par * 512:(par + 1) * 512]
                            wqh = wqh2[:, par * 512:(par + 1) * 512]
                            wql = wql2[:, par * 512:(par + 1) * 512]
                            w32b, whb, wlb = (f"wo32{par}", f"woh{par}",
                                              f"wol{par}")
                            dma((gpq, sq, aq)[par], w32,
                                wo_d[hc * 128:(hc + 1) * 128,
                                     ocg * 512:(ocg + 1) * 512],
                                wbuf=w32b)
                            comp(dve, lambda h, wqh=wqh, w32=w32:
                                 h.tensor_copy(wqh, w32),
                                 wbufs=(whb,), rbufs=(w32b,))
                            comp(gp, lambda h, wql=wql, w32=w32, wqh=wqh:
                                 h.tensor_tensor(wql, w32, wqh,
                                                 OP.subtract),
                                 wbufs=(wlb,), rbufs=(w32b, whb))
                            for j in range(4):
                                seq = [(wqh, aofh, whb, "aofh"),
                                       (wql, aofh, wlb, "aofh"),
                                       (wqh, aofl, whb, "aofl")]
                                for si, (wt, mt, wbn, mbn) in \
                                        enumerate(seq):
                                    st = (hc == 0 and si == 0)
                                    sp = (hc == NHC - 1 and si == 2)
                                    comp(pe, lambda h, wt=wt, mt=mt, hc=hc,
                                         j=j, st=st, sp=sp:
                                         h.matmul(
                                            bank(j, TQ),
                                            wt[:, j * 128:(j + 1) * 128],
                                            mt[:, hc * TQ:(hc + 1) * TQ],
                                            start=st, stop=sp),
                                         wbufs=(f"ps{j}",),
                                         rbufs=(wbn, mbn))
                        for j in range(4):
                            oc = ocg * 4 + j
                            jp = j % 2
                            at32 = at32_2[:, jp * TQ:(jp + 1) * TQ]
                            atb = f"at32{jp}"
                            comp(act, lambda h, at32=at32, j=j:
                                 h.activation(at32, bank(j, TQ), AF.Copy),
                                 wbufs=(atb,), rbufs=(f"ps{j}",))
                            dma(sq, attn_s[oc * 128:(oc + 1) * 128,
                                           t0:t0 + TQ],
                                at32, rbufs=(atb,))
                            if tap == "attn" and b == 0:
                                dma(gpq, dbg_d[oc * 128:(oc + 1) * 128,
                                               :TQ],
                                    at32, rbufs=(atb,))
            wo()
            barrier()

        for b in range(NB):
            phase2_block(b)
        barrier()

        with contextlib.ExitStack() as _st:
            qgT_h = _st.enter_context(SB("qgT", [128, NHC * TQ3], BF16))
            quT_h = _st.enter_context(SB("quT", [128, NHC * TQ3], BF16))
            intm_h = _st.enter_context(SB("intm", [128, NQS3 * I_FF], BF16))
            qiT_h = _st.enter_context(SB("qiT", [128, NIC * TQ3], BF16))
            stat_h = _st.enter_context(SB("stat", [128, 4], F32))
            sccol_h = _st.enter_context(SB("sccol", [128, 3 * NQS3], F32))
            scrow_h = _st.enter_context(SB("scrow", [1, 3 * TQ3], F32))
            sgrep_h = _st.enter_context(SB("sgrep", [128, TQ3], F32))
            surep_h = _st.enter_context(SB("surep", [128, TQ3], F32))
            sirep_h = _st.enter_context(SB("sirep", [128, TQ3], F32))
            qgT, quT = qgT_h[:], quT_h[:]
            intm, qiT = intm_h[:], qiT_h[:]
            stat, sccol, scrow = stat_h[:], sccol_h[:], scrow_h[:]
            sgrep, surep, sirep = sgrep_h[:], surep_h[:], sirep_h[:]
            qgT3 = qgT.rearrange("p (c m) -> p c m", m=TQ3)
            quT3 = quT.rearrange("p (c m) -> p c m", m=TQ3)
            intm3 = intm.rearrange("p (c m) -> p c m", m=I_FF)
            qiT3 = qiT.rearrange("p (c m) -> p c m", m=TQ3)

            def phase3_block(w):
                t0 = w * TQ3
                with contextlib.ExitStack() as _st:
                    xfb_h = _st.enter_context(SB("xfb", [128, 2 * H], BF16))
                    xsb_h = _st.enter_context(SB("xsb", [128, H // 2], F32))
                    tpb_h = _st.enter_context(SB("tpb", [128, H // 2], F32))
                    qrb_h = _st.enter_context(SB("qrb", [128, H // 2], F32))
                    qtm_h = _st.enter_context(SB("qtm", [128, H], BF16))
                    invgr_h = _st.enter_context(SB("invgr", [128, H], F32))
                    xfb2, xsb, tpb = xfb_h[:], xsb_h[:], tpb_h[:]
                    qrb, qtm = qrb_h[:], qtm_h[:]
                    invgr = invgr_h[:]
                    for branch, (invsrc, invsb, qT3, qTb) in enumerate((
                            (invg_r, "invg_r", qgT3, "qgT"),
                            (invu_r, "invu_r", quT3, "quT"))):
                        invt, invb = invgr, "invgr"
                        replicate_row(invsrc, invsb, H, invgr, "invgr")
                        for tt in range(NQS3):
                            xp = (branch * NQS3 + tt) % 2
                            xfb = xfb2[:, xp * H:(xp + 1) * H]
                            xfn = f"xfb{xp}"
                            dma(sq, xfb,
                                xf_s[t0 + tt * 128:t0 + (tt + 1) * 128, :],
                                wbuf=xfn)
                            for hf in range(2):
                                sl = slice(hf * (H // 2), (hf + 1) * (H // 2))
                                comp(dve, lambda h, invt=invt, sl=sl,
                                     xfb=xfb:
                                     h.tensor_tensor(
                                        xsb, xfb[:, sl], invt[:, sl],
                                        OP.mult),
                                     wbufs=("xsb",), rbufs=(xfn, invb))
                                comp(dve, lambda h, hf=hf: h.reduce_max(
                                        stat[:, hf:hf + 1], xsb, AX,
                                        apply_absolute_value=True),
                                     wbufs=(f"st{hf}",), rbufs=("xsb",))
                            comp(dve, lambda h: h.tensor_tensor(
                                    stat[:, 0:1], stat[:, 0:1],
                                    stat[:, 1:2], OP.max),
                                 wbufs=("st0",), rbufs=("st0", "st1"))
                            comp(dve, lambda h: h.tensor_scalar(
                                    stat[:, 1:2], stat[:, 0:1], inv127c,
                                    1e-8, OP.mult, OP.max),
                                 wbufs=("st1",), rbufs=("st0", "inv127c"))
                            comp(dve, lambda h: h.reciprocal(
                                    stat[:, 2:3], stat[:, 1:2]),
                                 wbufs=("st2",), rbufs=("st1",))
                            comp(dve, lambda h, branch=branch, tt=tt:
                                 h.tensor_copy(
                                    sccol[:, branch * NQS3 + tt:
                                          branch * NQS3 + tt + 1],
                                    stat[:, 1:2]),
                                 wbufs=("sccol",), rbufs=("st1",))
                            for hf in range(2):
                                sl = slice(hf * (H // 2), (hf + 1) * (H // 2))
                                comp(dve, lambda h, invt=invt, sl=sl,
                                     xfb=xfb:
                                     h.tensor_tensor(
                                        xsb, xfb[:, sl], invt[:, sl],
                                        OP.mult),
                                     wbufs=("xsb",), rbufs=(xfn, invb))
                                comp(dve, lambda h: h.tensor_scalar(
                                        tpb, xsb, stat[:, 2:3], magc,
                                        OP.mult, OP.add),
                                     wbufs=("tpb",),
                                     rbufs=("xsb", "st2", "magc"))
                                comp(dve, lambda h: h.tensor_scalar(
                                        qrb, tpb, magc, None, OP.subtract),
                                     wbufs=("qrb",), rbufs=("tpb", "magc"))
                                comp(dve, lambda h, sl=sl: h.tensor_scalar(
                                        qtm[:, sl], qrb, -127.0, 127.0,
                                        OP.max, OP.min),
                                     wbufs=("qtm",), rbufs=("qrb",))
                            transpose_batch(
                                qtm, "qtm", NHC,
                                lambda g0, cnt, tt=tt, qT3=qT3:
                                    qT3[:, g0:g0 + cnt,
                                        tt * 128:(tt + 1) * 128],
                                qTb, ident16, "ident16", bf=True)
                barrier()
                for branch, (rep, repb) in enumerate(((sgrep, "sgrep"),
                                                      (surep, "surep"))):
                    for tt in range(NQS3):
                        comp(pe, lambda h, branch=branch, tt=tt: h.transpose(
                                bank(1)[0:1, tt * 128:(tt + 1) * 128],
                                sccol[:, branch * NQS3 + tt:
                                      branch * NQS3 + tt + 1], ident32),
                             wbufs=("ps1",),
                             rbufs=("sccol", "ident32"))
                    comp(dve, lambda h, branch=branch: h.tensor_copy(
                            scrow[:1, branch * TQ3:(branch + 1) * TQ3],
                            bank(1, TQ3)[0:1, :]),
                         wbufs=("scrow",), rbufs=("ps1",))
                    replicate_row(scrow[:1, branch * TQ3:
                                        (branch + 1) * TQ3],
                                  "scrow", TQ3, rep, repb)

                with contextlib.ExitStack() as _st:
                    wgT_h = _st.enter_context(SB("wgT", [128, 2 * H], BF16))
                    s1b_h = _st.enter_context(SB("s1b", [128, NIC * TQ3], F32))
                    gtmp_h = _st.enter_context(SB("gtmp", [128, 2 * TQ3], F32))
                    utmp_h = _st.enter_context(SB("utmp", [128, TQ3], F32))
                    ibg_h = _st.enter_context(SB("ibg", [128, TQ3], BF16))
                    vcb_h = _st.enter_context(SB("vcb", [128, TQ3], F32))
                    dfb_h = _st.enter_context(SB("dfb", [128, TQ3], F32))
                    eib_h = _st.enter_context(SB("eib", [128, TQ3], I32))
                    wgT2 = wgT_h[:]
                    s1b = s1b_h[:]
                    gtmp2 = gtmp_h[:]
                    utmp, ibg = utmp_h[:], ibg_h[:]
                    vcb, dfb, eib = vcb_h[:], dfb_h[:], eib_h[:]
                    for ic in range(NIC):
                        par = ic % 2
                        wgT = wgT2[:, par * H:(par + 1) * H]
                        wgb = f"wgT{par}"
                        bk = 0 if par == 0 else 2
                        dma(gpq if par == 0 else sq, wgT,
                            wgT_s[ic * 128:(ic + 1) * 128, :], wbuf=wgb)
                        for hc in range(NHC):
                            comp(pe, lambda h, hc=hc, wgT=wgT, bk=bk:
                                 h.matmul(
                                    bank(bk, TQ3),
                                    wgT[:, hc * 128:(hc + 1) * 128],
                                    qgT[:, hc * TQ3:(hc + 1) * TQ3],
                                    start=(hc == 0), stop=(hc == NHC - 1)),
                                 wbufs=(f"ps{bk}",),
                                 rbufs=(wgb, "qgT"))
                        gtp = gtmp2[:, par * TQ3:(par + 1) * TQ3]
                        gtn = f"gtmp{par}"
                        comp(dve, lambda h, ic=ic, bk=bk, gtp=gtp:
                             h.scalar_tensor_tensor(
                                gtp, bank(bk, TQ3), swg_c[:, ic:ic + 1],
                                sgrep, OP.mult, OP.mult),
                             wbufs=(gtn,),
                             rbufs=(f"ps{bk}", "swg", "sgrep"))
                        comp(act, lambda h, ic=ic, gtp=gtp: h.activation(
                                s1b[:, ic * TQ3:(ic + 1) * TQ3], gtp,
                                AF.Silu),
                             wbufs=("s1b",), rbufs=(gtn,))
                    for ic in range(NIC):
                        par = ic % 2
                        wgT = wgT2[:, par * H:(par + 1) * H]
                        wgb = f"wgT{par}"
                        bk = 0 if par == 0 else 2
                        gtp = gtmp2[:, par * TQ3:(par + 1) * TQ3]
                        gtn = f"gtmp{par}"
                        dma(gpq if par == 0 else sq, wgT,
                            wuT_s[ic * 128:(ic + 1) * 128, :], wbuf=wgb)
                        for hc in range(NHC):
                            comp(pe, lambda h, hc=hc, wgT=wgT, bk=bk:
                                 h.matmul(
                                    bank(bk, TQ3),
                                    wgT[:, hc * 128:(hc + 1) * 128],
                                    quT[:, hc * TQ3:(hc + 1) * TQ3],
                                    start=(hc == 0), stop=(hc == NHC - 1)),
                                 wbufs=(f"ps{bk}",),
                                 rbufs=(wgb, "quT"))
                        comp(dve, lambda h, ic=ic, bk=bk:
                             h.scalar_tensor_tensor(
                                utmp, bank(bk, TQ3), swu_c[:, ic:ic + 1],
                                surep, OP.mult, OP.mult),
                             wbufs=("utmp",),
                             rbufs=(f"ps{bk}", "swu", "surep"))
                        comp(dve, lambda h, ic=ic: h.tensor_tensor(
                                gtp, utmp,
                                s1b[:, ic * TQ3:(ic + 1) * TQ3],
                                OP.mult),
                             wbufs=(gtn,), rbufs=("utmp", "s1b"))
                        comp(dve, lambda h: h.tensor_scalar(
                                vcb, gtp, -10.0, 10.0, OP.max, OP.min),
                             wbufs=("vcb",), rbufs=(gtn,))
                        emit_bf16_dround(vcb, "vcb", gtp, gtn,
                                         utmp, "utmp", dfb, "dfb",
                                         eib, "eib", ibg, "ibg")
                        transpose_batch(
                            ibg, "ibg", NQS3,
                            lambda g0, cnt, ic=ic:
                                intm3[:, g0:g0 + cnt, ic * 128:
                                      (ic + 1) * 128],
                            "intm", ident16, "ident16", bf=True)
                barrier()
                with contextlib.ExitStack() as _st:
                    xsi_h = _st.enter_context(SB("xsi", [128, I_FF], F32))
                    tpi_h = _st.enter_context(SB("tpi", [128, I_FF], F32))
                    qri_h = _st.enter_context(SB("qri", [128, I_FF], F32))
                    qtmi_h = _st.enter_context(SB("qtmi", [128, I_FF], BF16))
                    invir_h = _st.enter_context(SB("invir", [128, I_FF], F32))
                    xsi, tpi = xsi_h[:], tpi_h[:]
                    qri, qtmi = qri_h[:], qtmi_h[:]
                    invir = invir_h[:]
                    replicate_row(invi_r, "invi_r", I_FF, invir, "invir")
                    for tt in range(NQS3):
                        comp(dve, lambda h, tt=tt: h.tensor_tensor(
                                xsi, intm[:, tt * I_FF:(tt + 1) * I_FF],
                                invir, OP.mult),
                             wbufs=("xsi",), rbufs=("intm", "invir"))
                        comp(dve, lambda h: h.reduce_max(
                                stat[:, 0:1], xsi, AX,
                                apply_absolute_value=True),
                             wbufs=("st0",), rbufs=("xsi",))
                        comp(dve, lambda h: h.tensor_scalar(
                                stat[:, 1:2], stat[:, 0:1], inv127c,
                                1e-8, OP.mult, OP.max),
                             wbufs=("st1",), rbufs=("st0", "inv127c"))
                        comp(dve, lambda h: h.reciprocal(
                                stat[:, 2:3], stat[:, 1:2]),
                             wbufs=("st2",), rbufs=("st1",))
                        comp(dve, lambda h, tt=tt: h.tensor_copy(
                                sccol[:, 2 * NQS3 + tt:2 * NQS3 + tt + 1],
                                stat[:, 1:2]),
                             wbufs=("sccol",), rbufs=("st1",))
                        comp(dve, lambda h: h.tensor_scalar(
                                tpi, xsi, stat[:, 2:3], magc,
                                OP.mult, OP.add),
                             wbufs=("tpi",), rbufs=("xsi", "st2", "magc"))
                        comp(dve, lambda h: h.tensor_scalar(
                                qri, tpi, magc, None, OP.subtract),
                             wbufs=("qri",), rbufs=("tpi", "magc"))
                        comp(dve, lambda h: h.tensor_scalar(
                                qtmi, qri, -127.0, 127.0,
                                OP.max, OP.min),
                             wbufs=("qtmi",), rbufs=("qri",))
                        transpose_batch(
                            qtmi, "qtmi", NIC,
                            lambda g0, cnt, tt=tt:
                                qiT3[:, g0:g0 + cnt,
                                     tt * 128:(tt + 1) * 128],
                            "qiT", ident16, "ident16", bf=True)
                    for tt in range(NQS3):
                        comp(pe, lambda h, tt=tt: h.transpose(
                                bank(1)[0:1, tt * 128:(tt + 1) * 128],
                                sccol[:, 2 * NQS3 + tt:2 * NQS3 + tt + 1],
                                ident32),
                             wbufs=("ps1",),
                             rbufs=("sccol", "ident32"))
                    comp(dve, lambda h: h.tensor_copy(
                            scrow[:1, 2 * TQ3:3 * TQ3], bank(1, TQ3)[0:1, :]),
                         wbufs=("scrow",), rbufs=("ps1",))
                    replicate_row(scrow[:1, 2 * TQ3:3 * TQ3], "scrow",
                                  TQ3, sirep, "sirep")

                barrier()
                with contextlib.ExitStack() as _st:
                    wdT_h = _st.enter_context(SB("wdT", [128, 2 * I_FF], BF16))
                    dtmp_h = _st.enter_context(SB("dtmp", [128, 2 * TQ3], F32))
                    attnb_h = _st.enter_context(SB("attnb", [128, 2 * TQ3], F32))
                    xcol_h = _st.enter_context(SB("xcol", [128, 256], F32))
                    youtb_h = _st.enter_context(SB("youtb", [128, 256], F32))
                    wdT2 = wdT_h[:]
                    dtmp2, attnb2 = dtmp_h[:], attnb_h[:]
                    xcol2, youtb2 = xcol_h[:], youtb_h[:]
                    for hc in range(NHC):
                        par = hc % 2
                        wdT = wdT2[:, par * I_FF:(par + 1) * I_FF]
                        wdb = f"wdT{par}"
                        dtmp = dtmp2[:, par * TQ3:(par + 1) * TQ3]
                        attnb = attnb2[:, par * TQ3:(par + 1) * TQ3]
                        dtb, anb = f"dtmp{par}", f"attnb{par}"
                        bk = 0 if par == 0 else 2
                        dma(gpq if par == 0 else sq, wdT,
                            wdT_s[hc * 128:(hc + 1) * 128, :], wbuf=wdb)
                        for ic in range(NIC):
                            comp(pe, lambda h, ic=ic, wdT=wdT, bk=bk:
                                 h.matmul(
                                    bank(bk, TQ3),
                                    wdT[:, ic * 128:(ic + 1) * 128],
                                    qiT[:, ic * TQ3:(ic + 1) * TQ3],
                                    start=(ic == 0), stop=(ic == NIC - 1)),
                                 wbufs=(f"ps{bk}",),
                                 rbufs=(wdb, "qiT"))
                        comp(dve, lambda h, hc=hc, bk=bk, dtmp=dtmp:
                             h.scalar_tensor_tensor(
                                dtmp, bank(bk, TQ3), swd_c[:, hc:hc + 1],
                                sirep, OP.mult, OP.mult),
                             wbufs=(dtb,),
                             rbufs=(f"ps{bk}", "swd", "sirep"))
                        dma(sq, attnb,
                            attn_s[hc * 128:(hc + 1) * 128, t0:t0 + TQ3],
                            wbuf=anb)
                        comp(dve, lambda h, dtmp=dtmp, attnb=attnb:
                             h.tensor_tensor(dtmp, dtmp, attnb, OP.add),
                             wbufs=(dtb,), rbufs=(dtb, anb))
                        for j in range(NQS3):
                            jp = j % 2
                            xcol = xcol2[:, jp * 128:(jp + 1) * 128]
                            youtb = youtb2[:, jp * 128:(jp + 1) * 128]
                            xcb, ytb = f"xcol{jp}", f"youtb{jp}"
                            tbk = 1 if jp == 0 else 3
                            comp(pe, lambda h, j=j, dtmp=dtmp, tbk=tbk:
                                 h.transpose(
                                    bank(tbk)[:, 0:128],
                                    dtmp[:, j * 128:(j + 1) * 128],
                                    ident32),
                                 wbufs=(f"ps{tbk}",),
                                 rbufs=(dtb, "ident32"))
                            dma(sq, xcol,
                                x_d[t0 + j * 128:t0 + (j + 1) * 128,
                                    hc * 128:(hc + 1) * 128],
                                wbuf=xcb)
                            comp(dve, lambda h, youtb=youtb, xcol=xcol,
                                 tbk=tbk:
                                 h.scalar_tensor_tensor(
                                    youtb, bank(tbk, 128), 0.0, xcol,
                                    OP.bypass, OP.add),
                                 wbufs=(ytb,), rbufs=(f"ps{tbk}", xcb))
                            dma(gpq, y_d[t0 + j * 128:t0 + (j + 1) * 128,
                                        hc * 128:(hc + 1) * 128],
                                youtb, rbufs=(ytb,))
                barrier()


            for w in range(NB3):
                phase3_block(w)
        for ds in dmasems.values():
            if ds.n > 0:
                prog["sq"].append(("w", ds.sem, ds.n))

        def play(h, key):
            for item in prog[key]:
                if item[0] == "w":
                    h.wait_ge(item[1], item[2])
                else:
                    item[1](h).then_inc(item[2], item[3])

        @block.tensor
        def _(h):
            play(h, "pe")

        @block.vector
        def _(h):
            play(h, "dve")

        @block.scalar
        def _(h):
            play(h, "act")

        @block.gpsimd
        def _(h):
            play(h, "gp")

        @block.sync
        def _(h):
            play(h, "sq")

    return nc


def _consts(S):
    id32 = np.eye(128, dtype=np.float32)
    id16 = np.eye(128).astype(ml_dtypes.bfloat16)
    ones1 = np.ones((1, 128), dtype=np.float32)
    c = np.arange(S, dtype=np.float64)
    p = np.arange(128, dtype=np.float64)
    jrow2 = (c[None, :] - (S - 128) - p[:, None]).astype(np.float32)
    mask = np.where(np.arange(128)[None, :] > np.arange(128)[:, None],
                    np.float32(NEG_BIG), np.float32(0.0)).astype(np.float32)
    return id32, id16, ones1, jrow2, mask


def kernel(L0_attn_hidden_states, L0_attn_Wq, L0_attn_Wkv, L0_attn_Wo,
           L0_ffn_w_gate, L0_ffn_s_wgate, L0_ffn_w_up, L0_ffn_s_wup,
           L0_ffn_w_down, L0_ffn_s_wdown, L0_ffn_inv_gate, L0_ffn_inv_up,
           L0_ffn_inv_inter):
    x = np.ascontiguousarray(L0_attn_hidden_states)
    Bb, S, _ = x.shape
    nc = build_nc(S=S, TQ=min(512, S))
    id32, id16, ones1, jrow2, mask = _consts(S)
    base = {
        "Wq": np.ascontiguousarray(L0_attn_Wq),
        "Wkv": np.ascontiguousarray(L0_attn_Wkv),
        "Wo": np.ascontiguousarray(L0_attn_Wo),
        "w_gate": np.ascontiguousarray(L0_ffn_w_gate),
        "w_up": np.ascontiguousarray(L0_ffn_w_up),
        "w_down": np.ascontiguousarray(L0_ffn_w_down),
        "s_wgate": np.ascontiguousarray(L0_ffn_s_wgate),
        "s_wup": np.ascontiguousarray(L0_ffn_s_wup),
        "s_wdown": np.ascontiguousarray(L0_ffn_s_wdown),
        "inv_gate": np.ascontiguousarray(
            np.asarray(L0_ffn_inv_gate).reshape(1, -1)),
        "inv_up": np.ascontiguousarray(
            np.asarray(L0_ffn_inv_up).reshape(1, -1)),
        "inv_inter": np.ascontiguousarray(
            np.asarray(L0_ffn_inv_inter).reshape(1, -1)),
        "ident32": id32, "ident16": id16, "ones1": ones1,
        "jrow2": jrow2, "maskt": mask,
    }
    in_maps = [dict(base, x=np.ascontiguousarray(x[b])) for b in range(Bb)]
    res = run_bass_kernel_spmd(nc, in_maps, core_ids=list(range(Bb)))
    return np.stack([np.asarray(res.results[b]["y"]) for b in range(Bb)],
                    axis=0)
